# version 01
# baseline (speedup 1.0000x reference)
import numpy as np
import ml_dtypes
from contextlib import ExitStack

import concourse.bass as bass
import concourse.mybir as mybir
from concourse.bass_utils import run_bass_kernel_spmd

F32 = mybir.dt.float32
BF16 = mybir.dt.bfloat16
AF = mybir.ActivationFunctionType
ALU = mybir.AluOpType
AX = mybir.AxisListType

BF = ml_dtypes.bfloat16

NCORES = 8
D = 1024
NCH = D // 128
SEQ = 8192
BATCH = 2
T_OWN = 2048
T = T_OWN + 1
T_PAD = 2560
EPS = 1e-5



class Ctr:
    __slots__ = ("h", "v")

    def __init__(self, h):
        self.h = h
        self.v = 0

    def inc(self, n=1):
        self.v += n
        return self.v


class Prog:
    ENGINES = ("sync", "tensor", "vector", "scalar", "gpsimd")

    def __init__(self, nc, es):
        self.nc = nc
        self.es = es
        self.streams = {e: [] for e in self.ENGINES}

    def sem(self, name):
        h = self.es.enter_context(self.nc.semaphore(name))
        return Ctr(h)

    def emit(self, eng, fn, waits=(), inc=None, inc_n=1):
        waits = tuple((c, v) for (c, v) in waits if v > 0)

        def run(e, fn=fn, waits=waits, inc=inc, inc_n=inc_n):
            for c, v in waits:
                e.wait_ge(c.h, v)
            inst = fn(e)
            if inc is not None:
                assert inst is not None
                inst.then_inc(inc.h, inc_n)

        self.streams[eng].append(run)
        if inc is not None:
            return inc.inc(inc_n)
        return None

    def wait(self, eng, ctr, val):
        if val > 0:
            self.emit(eng, lambda e: None, waits=((ctr, val),))

    def finalize(self, block):
        for eng in self.ENGINES:
            fns = self.streams[eng]
            if not fns:
                continue
            getattr(block, eng)(lambda e, fns=fns: [f(e) for f in fns])


def strips(total, width, start=0):
    return [(s, min(width, total + start - s)) for s in range(start, total + start, width)]



def emit_ln_rows(p, *, x_chunks, w, prow_m, prow_v, sq, row_a, row_b, ones,
                 t_c, v_c, s_c, eps_ap=None, first_t_waits=(), first_s_waits=()):
    for i, xc in enumerate(x_chunks):
        p.emit("tensor",
               lambda e, xc=xc, i=i: e.matmul(prow_m, ones, xc,
                                              start=(i == 0), stop=(i == NCH - 1)),
               waits=first_t_waits if i == 0 else (),
               inc=t_c if i == NCH - 1 else None)
    mdone = t_c.v
    for i, xc in enumerate(x_chunks):
        p.emit("scalar",
               lambda e, xc=xc: e.activation(out=sq, in_=xc, func=AF.Square),
               waits=first_s_waits if i == 0 else (), inc=s_c)
        sdone = s_c.v
        p.emit("tensor",
               lambda e, i=i: e.matmul(prow_v, ones, sq,
                                       start=(i == 0), stop=(i == NCH - 1)),
               waits=((s_c, sdone),), inc=t_c)
        if i < NCH - 1:
            p.wait("scalar", t_c, t_c.v)
    vsum_done = t_c.v
    p.emit("vector", lambda e: e.tensor_scalar_mul(out=row_b, in0=prow_m, scalar1=1.0 / D),
           waits=((t_c, mdone),))
    p.emit("vector", lambda e: e.tensor_scalar_mul(out=row_a, in0=prow_v, scalar1=1.0 / D),
           waits=((t_c, vsum_done),))
    p.emit("vector", lambda e: e.tensor_mul(out=sq[0:1, :w], in0=row_b, in1=row_b))
    p.emit("vector", lambda e: e.tensor_sub(out=row_a, in0=row_a, in1=sq[0:1, :w]),
           inc=v_c)
    vdone = v_c.v
    p.emit("scalar", lambda e: e.activation(out=row_a, in_=row_a, func=AF.Ln, bias=eps_ap),
           waits=((v_c, vdone),))
    p.emit("scalar", lambda e: e.activation(out=row_a, in_=row_a, func=AF.Exp, scale=-0.5),
           inc=s_c)
    rstd_done = s_c.v
    p.emit("vector", lambda e: e.tensor_mul(out=row_b, in0=row_b, in1=row_a),
           waits=((s_c, rstd_done),))
    p.emit("vector", lambda e: e.tensor_scalar_mul(out=row_b, in0=row_b, scalar1=-1.0),
           inc=v_c)
    return v_c.v



def build_l1_ffn():
    nc = bass.Bass("TRN2", target_bir_lowering=False, debug=False)
    x_h = nc.dram_tensor("x", [D, T_PAD], F32, kind="ExternalInput").ap()
    w_h = {k: nc.dram_tensor(k, [D, D], F32, kind="ExternalInput").ap()
           for k in ("wr", "wk", "wv")}
    mu_h = {k: nc.dram_tensor(k, [128, NCH], F32, kind="ExternalInput").ap()
            for k in ("mur", "muk")}
    y_h = nc.dram_tensor("y", [D, T_OWN], F32, kind="ExternalOutput").ap()

    es = ExitStack()
    sb = lambda n, s, d=F32: es.enter_context(nc.sbuf_tensor(n, s, d)).ap()
    psb = lambda n, s, d=F32: es.enter_context(nc.psum_tensor(n, s, d)).ap()
    with es:
        sxs = sb("sxs", [128, NCH, 512])
        sxo = sb("sxo", [128, NCH, 512])
        swf = sb("swf", [128, 2, D])
        sw = {k: sb("sw_" + k, [128, NCH, D], BF16) for k in ("wr", "wk", "wv")}
        smu = {k: sb("smu_" + k, [128, NCH]) for k in ("mur", "muk")}
        som = {k: sb("som_" + k, [128, NCH]) for k in ("mur", "muk")}
        sxln = sb("sxln", [128, NCH, T_PAD], BF16)
        sxr = sb("sxr", [128, NCH, 512], BF16)
        sxk = sb("sxk", [128, NCH, 512], BF16)
        sr = sb("sr", [128, NCH, 512])
        sk2 = sb("sk2", [128, NCH, 512], BF16)
        sy = sb("sy", [128, NCH, 512])
        ssq = sb("ssq", [128, 512])
        ssq2 = sb("ssq2", [128, 512])
        srow_a = sb("srow_a", [1, 512])
        srow_b = sb("srow_b", [1, 512])
        sones = sb("sones", [128, 1])
        sones_r = sb("sones_r", [1, 128])
        seps = sb("seps", [1, 1])
        prow_m = psb("prow_m", [1, 512])
        prow_v = psb("prow_v", [1, 512])
        pbcA = psb("pbcA", [128, 512])
        pbcB = psb("pbcB", [128, 512])
        pmm = psb("pmm", [128, 512])
        pmm2 = psb("pmm2", [128, 512])

        p = Prog(nc, es)
        li = p.sem("li")
        lx = p.sem("lx")
        lw = [p.sem("lw0"), p.sem("lw1")]
        t_c = p.sem("t_c")
        v_c = p.sem("v_c")
        s_c = p.sem("s_c")
        st = p.sem("st")

        block = es.enter_context(nc.Block())

        for k in ("mur", "muk"):
            p.emit("sync", lambda e, k=k: e.dma_start(out=smu[k], in_=mu_h[k]),
                   inc=li, inc_n=16)
        mu_ld = li.v
        wcast_done = {}
        prev_cast = [0, 0]
        for k in ("wr", "wk", "wv"):
            for i in range(NCH):
                ib = i % 2
                p.emit("sync", lambda e, k=k, i=i, ib=ib: e.dma_start(
                    out=swf[:, ib, :], in_=w_h[k][i * 128:(i + 1) * 128, :]),
                       waits=((v_c, prev_cast[ib]),),
                       inc=lw[ib], inc_n=16)
                wl = lw[ib].v
                p.emit("vector", lambda e, k=k, i=i, ib=ib: e.tensor_copy(
                    out=sw[k][:, i, :], in_=swf[:, ib, :]),
                       waits=((lw[ib], wl),), inc=v_c)
                prev_cast[ib] = v_c.v
            wcast_done[k] = v_c.v
        p.emit("vector", lambda e: e.tensor_scalar(
            out=som["mur"], in0=smu["mur"], scalar1=-1.0, scalar2=1.0,
            op0=ALU.mult, op1=ALU.add), waits=((li, mu_ld),))
        p.emit("vector", lambda e: e.tensor_scalar(
            out=som["muk"], in0=smu["muk"], scalar1=-1.0, scalar2=1.0,
            op0=ALU.mult, op1=ALU.add), inc=v_c)
        p.emit("vector", lambda e: e.memset(sones, 1.0))
        p.emit("vector", lambda e: e.memset(sones_r, 1.0))
        p.emit("vector", lambda e: e.memset(seps, EPS), inc=v_c)
        ones_done = v_c.v

        for (s0, w) in strips(T_PAD, 512):
            t_snap, v_snap, s_snap = t_c.v, v_c.v, s_c.v
            for i in range(NCH):
                p.emit("sync", lambda e, i=i, s0=s0, w=w: e.dma_start(
                    out=sxs[:, i, :w], in_=x_h[i * 128:(i + 1) * 128, s0:s0 + w]),
                       waits=((v_c, v_snap),) if i == 0 else (),
                       inc=lx, inc_n=16)
            x_ld = lx.v
            emit_ln_rows(
                p, x_chunks=[sxs[:, i, :w] for i in range(NCH)], w=w,
                prow_m=prow_m[:, :w], prow_v=prow_v[:, :w], sq=ssq2[:, :w],
                row_a=srow_a[:, :w], row_b=srow_b[:, :w],
                ones=sones, t_c=t_c, v_c=v_c, s_c=s_c, eps_ap=seps,
                first_t_waits=((lx, x_ld), (v_c, max(ones_done, v_snap))),
                first_s_waits=((lx, x_ld), (t_c, t_snap), (v_c, v_snap)))
            rows_done = v_c.v
            p.emit("tensor", lambda e, w=w: e.matmul(
                pbcA[:, :w], sones_r, srow_a[:, :w], start=True, stop=True),
                   waits=((v_c, rows_done),))
            p.emit("tensor", lambda e, w=w: e.matmul(
                pbcB[:, :w], sones_r, srow_b[:, :w], start=True, stop=True),
                   inc=t_c)
            bc_done = t_c.v
            for i in range(NCH):
                p.emit("vector", lambda e, i=i, s0=s0, w=w: e.tensor_mul(
                    out=ssq[:, :w], in0=sxs[:, i, :w], in1=pbcA[:, :w]),
                       waits=((t_c, bc_done),) if i == 0 else ())
                p.emit("vector", lambda e, i=i, s0=s0, w=w: e.tensor_add(
                    out=sxln[:, i, s0:s0 + w], in0=ssq[:, :w], in1=pbcB[:, :w]),
                       inc=v_c)
            p.wait("tensor", v_c, v_c.v)
        xln_done = v_c.v

        for si, (s0, w) in enumerate(strips(T_OWN, 512)):
            c0 = s0 + 1
            t_snap = t_c.v
            v_snap2 = v_c.v
            for i in range(NCH):
                p.emit("sync", lambda e, i=i, c0=c0, w=w: e.dma_start(
                    out=sxo[:, i, :w], in_=x_h[i * 128:(i + 1) * 128, c0:c0 + w]),
                       waits=((v_c, v_snap2),) if i == 0 else (),
                       inc=lx, inc_n=16)
            xo_ld = lx.v
            for i in range(NCH):
                p.emit("vector", lambda e, i=i, c0=c0, w=w: e.tensor_scalar_mul(
                    out=ssq[:, :w], in0=sxln[:, i, c0:c0 + w],
                    scalar1=smu["mur"][:, i:i + 1]),
                       waits=((v_c, xln_done), (t_c, t_snap)) if i == 0 else ())
                p.emit("vector", lambda e, i=i, c0=c0, w=w: e.scalar_tensor_tensor(
                    out=sxr[:, i, :w], in0=sxln[:, i, c0 - 1:c0 - 1 + w],
                    scalar=som["mur"][:, i:i + 1], in1=ssq[:, :w],
                    op0=ALU.mult, op1=ALU.add))
                p.emit("vector", lambda e, i=i, c0=c0, w=w: e.tensor_scalar_mul(
                    out=ssq[:, :w], in0=sxln[:, i, c0:c0 + w],
                    scalar1=smu["muk"][:, i:i + 1]))
                p.emit("vector", lambda e, i=i, c0=c0, w=w: e.scalar_tensor_tensor(
                    out=sxk[:, i, :w], in0=sxln[:, i, c0 - 1:c0 - 1 + w],
                    scalar=som["muk"][:, i:i + 1], in1=ssq[:, :w],
                    op0=ALU.mult, op1=ALU.add), inc=v_c)
            mix_done = v_c.v

            for o in range(NCH):
                for i in range(NCH):
                    p.emit("tensor", lambda e, o=o, i=i, w=w: e.matmul(
                        pmm[:, :w], sw["wr"][:, i, o * 128:(o + 1) * 128],
                        sxr[:, i, :w], start=(i == 0), stop=(i == NCH - 1)),
                           waits=((v_c, max(mix_done, wcast_done["wr"])),)
                           if i == 0 else (),
                           inc=t_c if i == NCH - 1 else None)
                mmr = t_c.v
                p.emit("scalar", lambda e, o=o, w=w: e.activation(
                    out=sr[:, o, :w], in_=pmm[:, :w], func=AF.Sigmoid),
                       waits=((t_c, mmr),), inc=s_c)
                for i in range(NCH):
                    p.emit("tensor", lambda e, o=o, i=i, w=w: e.matmul(
                        pmm2[:, :w], sw["wk"][:, i, o * 128:(o + 1) * 128],
                        sxk[:, i, :w], start=(i == 0), stop=(i == NCH - 1)),
                           waits=((v_c, wcast_done["wv"]),) if (o == 0 and i == 0) else (),
                           inc=t_c if i == NCH - 1 else None)
                mmk = t_c.v
                p.emit("scalar", lambda e, o=o, w=w: e.activation(
                    out=ssq2[:, :w], in_=pmm2[:, :w], func=AF.Relu),
                       waits=((t_c, mmk),))
                p.emit("scalar", lambda e, o=o, w=w: e.activation(
                    out=sk2[:, o, :w], in_=ssq2[:, :w], func=AF.Square), inc=s_c)
                k2_done = s_c.v
                p.wait("tensor", s_c, k2_done)

            for o in range(NCH):
                for i in range(NCH):
                    p.emit("tensor", lambda e, o=o, i=i, w=w: e.matmul(
                        pmm[:, :w], sw["wv"][:, i, o * 128:(o + 1) * 128],
                        sk2[:, i, :w], start=(i == 0), stop=(i == NCH - 1)),
                           inc=t_c if i == NCH - 1 else None)
                mmv = t_c.v
                p.emit("vector", lambda e, o=o, w=w: e.tensor_mul(
                    out=sy[:, o, :w], in0=sr[:, o, :w], in1=pmm[:, :w]),
                       waits=((t_c, mmv),))
                p.emit("vector", lambda e, o=o, w=w: e.tensor_add(
                    out=sy[:, o, :w], in0=sy[:, o, :w], in1=sxo[:, o, :w]),
                       waits=((lx, xo_ld),) if o == 0 else (), inc=v_c)
                ydone = v_c.v
                p.wait("tensor", v_c, ydone)
                p.emit("gpsimd", lambda e, o=o, s0=s0, w=w: e.dma_start(
                    out=y_h[o * 128:(o + 1) * 128, s0:s0 + w], in_=sy[:, o, :w]),
                       waits=((v_c, ydone),), inc=st, inc_n=16)
            p.wait("vector", st, st.v)

        p.wait("gpsimd", st, st.v)
        p.finalize(block)
    return nc



def shard_tokens_fm(x_bt_d):
    out = []
    for c in range(NCORES):
        b, s = divmod(c, 4)
        s0 = s * T_OWN
        sl = np.zeros((D, T_PAD), dtype=np.float32)
        sl[:, 1:T] = x_bt_d[b, s0:s0 + T_OWN, :].T
        if s0 > 0:
            sl[:, 0] = x_bt_d[b, s0 - 1, :]
        out.append(np.ascontiguousarray(sl))
    return out


def mu_fm(mu):
    return np.ascontiguousarray(np.asarray(mu, np.float32).reshape(NCH, 128).T)


def l1_ffn_in_maps(x, W_r, W_k, W_v, mu_r, mu_k):
    wr = np.ascontiguousarray(np.asarray(W_r, np.float32).T)
    wk = np.ascontiguousarray(np.asarray(W_k, np.float32).T)
    wv = np.ascontiguousarray(np.asarray(W_v, np.float32).T)
    mur = mu_fm(mu_r)
    muk = mu_fm(mu_k)
    xs = shard_tokens_fm(x)
    return [dict(x=xs[c], wr=wr, wk=wk, wv=wv, mur=mur, muk=muk)
            for c in range(NCORES)]


def assemble_fm_out(results, key="y"):
    out = np.empty((BATCH, SEQ, D), dtype=np.float32)
    for c in range(NCORES):
        b, s = divmod(c, 4)
        out[b, s * T_OWN:(s + 1) * T_OWN, :] = results[c][key].T
    return out






def strip_barrier(p, t_c, v_c, s_c):
    ct, cv, cs = t_c.v, v_c.v, s_c.v
    p.wait("tensor", v_c, cv)
    p.wait("tensor", s_c, cs)
    p.wait("scalar", v_c, cv)
    p.wait("scalar", t_c, ct)
    p.wait("vector", t_c, ct)
    p.wait("vector", s_c, cs)


def emit_attn_b5(p, *, h, skne, sqne, svtm, psc, sp, pkv, snrm, sto, south,
                 sones_r, osc_h, t_c, v_c, s_c, g_c,
                 first_waits=(), qs_range=None, qne_col=None):
    if qs_range is None:
        qs_range = range(QSTRIPS)
    if qne_col is None:
        qne_col = lambda qs: qs * 512
    first = True
    for qs in qs_range:
        q0 = qs * 512
        qc = qne_col(qs)
        g_snap = g_c.v
        v_snap_b5 = v_c.v
        ngroups = KCH // 2
        fill_done = [0] * ngroups
        exp_done = [0] * ngroups
        for g in range(ngroups):
            gb = g % 2
            for cc in range(2):
                kchunk = g * 2 + cc
                k0 = kchunk * 128
                p.emit("tensor", lambda e, gb=gb, cc=cc, k0=k0, qc=qc: e.matmul(
                    psc[:, gb, cc * 512:(cc + 1) * 512],
                    skne[:, k0:k0 + 128], sqne[:, qc:qc + 512],
                    start=True, stop=True),
                       waits=tuple(
                           (list(first_waits) if (first and g == 0 and cc == 0) else [])
                           + ([(s_c, exp_done[g - 2])] if (g >= 2 and cc == 0) else [])),
                       inc=t_c if cc == 1 else None)
            fill_done[g] = t_c.v
            p.emit("scalar", lambda e, gb=gb: e.activation(
                out=sp[:, gb, :], in_=psc[:, gb, :], func=AF.Exp),
                   waits=((t_c, fill_done[g]),), inc=s_c)
            exp_done[g] = s_c.v
            for cc in range(2):
                kchunk = g * 2 + cc
                p.emit("tensor", lambda e, gb=gb, cc=cc, kchunk=kchunk: e.matmul(
                    pkv[0:65, :], svtm[:, kchunk, :], sp[:, gb, cc * 512:(cc + 1) * 512],
                    start=(kchunk == 0), stop=(kchunk == KCH - 1)),
                       waits=((s_c, exp_done[g]),) if cc == 0 else (),
                       inc=t_c if (cc == 1 and kchunk == KCH - 1) else None)
        pv_done = t_c.v
        p.emit("vector", lambda e: e.reciprocal(out=snrm[:, :512], in_=pkv[64:65, :512]),
               waits=((t_c, pv_done),), inc=v_c)
        den_done = v_c.v
        p.emit("scalar", lambda e: e.activation(
            out=sto[:, :], in_=pkv[0:64, :512], func=AF.Copy),
               waits=((t_c, pv_done), (v_c, v_snap_b5)), inc=s_c)
        out_evac = s_c.v
        p.emit("tensor", lambda e: e.matmul(
            psc[0:64, 0, 0:512],
            sones_r[:, 0:64], snrm[:, :512], start=True, stop=True),
               waits=((v_c, den_done), (s_c, exp_done[ngroups - 1]),), inc=t_c)
        den_bc = t_c.v
        p.emit("vector", lambda e: e.tensor_mul(
            out=south, in0=sto, in1=psc[0:64, 0, 0:512]),
               waits=((t_c, den_bc), (s_c, out_evac), (g_c, g_snap)), inc=v_c)
        south_done = v_c.v
        p.emit("gpsimd", lambda e, h=h, q0=q0: e.dma_start(
            out=osc_h[h * 64:(h + 1) * 64, q0:q0 + 512], in_=south),
               waits=((v_c, south_done),), inc=g_c, inc_n=16)
        p.wait("tensor", v_c, south_done)
        p.wait("scalar", v_c, south_done)
        first = False






NHEAD = 16
HD = 64
KCH = SEQ // 128
QSTRIPS = T_OWN // 512


def build_l1_attn():
    nc = bass.Bass("TRN2", target_bir_lowering=False, debug=False)
    xf_h = nc.dram_tensor("xf", [D, SEQ], F32, kind="ExternalInput").ap()
    wq_h = nc.dram_tensor("wq", [NHEAD, D, HD], F32, kind="ExternalInput").ap()
    wkv_h = nc.dram_tensor("wkv", [NHEAD, D, 128], F32, kind="ExternalInput").ap()
    ident_h = nc.dram_tensor("ident", [128, 128], BF16, kind="ExternalInput").ap()
    y_h = nc.dram_tensor("y", [D, T_OWN], F32, kind="ExternalOutput").ap()
    osc_h = nc.dram_tensor("osc", [D, T_OWN], BF16, kind="Internal").ap()

    es = ExitStack()
    sb = lambda n, s, d=F32: es.enter_context(nc.sbuf_tensor(n, s, d)).ap()
    psb = lambda n, s, d=F32: es.enter_context(nc.psum_tensor(n, s, d)).ap()
    with es:
        sxln = sb("sxln", [128, NCH, SEQ], BF16)
        skne = sb("skne", [65, SEQ], BF16)
        svtm = sb("svtm", [128, KCH, 65], BF16)
        sqne = sb("sqne", [65, T_OWN], BF16)
        swkv = sb("swkv", [128, 2, NCH, 128], BF16)
        swq = sb("swq", [128, 2, NCH, HD], BF16)
        swf = sb("swf", [128, 2, 128])
        sxf = sb("sxf", [128, NCH, 512])
        svfm = sb("svfm", [64, 512], BF16)
        sto = sb("sto", [64, 512], BF16)
        sp = sb("sp", [128, 2, 1024], BF16)
        snrm = sb("snrm", [1, 512])
        skmax = sb("skmax", [1, 1])
        srow_a = sb("srow_a", [1, 512])
        srow_b = sb("srow_b", [1, 512])
        ssq = sb("ssq", [128, 512])
        ssq2 = sb("ssq2", [128, 512])
        sident = sb("sident", [128, 128], BF16)
        sones = sb("sones", [128, 1])
        sones_r = sb("sones_r", [1, 128])
        seps = sb("seps", [1, 1])
        south = sb("south", [64, 512], BF16)
        sres = sb("sres", [128, NCH, 512], BF16)
        prow_m = psb("prow_m", [1, 512])
        prow_v = psb("prow_v", [1, 512])
        pkv = psb("pkv", [128, 512])
        ptr = psb("ptr", [128, 64], BF16)
        psc = psb("psc", [128, 2, 1024])

        p = Prog(nc, es)
        li = p.sem("li")
        lx = p.sem("lx")
        lw = [p.sem("lw0"), p.sem("lw1")]
        t_c = p.sem("t_c")
        v_c = p.sem("v_c")
        s_c = p.sem("s_c")
        g_c = p.sem("g_c")
        st = p.sem("st")

        block = es.enter_context(nc.Block())

        p.emit("sync", lambda e: e.dma_start(out=sident, in_=ident_h), inc=li, inc_n=16)
        p.emit("vector", lambda e: e.memset(sones, 1.0))
        p.emit("vector", lambda e: e.memset(sones_r, 1.0))
        p.emit("vector", lambda e: e.memset(seps, EPS))
        p.emit("vector", lambda e: e.memset(skne[64:65, :], 1.0))
        p.emit("vector", lambda e: e.memset(svtm[:, :, 64], 1.0), inc=v_c)
        init_done = v_c.v

        for (s0, w) in strips(SEQ, 512):
            t_snap, v_snap, s_snap = t_c.v, v_c.v, s_c.v
            for i in range(NCH):
                p.emit("sync", lambda e, i=i, s0=s0, w=w: e.dma_start(
                    out=sxf[:, i, :w], in_=xf_h[i * 128:(i + 1) * 128, s0:s0 + w]),
                       waits=((v_c, v_snap),) if i == 0 else (),
                       inc=lx, inc_n=16)
            x_ld = lx.v
            emit_ln_rows(
                p, x_chunks=[sxf[:, i, :w] for i in range(NCH)], w=w,
                prow_m=prow_m[:, :w], prow_v=prow_v[:, :w], sq=ssq2[:, :w],
                row_a=srow_a[:, :w], row_b=srow_b[:, :w],
                ones=sones, t_c=t_c, v_c=v_c, s_c=s_c, eps_ap=seps,
                first_t_waits=((lx, x_ld), (v_c, max(init_done, v_snap))),
                first_s_waits=((lx, x_ld), (t_c, t_snap), (v_c, v_snap)))
            rows_done = v_c.v
            p.emit("tensor", lambda e, w=w: e.matmul(
                pkv[:, :w], sones_r, srow_a[:, :w], start=True, stop=True),
                   waits=((v_c, rows_done),))
            p.emit("tensor", lambda e, w=w: e.matmul(
                psc[:, 0, :w], sones_r, srow_b[:, :w], start=True, stop=True),
                   inc=t_c)
            bc_done = t_c.v
            for i in range(NCH):
                p.emit("vector", lambda e, i=i, w=w: e.tensor_mul(
                    out=ssq[:, :w], in0=sxf[:, i, :w], in1=pkv[:, :w]),
                       waits=((t_c, bc_done),) if i == 0 else ())
                p.emit("vector", lambda e, i=i, s0=s0, w=w: e.tensor_add(
                    out=sxln[:, i, s0:s0 + w], in0=ssq[:, :w], in1=psc[:, 0, :w]),
                       inc=v_c)
            p.wait("tensor", v_c, v_c.v)
        xln_done = v_c.v

        for h in range(NHEAD):
            hb = h % 2
            cast_prev = [v_c.v, v_c.v]
            for i in range(NCH):
                ib = i % 2
                p.emit("sync", lambda e, h=h, i=i, ib=ib: e.dma_start(
                    out=swf[:, ib, :], in_=wkv_h[h, i * 128:(i + 1) * 128, :]),
                       waits=((v_c, cast_prev[ib]),), inc=lw[ib], inc_n=16)
                wld = lw[ib].v
                p.emit("vector", lambda e, hb=hb, i=i, ib=ib: e.tensor_copy(
                    out=swkv[:, hb, i, :], in_=swf[:, ib, :]),
                       waits=((lw[ib], wld),), inc=v_c)
                cast_prev[ib] = v_c.v
            kvw_done = v_c.v
            for i in range(NCH):
                ib = i % 2
                p.emit("sync", lambda e, h=h, i=i, ib=ib: e.dma_start(
                    out=swf[:, ib, :HD], in_=wq_h[h, i * 128:(i + 1) * 128, :]),
                       waits=((v_c, cast_prev[ib]),), inc=lw[ib], inc_n=16)
                wld = lw[ib].v
                p.emit("vector", lambda e, hb=hb, i=i, ib=ib: e.tensor_copy(
                    out=swq[:, hb, i, :], in_=swf[:, ib, :HD]),
                       waits=((lw[ib], wld),), inc=v_c)
                cast_prev[ib] = v_c.v
            qw_done = v_c.v

            p.emit("vector", lambda e: e.memset(skmax, 0.0), inc=v_c)
            prev_head_attn_t = t_c.v
            for (s0, w) in strips(SEQ, 512):
                strip_barrier(p, t_c, v_c, s_c)
                for i in range(NCH):
                    p.emit("tensor", lambda e, hb=hb, i=i, s0=s0, w=w: e.matmul(
                        pkv[:, :w], swkv[:, hb, i, :], sxln[:, i, s0:s0 + w],
                        start=(i == 0), stop=(i == NCH - 1)),
                           waits=((v_c, max(kvw_done, xln_done)),) if i == 0 else (),
                           inc=t_c if i == NCH - 1 else None)
                kv_mm = t_c.v
                p.emit("scalar", lambda e, s0=s0, w=w: e.activation(
                    out=skne[0:64, s0:s0 + w], in_=pkv[0:64, :w], func=AF.Copy),
                       waits=((t_c, kv_mm),))
                p.emit("scalar", lambda e, w=w: e.activation(
                    out=svfm[:, :w], in_=pkv[64:128, :w], func=AF.Copy), inc=s_c)
                kv_evac = s_c.v
                p.emit("scalar", lambda e, s0=s0, w=w: e.activation(
                    out=ssq2[0:64, :w], in_=pkv[0:64, :w], func=AF.Square), inc=s_c)
                ksq_done = s_c.v
                p.wait("tensor", s_c, kv_evac)
                p.emit("tensor", lambda e, w=w: e.matmul(
                    prow_v[:, :w], sones[0:64], ssq2[0:64, :w], start=True, stop=True),
                       waits=((s_c, ksq_done),), inc=t_c)
                knorm_mm = t_c.v
                p.emit("vector", lambda e, w=w: e.reduce_max(
                    out=snrm[:, 0:1], in_=prow_v[:, :w], axis=AX.X),
                       waits=((t_c, knorm_mm),))
                p.emit("vector", lambda e: e.tensor_max(
                    out=skmax, in0=skmax, in1=snrm[:, 0:1]), inc=v_c)
                for j in range(w // 128):
                    kchunk = s0 // 128 + j
                    p.emit("tensor", lambda e, j=j: e.transpose(
                        ptr[:, :], svfm[:, j * 128:(j + 1) * 128], sident[0:64, 0:64]),
                           waits=((s_c, kv_evac),) if j == 0 else (), inc=t_c)
                    tr_done = t_c.v
                    p.emit("vector", lambda e, kchunk=kchunk: e.tensor_copy(
                        out=svtm[:, kchunk, 0:64], in_=ptr[:, :]),
                           waits=((t_c, tr_done),), inc=v_c)
                    p.wait("tensor", v_c, v_c.v)
                p.wait("scalar", t_c, t_c.v)
            kn_done_v = v_c.v
            kn_done_s = s_c.v

            p.emit("scalar", lambda e: e.activation(out=skmax, in_=skmax, func=AF.Ln),
                   waits=((v_c, kn_done_v),))
            p.emit("scalar", lambda e: e.activation(out=skmax, in_=skmax, func=AF.Exp,
                                                    scale=0.5), inc=s_c)
            kmax_done = s_c.v

            for (s0, w) in strips(T_OWN, 512):
                strip_barrier(p, t_c, v_c, s_c)
                for i in range(NCH):
                    p.emit("tensor", lambda e, hb=hb, i=i, s0=s0, w=w: e.matmul(
                        pkv[0:64, :w], swq[:, hb, i, :], sxln[:, i, s0:s0 + w],
                        start=(i == 0), stop=(i == NCH - 1)),
                           waits=((v_c, qw_done),) if i == 0 else (),
                           inc=t_c if i == NCH - 1 else None)
                q_mm = t_c.v
                p.emit("scalar", lambda e, s0=s0, w=w: e.activation(
                    out=sqne[0:64, s0:s0 + w], in_=pkv[0:64, :w], func=AF.Copy,
                    scale=0.125), waits=((t_c, q_mm),))
                p.emit("scalar", lambda e, w=w: e.activation(
                    out=ssq2[0:64, :w], in_=pkv[0:64, :w], func=AF.Square), inc=s_c)
                qsq_done = s_c.v
                p.wait("tensor", s_c, qsq_done)
                p.emit("tensor", lambda e, w=w: e.matmul(
                    prow_v[:, :w], sones[0:64], ssq2[0:64, :w], start=True, stop=True),
                       inc=t_c)
                qn_mm = t_c.v
                p.emit("scalar", lambda e, w=w: e.activation(
                    out=snrm[:, :w], in_=prow_v[:, :w], func=AF.Ln),
                       waits=((t_c, qn_mm),))
                p.emit("scalar", lambda e, w=w: e.activation(
                    out=snrm[:, :w], in_=snrm[:, :w], func=AF.Exp,
                    scale=0.5), inc=s_c)
                qrow_done = s_c.v
                p.wait("tensor", s_c, qrow_done)
                p.emit("vector", lambda e, s0=s0, w=w: e.tensor_scalar(
                    out=sqne[64:65, s0:s0 + w], in0=snrm[:, :w], scalar1=skmax,
                    scalar2=-0.125, op0=ALU.mult, op1=ALU.mult),
                       waits=((s_c, max(qrow_done, kmax_done)),))
                p.emit("vector", lambda e, s0=s0, w=w: e.tensor_scalar_add(
                    out=sqne[64:65, s0:s0 + w], in0=sqne[64:65, s0:s0 + w],
                    scalar1=-0.05), inc=v_c)
                p.wait("scalar", v_c, v_c.v)
            bias_done = v_c.v

            emit_attn_b5(p, h=h, skne=skne, sqne=sqne, svtm=svtm, psc=psc,
                         sp=sp, pkv=pkv, snrm=snrm, sto=sto, south=south,
                         sones_r=sones_r, osc_h=osc_h,
                         t_c=t_c, v_c=v_c, s_c=s_c, g_c=g_c,
                         first_waits=((v_c, bias_done), (s_c, kn_done_s)))
        attn_out_dma = g_c.v

        for (s0, w) in strips(T_OWN, 512):
            v_snap = v_c.v
            for i in range(NCH):
                p.emit("sync", lambda e, i=i, s0=s0, w=w: e.dma_start(
                    out=sxf[:, i, :w], in_=xf_h[i * 128:(i + 1) * 128, s0:s0 + w]),
                       waits=((v_c, v_snap),) if i == 0 else (), inc=lx, inc_n=16)
            x_ld2 = lx.v
            p.emit("sync", lambda e, s0=s0, w=w: e.dma_start(
                out=sres[:, :, :w],
                in_=osc_h.rearrange("(c p) t -> p c t", p=128)[:, :, s0:s0 + w]),
                   waits=((g_c, attn_out_dma),), inc=lx, inc_n=16)
            o_ld = lx.v
            for i in range(NCH):
                p.emit("vector", lambda e, i=i, w=w: e.tensor_add(
                    out=ssq[:, :w], in0=sxf[:, i, :w], in1=sres[:, i, :w]),
                       waits=((lx, max(x_ld2, o_ld)),) if i == 0 else ())
                p.emit("vector", lambda e, i=i, w=w: e.tensor_copy(
                    out=sxf[:, i, :w], in_=ssq[:, :w]), inc=v_c)
                ydone = v_c.v
                p.emit("gpsimd", lambda e, i=i, s0=s0, w=w: e.dma_start(
                    out=y_h[i * 128:(i + 1) * 128, s0:s0 + w], in_=sxf[:, i, :w]),
                       waits=((v_c, ydone),), inc=st, inc_n=16)
            p.wait("sync", st, st.v)

        p.wait("gpsimd", st, st.v)
        p.finalize(block)
    return nc


def l1_attn_in_maps(x2):
    maps = []
    for c in range(NCORES):
        b, s = divmod(c, 4)
        xf = np.ascontiguousarray(np.roll(x2[b], -s * T_OWN, axis=0).T)
        maps.append({"xf": xf})
    return maps


def l1_attn_weight_maps(Wq, Wk, Wv):
    wqT = np.asarray(Wq, np.float32).T
    wkT = np.asarray(Wk, np.float32).T
    wvT = np.asarray(Wv, np.float32).T
    wq = np.stack([np.ascontiguousarray(wqT[:, h * HD:(h + 1) * HD])
                   for h in range(NHEAD)])
    wkv = np.stack([np.ascontiguousarray(np.concatenate(
        [wkT[:, h * HD:(h + 1) * HD], wvT[:, h * HD:(h + 1) * HD]], axis=1))
        for h in range(NHEAD)])
    ident = np.eye(128, dtype=BF)
    return dict(wq=wq, wkv=wkv, ident=ident)


def assemble_l1_attn_out(results):
    out = np.empty((BATCH, SEQ, D), dtype=np.float32)
    for c in range(NCORES):
        b, s = divmod(c, 4)
        out[b, s * T_OWN:(s + 1) * T_OWN, :] = results[c]["y"].T
    return out



def build_l0_attn():
    nc = bass.Bass("TRN2", target_bir_lowering=False, debug=False)
    xf_h = nc.dram_tensor("xf", [D, SEQ], F32, kind="ExternalInput").ap()
    cos_h = nc.dram_tensor("costab", [32, SEQ], F32, kind="ExternalInput").ap()
    sin_h = nc.dram_tensor("sintab", [32, SEQ], F32, kind="ExternalInput").ap()
    wq_h = nc.dram_tensor("wq", [NHEAD, D, HD], F32, kind="ExternalInput").ap()
    wkv_h = nc.dram_tensor("wkv", [NHEAD, D, 128], F32, kind="ExternalInput").ap()
    qb_h = nc.dram_tensor("qb", [HD, NHEAD], F32, kind="ExternalInput").ap()
    kvb_h = nc.dram_tensor("kvb", [128, NHEAD], F32, kind="ExternalInput").ap()
    lnwb_h = nc.dram_tensor("lnwb", [HD, 2], F32, kind="ExternalInput").ap()
    qnw_h = nc.dram_tensor("qnw", [128, NCH], F32, kind="ExternalInput").ap()
    qnb_h = nc.dram_tensor("qnb", [128, NCH], F32, kind="ExternalInput").ap()
    kvnw_h = nc.dram_tensor("kvnw", [128, NCH], F32, kind="ExternalInput").ap()
    kvnb_h = nc.dram_tensor("kvnb", [128, NCH], F32, kind="ExternalInput").ap()
    rh_h = nc.dram_tensor("rh", [128, NCH], F32, kind="ExternalInput").ap()
    rhb_h = nc.dram_tensor("rhb", [1, 1], F32, kind="ExternalInput").ap()
    ow_h = nc.dram_tensor("ow", [D, D], F32, kind="ExternalInput").ap()
    ob_h = nc.dram_tensor("ob", [128, NCH], F32, kind="ExternalInput").ap()
    ident_h = nc.dram_tensor("ident", [128, 128], BF16, kind="ExternalInput").ap()
    y_h = nc.dram_tensor("y", [D, T_OWN], F32, kind="ExternalOutput").ap()
    osc_h = nc.dram_tensor("osc", [D, T_OWN], BF16, kind="Internal").ap()

    AW = 128

    es = ExitStack()
    sb = lambda n, s, d=F32: es.enter_context(nc.sbuf_tensor(n, s, d)).ap()
    psb = lambda n, s, d=F32: es.enter_context(nc.psum_tensor(n, s, d)).ap()
    with es:
        st_ = sb("st_", [128, NCH, SEQ], BF16)
        srad = sb("srad", [1, SEQ], BF16)
        skne = sb("skne", [65, SEQ], BF16)
        svtm = sb("svtm", [128, KCH, 65], BF16)
        sqne = sb("sqne", [65, 512], BF16)
        stabA = sb("stabA", [64, 512])
        stabB = sb("stabB", [64, 512])
        sswap = sb("sswap", [64, 512])
        swkv = sb("swkv", [128, NCH, 128], BF16)
        swq = sb("swq", [128, NCH, HD], BF16)
        swf = sb("swf", [128, 2, 128])
        sbias = sb("sbias", [128, 1])
        sqbias = sb("sqbias", [64, 1])
        skr = sb("skr", [64, 512])
        skrot = sb("skrot", [64, 512])
        svfm = sb("svfm", [64, 512], BF16)
        sto = svfm
        south = svfm
        sp = sb("sp", [128, 2, 1024], BF16)
        sxf = sb("sxf", [128, NCH, AW])
        ssq = sb("ssq", [128, 512])
        ssq2 = sb("ssq2", [128, 512])
        srow1 = sb("srow1", [1, 512])
        srow2 = sb("srow2", [1, 512])
        srow3 = sb("srow3", [1, 512])
        snrm = sb("snrm", [1, 512])
        skmax = sb("skmax", [1, 1])
        ssmall = sb("ssmall", [128, 16])
        sqb_s = sb("sqb_s", [64, 16])
        slnwb = sb("slnwb", [64, 2])
        slnq = sb("slnq", [64, 2])
        sqnw = sb("sqnw", [128, NCH])
        sqnb = sb("sqnb", [128, NCH])
        skvnw = sb("skvnw", [128, NCH])
        skvnb = sb("skvnb", [128, NCH])
        srh = sb("srh", [128, NCH])
        srhb = sb("srhb", [1, 1])
        srhb01 = sb("srhb01", [1, 1])
        sob = sb("sob", [128, NCH])
        sow = sb("sow", [128, 1, 128], BF16)
        sident = sb("sident", [64, 64], BF16)
        sones = sb("sones", [128, 1])
        sones_r = sb("sones_r", [1, 128])
        seps = sb("seps", [1, 1])
        sy = sb("sy", [128, AW])
        prow_m = psb("prow_m", [1, 512])
        prow_v = psb("prow_v", [1, 512])
        pkv = psb("pkv", [128, 512])
        ptr = psb("ptr", [128, 64], BF16)
        psc = psb("psc", [128, 2, 1024])

        p = Prog(nc, es)
        li = p.sem("li")
        lx = p.sem("lx")
        lt = p.sem("lt")
        lw = [p.sem("lw0"), p.sem("lw1")]
        t_c = p.sem("t_c")
        v_c = p.sem("v_c")
        s_c = p.sem("s_c")
        g_c = p.sem("g_c")
        st = p.sem("st")

        sres = sp.rearrange("p a b -> p (a b)").rearrange(
            "p (c t) -> p c t", c=NCH)[:, :, :AW]

        block = es.enter_context(nc.Block())

        for name, dst, srch in (("ident", sident, ident_h), ("kvb", ssmall, kvb_h),
                                ("qb", sqb_s, qb_h), ("lnwb", slnwb, lnwb_h),
                                ("qnw", sqnw, qnw_h), ("qnb", sqnb, qnb_h),
                                ("kvnw", skvnw, kvnw_h), ("kvnb", skvnb, kvnb_h),
                                ("rh", srh, rh_h), ("rhb", srhb, rhb_h),
                                ("ob", sob, ob_h)):
            p.emit("sync", lambda e, dst=dst, srch=srch: e.dma_start(
                out=dst, in_=srch[0:64, 0:64] if dst is sident else srch),
                   inc=li, inc_n=16)
        init_ld = li.v
        p.emit("vector", lambda e: e.memset(sones, 1.0))
        p.emit("vector", lambda e: e.memset(sones_r, 1.0))
        p.emit("vector", lambda e: e.memset(seps, EPS))
        p.emit("vector", lambda e: e.memset(skne[64:65, :], 1.0))
        p.emit("vector", lambda e: e.memset(svtm[:, :, 64], 1.0))
        p.emit("vector", lambda e: e.tensor_scalar_mul(out=srhb01, in0=srhb, scalar1=0.01),
               waits=((li, init_ld),))
        p.emit("vector", lambda e: e.tensor_scalar_mul(
            out=slnq[:, :], in0=slnwb[:, :], scalar1=0.125), inc=v_c)
        init_done = v_c.v

        p.emit("tensor", lambda e: e.matmul(prow_m[:, 0:NCH], sones, srh,
                                            start=True, stop=True),
               waits=((li, init_ld), (v_c, init_done)), inc=t_c)
        rhs_mm = t_c.v
        p.emit("vector", lambda e: e.reduce_sum(out=snrm[:, 0:1], in_=prow_m[:, 0:NCH],
                                                axis=AX.X), waits=((t_c, rhs_mm),))
        srhsum = sb("srhsum", [1, 1])
        p.emit("vector", lambda e: e.tensor_copy(out=srhsum, in_=snrm[:, 0:1]), inc=v_c)
        pre_done = v_c.v

        prev_t_end, prev_v_end, prev_s_end = t_c.v, pre_done, s_c.v
        for (s0, w) in strips(SEQ, AW):
            for i in range(NCH):
                p.emit("sync", lambda e, i=i, s0=s0, w=w: e.dma_start(
                    out=sxf[:, i, :w], in_=xf_h[i * 128:(i + 1) * 128, s0:s0 + w]),
                       waits=((v_c, prev_v_end), (t_c, prev_t_end), (s_c, prev_s_end))
                       if i == 0 else (),
                       inc=lx, inc_n=16)
            x_ld = lx.v
            for i in range(NCH):
                p.emit("tensor", lambda e, i=i, w=w: e.matmul(
                    prow_m[:, :w], sones, sxf[:, i, :w],
                    start=(i == 0), stop=(i == NCH - 1)),
                       waits=((lx, x_ld), (v_c, prev_v_end)) if i == 0 else (),
                       inc=t_c if i == NCH - 1 else None)
            m_done = t_c.v
            for i in range(NCH):
                p.emit("tensor", lambda e, i=i, w=w: e.matmul(
                    psc[0:1, 1, :w], srh[:, i:i + 1], sxf[:, i, :w],
                    start=(i == 0), stop=(i == NCH - 1)),
                       inc=t_c if i == NCH - 1 else None)
            r_done = t_c.v
            for i in range(NCH):
                p.emit("scalar", lambda e, i=i, w=w: e.activation(
                    out=ssq2[:, :w], in_=sxf[:, i, :w], func=AF.Square),
                       waits=((lx, x_ld), (t_c, prev_t_end), (v_c, prev_v_end))
                       if i == 0 else ((t_c, t_c.v),), inc=s_c)
                sq_i = s_c.v
                p.emit("tensor", lambda e, i=i, w=w: e.matmul(
                    prow_v[:, :w], sones, ssq2[:, :w],
                    start=(i == 0), stop=(i == NCH - 1)),
                       waits=((s_c, sq_i),), inc=t_c)
            v_done = t_c.v
            p.emit("vector", lambda e, w=w: e.tensor_scalar_mul(
                out=srow1[:, :w], in0=prow_m[:, :w], scalar1=1.0 / D),
                   waits=((t_c, m_done),))
            p.emit("vector", lambda e, w=w: e.tensor_scalar_mul(
                out=srow2[:, :w], in0=prow_v[:, :w], scalar1=1.0 / D),
                   waits=((t_c, v_done),))
            p.emit("vector", lambda e, w=w: e.tensor_mul(
                out=srow3[:, :w], in0=srow1[:, :w], in1=srow1[:, :w]))
            p.emit("vector", lambda e, w=w: e.tensor_sub(
                out=srow2[:, :w], in0=srow2[:, :w], in1=srow3[:, :w]), inc=v_c)
            var_done = v_c.v
            p.emit("scalar", lambda e, w=w: e.activation(
                out=srow3[:, :w], in_=srow2[:, :w], func=AF.Ln, bias=seps),
                   waits=((v_c, var_done),))
            p.emit("scalar", lambda e, w=w: e.activation(
                out=srow3[:, :w], in_=srow3[:, :w], func=AF.Exp, scale=-0.5), inc=s_c)
            rstd1_done = s_c.v
            p.emit("vector", lambda e, w=w: e.tensor_scalar_mul(
                out=snrm[:, :w], in0=srow1[:, :w], scalar1=srhsum),
                   waits=((s_c, rstd1_done), (t_c, r_done)))
            p.emit("vector", lambda e, w=w: e.tensor_sub(
                out=snrm[:, :w], in0=psc[0:1, 1, :w], in1=snrm[:, :w]))
            p.emit("vector", lambda e, w=w: e.tensor_mul(
                out=snrm[:, :w], in0=snrm[:, :w], in1=srow3[:, :w]))
            p.emit("vector", lambda e, w=w: e.tensor_scalar(
                out=snrm[:, :w], in0=snrm[:, :w], scalar1=0.01, scalar2=srhb01,
                op0=ALU.mult, op1=ALU.add))
            p.emit("vector", lambda e, w=w: e.tensor_scalar_max(
                out=snrm[:, :w], in0=snrm[:, :w], scalar1=0.5))
            p.emit("vector", lambda e, w=w: e.tensor_scalar_min(
                out=snrm[:, :w], in0=snrm[:, :w], scalar1=2.0))
            p.emit("vector", lambda e, s0=s0, w=w: e.tensor_copy(
                out=srad[0:1, s0:s0 + w], in_=snrm[:, :w]), inc=v_c)
            p.emit("vector", lambda e, w=w: e.tensor_mul(
                out=snrm[:, :w], in0=srow2[:, :w], in1=srow3[:, :w]))
            p.emit("vector", lambda e, w=w: e.tensor_mul(
                out=snrm[:, :w], in0=snrm[:, :w], in1=srow3[:, :w]), inc=v_c)
            var2_done = v_c.v
            p.emit("scalar", lambda e, w=w: e.activation(
                out=snrm[:, :w], in_=snrm[:, :w], func=AF.Ln, bias=seps),
                   waits=((v_c, var2_done),))
            p.emit("scalar", lambda e, w=w: e.activation(
                out=snrm[:, :w], in_=snrm[:, :w], func=AF.Exp, scale=-0.5), inc=s_c)
            rstd2_done = s_c.v
            p.emit("vector", lambda e, w=w: e.tensor_mul(
                out=srow3[:, :w], in0=srow3[:, :w], in1=snrm[:, :w]),
                   waits=((s_c, rstd2_done),))
            p.emit("vector", lambda e, w=w: e.tensor_mul(
                out=srow1[:, :w], in0=srow1[:, :w], in1=srow3[:, :w]))
            p.emit("vector", lambda e, w=w: e.tensor_scalar_mul(
                out=srow1[:, :w], in0=srow1[:, :w], scalar1=-1.0), inc=v_c)
            rows_done = v_c.v
            p.emit("tensor", lambda e, w=w: e.matmul(
                psc[:, 0, 0:512][:, :w], sones_r, srow3[:, :w], start=True, stop=True),
                   waits=((v_c, rows_done),))
            p.emit("tensor", lambda e, w=w: e.matmul(
                psc[:, 0, 512:1024][:, :w], sones_r, srow1[:, :w], start=True, stop=True),
                   inc=t_c)
            bc_done = t_c.v
            for i in range(NCH):
                p.emit("vector", lambda e, i=i, w=w: e.tensor_mul(
                    out=ssq[:, :w], in0=sxf[:, i, :w], in1=psc[:, 0, 0:512][:, :w]),
                       waits=((t_c, bc_done),) if i == 0 else ())
                p.emit("vector", lambda e, i=i, s0=s0, w=w: e.tensor_add(
                    out=st_[:, i, s0:s0 + w], in0=ssq[:, :w],
                    in1=psc[:, 0, 512:1024][:, :w]), inc=v_c)
            prev_t_end, prev_v_end, prev_s_end = t_c.v, v_c.v, s_c.v
        a_done_v, a_done_t, a_done_s = v_c.v, t_c.v, s_c.v

        sones_rb = sb("sones_rb", [1, 128], BF16)
        p.emit("vector", lambda e: e.memset(sones_rb, 1.0), inc=v_c)

        for h in range(NHEAD):
            p.emit("vector", lambda e: e.memset(skmax, 0.0), inc=v_c)
            swf_free = [v_c.v, v_c.v]
            swf_free_t = [t_c.v, t_c.v]
            for i in range(NCH):
                ib = i % 2
                p.emit("sync", lambda e, h=h, i=i, ib=ib: e.dma_start(
                    out=swf[:, ib, :], in_=wkv_h[h, i * 128:(i + 1) * 128, :]),
                       waits=((v_c, swf_free[ib]), (t_c, swf_free_t[ib])),
                       inc=lw[ib], inc_n=16)
                wld = lw[ib].v
                p.emit("tensor", lambda e, i=i, ib=ib: e.matmul(
                    pkv[:, 0:1], swf[:, ib, :], skvnb[:, i:i + 1],
                    start=(i == 0), stop=(i == NCH - 1)),
                       waits=((lw[ib], wld), (li, init_ld)), inc=t_c)
                swf_free_t[ib] = t_c.v
                p.emit("vector", lambda e, i=i, ib=ib: e.tensor_scalar_mul(
                    out=swkv[:, i, :], in0=swf[:, ib, :], scalar1=skvnw[:, i:i + 1]),
                       waits=((lw[ib], wld),), inc=v_c)
                swf_free[ib] = v_c.v
            kvb_mm = t_c.v
            p.emit("vector", lambda e, h=h: e.tensor_add(
                out=sbias, in0=pkv[:, 0:1], in1=ssmall[:, h:h + 1]),
                   waits=((t_c, kvb_mm),), inc=v_c)
            kvprep = v_c.v
            for i in range(NCH):
                ib = i % 2
                p.emit("sync", lambda e, h=h, i=i, ib=ib: e.dma_start(
                    out=swf[:, ib, 0:HD], in_=wq_h[h, i * 128:(i + 1) * 128, :]),
                       waits=((v_c, max(swf_free[ib], kvprep)), (t_c, swf_free_t[ib])),
                       inc=lw[ib], inc_n=16)
                wld = lw[ib].v
                p.emit("tensor", lambda e, i=i, ib=ib: e.matmul(
                    pkv[0:64, 0:1], swf[:, ib, 0:HD], sqnb[:, i:i + 1],
                    start=(i == 0), stop=(i == NCH - 1)),
                       waits=((lw[ib], wld), (v_c, kvprep)), inc=t_c)
                swf_free_t[ib] = t_c.v
                p.emit("vector", lambda e, i=i, ib=ib: e.tensor_scalar_mul(
                    out=swq[:, i, :], in0=swf[:, ib, 0:HD], scalar1=sqnw[:, i:i + 1]),
                       waits=((lw[ib], wld),), inc=v_c)
                swf_free[ib] = v_c.v
            qb_mm = t_c.v
            p.emit("vector", lambda e, h=h: e.tensor_add(
                out=sqbias, in0=pkv[0:64, 0:1], in1=sqb_s[:, h:h + 1]),
                   waits=((t_c, qb_mm),), inc=v_c)
            qprep = v_c.v

            def head_vec(side, s0, w, out_slice, lnw_pair, bias_ap):
                kv_mm = t_c.v
                p.emit("scalar", lambda e, w=w, bias_ap=bias_ap: e.activation(
                    out=skr[:, :w], in_=pkv[0:64, :w], func=AF.Identity, bias=bias_ap),
                       waits=((t_c, kv_mm),), inc=s_c)
                kr_evac = s_c.v
                tab_v_guard = v_c.v
                p.emit("sync", lambda e, s0=s0, w=w: e.dma_start(
                    out=stabA[0:32, :w], in_=cos_h[:, s0:s0 + w]),
                       waits=((v_c, tab_v_guard),), inc=lt, inc_n=16)
                p.emit("sync", lambda e, s0=s0, w=w: e.dma_start(
                    out=stabA[32:64, :w], in_=sin_h[:, s0:s0 + w]), inc=lt, inc_n=16)
                p.emit("sync", lambda e, s0=s0, w=w: e.dma_start(
                    out=stabB[0:32, :w], in_=sin_h[:, s0:s0 + w]), inc=lt, inc_n=16)
                p.emit("sync", lambda e, s0=s0, w=w: e.dma_start(
                    out=stabB[32:64, :w], in_=cos_h[:, s0:s0 + w]), inc=lt, inc_n=16)
                p.emit("sync", lambda e, w=w: e.dma_start(
                    out=sswap[0:32, :w], in_=skr[32:64, :w]),
                       waits=((s_c, kr_evac),), inc=lt, inc_n=16)
                p.emit("sync", lambda e, w=w: e.dma_start(
                    out=sswap[32:64, :w], in_=skr[0:32, :w]), inc=lt, inc_n=16)
                tab_ld = lt.v
                p.emit("tensor", lambda e, s0=s0, w=w: e.matmul(
                    psc[0:64, 0, 0:512][:, :w], sones_rb[:, 0:64],
                    srad[0:1, s0:s0 + w], start=True, stop=True), inc=t_c)
                rad_bc = t_c.v
                p.emit("vector", lambda e, w=w: e.tensor_mul(
                    out=ssq[0:32, :w], in0=skr[0:32, :w], in1=stabA[0:32, :w]),
                       waits=((s_c, kr_evac), (lt, tab_ld)))
                p.emit("vector", lambda e, w=w: e.tensor_mul(
                    out=ssq2[0:32, :w], in0=sswap[0:32, :w], in1=stabB[0:32, :w]))
                p.emit("vector", lambda e, w=w: e.tensor_sub(
                    out=skrot[0:32, :w], in0=ssq[0:32, :w], in1=ssq2[0:32, :w]))
                p.emit("vector", lambda e, w=w: e.tensor_mul(
                    out=ssq[32:64, :w], in0=sswap[32:64, :w], in1=stabA[32:64, :w]))
                p.emit("vector", lambda e, w=w: e.tensor_mul(
                    out=ssq2[32:64, :w], in0=skr[32:64, :w], in1=stabB[32:64, :w]))
                p.emit("vector", lambda e, w=w: e.tensor_add(
                    out=skrot[32:64, :w], in0=ssq[32:64, :w], in1=ssq2[32:64, :w]))
                p.emit("vector", lambda e, w=w: e.tensor_mul(
                    out=skrot[:, :w], in0=skrot[:, :w], in1=psc[0:64, 0, 0:512][:, :w]),
                       waits=((t_c, rad_bc),), inc=v_c)
                rot_done = v_c.v
                p.emit("tensor", lambda e, w=w: e.matmul(
                    prow_m[:, :w], sones[0:64], skrot[:, :w], start=True, stop=True),
                       waits=((v_c, rot_done),), inc=t_c)
                hm_done = t_c.v
                p.emit("scalar", lambda e, w=w: e.activation(
                    out=ssq2[0:64, :w], in_=skrot[:, :w], func=AF.Square),
                       waits=((v_c, rot_done),), inc=s_c)
                hsq_done = s_c.v
                p.emit("tensor", lambda e, w=w: e.matmul(
                    prow_v[:, :w], sones[0:64], ssq2[0:64, :w], start=True, stop=True),
                       waits=((s_c, hsq_done),), inc=t_c)
                hv_done = t_c.v
                p.emit("vector", lambda e, w=w: e.tensor_scalar_mul(
                    out=srow1[:, :w], in0=prow_m[:, :w], scalar1=1.0 / HD),
                       waits=((t_c, hm_done),))
                p.emit("vector", lambda e, w=w: e.tensor_scalar_mul(
                    out=srow2[:, :w], in0=prow_v[:, :w], scalar1=1.0 / HD),
                       waits=((t_c, hv_done),))
                p.emit("vector", lambda e, w=w: e.tensor_mul(
                    out=srow3[:, :w], in0=srow1[:, :w], in1=srow1[:, :w]))
                p.emit("vector", lambda e, w=w: e.tensor_sub(
                    out=srow2[:, :w], in0=srow2[:, :w], in1=srow3[:, :w]), inc=v_c)
                hvar_done = v_c.v
                p.emit("scalar", lambda e, w=w: e.activation(
                    out=srow2[:, :w], in_=srow2[:, :w], func=AF.Ln, bias=seps),
                       waits=((v_c, hvar_done),))
                p.emit("scalar", lambda e, w=w: e.activation(
                    out=srow2[:, :w], in_=srow2[:, :w], func=AF.Exp, scale=-0.5),
                       inc=s_c)
                hr_done = s_c.v
                p.emit("vector", lambda e, w=w: e.tensor_mul(
                    out=srow1[:, :w], in0=srow1[:, :w], in1=srow2[:, :w]),
                       waits=((s_c, hr_done),))
                p.emit("vector", lambda e, w=w: e.tensor_scalar_mul(
                    out=srow1[:, :w], in0=srow1[:, :w], scalar1=-1.0), inc=v_c)
                hrows_done = v_c.v
                p.emit("tensor", lambda e, w=w: e.matmul(
                    psc[0:64, 0, 512:1024][:, :w], sones_r[:, 0:64], srow2[:, :w],
                    start=True, stop=True), waits=((v_c, hrows_done),))
                p.emit("tensor", lambda e, w=w: e.matmul(
                    psc[0:64, 1, 0:512][:, :w], sones_r[:, 0:64], srow1[:, :w],
                    start=True, stop=True), inc=t_c)
                hbc_done = t_c.v
                p.emit("vector", lambda e, w=w: e.tensor_mul(
                    out=ssq[0:64, :w], in0=skrot[:, :w],
                    in1=psc[0:64, 0, 512:1024][:, :w]),
                       waits=((t_c, hbc_done),))
                p.emit("vector", lambda e, w=w: e.tensor_add(
                    out=ssq[0:64, :w], in0=ssq[0:64, :w],
                    in1=psc[0:64, 1, 0:512][:, :w]))
                p.emit("vector", lambda e, out_slice=out_slice, w=w,
                       lnw_pair=lnw_pair: e.tensor_scalar(
                    out=out_slice, in0=ssq[0:64, :w],
                    scalar1=lnw_pair[:, 0:1], scalar2=lnw_pair[:, 1:2],
                    op0=ALU.mult, op1=ALU.add), inc=v_c)
                return v_c.v

            for (s0, w) in strips(SEQ, 512):
                strip_barrier(p, t_c, v_c, s_c)
                for i in range(NCH):
                    p.emit("tensor", lambda e, i=i, s0=s0, w=w: e.matmul(
                        pkv[:, :w], swkv[:, i, :], st_[:, i, s0:s0 + w],
                        start=(i == 0), stop=(i == NCH - 1)),
                           waits=((v_c, max(kvprep, qprep, a_done_v)),)
                           if i == 0 else (),
                           inc=t_c if i == NCH - 1 else None)
                kv_mm = t_c.v
                p.emit("scalar", lambda e, w=w: e.activation(
                    out=svfm[:, :w], in_=pkv[64:128, :w], func=AF.Identity,
                    bias=sbias[64:128]), waits=((t_c, kv_mm),), inc=s_c)
                vfm_evac = s_c.v
                kn_done = head_vec("k", s0, w, skne[0:64, s0:s0 + w], slnwb,
                                   sbias[0:64])
                p.emit("scalar", lambda e, s0=s0, w=w: e.activation(
                    out=ssq2[0:64, :w], in_=skne[0:64, s0:s0 + w], func=AF.Square),
                       waits=((v_c, kn_done),), inc=s_c)
                knsq_done = s_c.v
                p.emit("tensor", lambda e, w=w: e.matmul(
                    prow_v[:, :w], sones[0:64], ssq2[0:64, :w], start=True, stop=True),
                       waits=((s_c, knsq_done),), inc=t_c)
                knn_mm = t_c.v
                p.emit("vector", lambda e, w=w: e.reduce_max(
                    out=snrm[:, 0:1], in_=prow_v[:, :w], axis=AX.X),
                       waits=((t_c, knn_mm),))
                p.emit("vector", lambda e: e.tensor_max(
                    out=skmax, in0=skmax, in1=snrm[:, 0:1]), inc=v_c)
                for j in range(w // 128):
                    kchunk = s0 // 128 + j
                    p.emit("tensor", lambda e, j=j: e.transpose(
                        ptr[:, :], svfm[:, j * 128:(j + 1) * 128],
                        sident),
                           waits=((s_c, vfm_evac),) if j == 0 else (), inc=t_c)
                    tr_done = t_c.v
                    p.emit("vector", lambda e, kchunk=kchunk: e.tensor_copy(
                        out=svtm[:, kchunk, 0:64], in_=ptr[:, :]),
                           waits=((t_c, tr_done),), inc=v_c)
                    p.wait("tensor", v_c, v_c.v)
                p.wait("scalar", t_c, t_c.v)
            kn_done_s = s_c.v
            p.emit("scalar", lambda e: e.activation(out=skmax, in_=skmax, func=AF.Ln),
                   waits=((v_c, v_c.v),))
            p.emit("scalar", lambda e: e.activation(out=skmax, in_=skmax, func=AF.Exp,
                                                    scale=0.5), inc=s_c)
            kmax_done = s_c.v

            for qs in range(QSTRIPS):
                q0 = qs * 512
                w = 512
                strip_barrier(p, t_c, v_c, s_c)
                for i in range(NCH):
                    p.emit("tensor", lambda e, i=i, q0=q0, w=w: e.matmul(
                        pkv[0:64, :w], swq[:, i, :], st_[:, i, q0:q0 + w],
                        start=(i == 0), stop=(i == NCH - 1)),
                           inc=t_c if i == NCH - 1 else None)
                q_done = head_vec("q", q0, w, sqne[0:64, 0:w], slnq, sqbias)
                p.emit("scalar", lambda e, w=w: e.activation(
                    out=ssq2[0:64, :w], in_=sqne[0:64, 0:w], func=AF.Square),
                       waits=((v_c, q_done),), inc=s_c)
                qsq_done = s_c.v
                p.emit("tensor", lambda e, w=w: e.matmul(
                    prow_v[:, :w], sones[0:64], ssq2[0:64, :w], start=True, stop=True),
                       waits=((s_c, qsq_done),), inc=t_c)
                qn_mm = t_c.v
                p.emit("scalar", lambda e, w=w: e.activation(
                    out=snrm[:, :w], in_=prow_v[:, :w], func=AF.Ln),
                       waits=((t_c, qn_mm),))
                p.emit("scalar", lambda e, w=w: e.activation(
                    out=snrm[:, :w], in_=snrm[:, :w], func=AF.Exp, scale=0.5),
                       inc=s_c)
                qrow_done = s_c.v
                p.wait("tensor", s_c, qrow_done)
                p.emit("vector", lambda e, w=w: e.tensor_scalar(
                    out=sqne[64:65, 0:w], in0=snrm[:, :w], scalar1=skmax,
                    scalar2=-1.0, op0=ALU.mult, op1=ALU.mult),
                       waits=((s_c, max(qrow_done, kmax_done)),))
                p.emit("vector", lambda e, w=w: e.tensor_scalar_add(
                    out=sqne[64:65, 0:w], in0=sqne[64:65, 0:w],
                    scalar1=-0.05), inc=v_c)
                p.wait("scalar", v_c, v_c.v)
                bias_done = v_c.v
                emit_attn_b5(p, h=h, skne=skne, sqne=sqne, svtm=svtm, psc=psc,
                             sp=sp, pkv=pkv, snrm=snrm, sto=sto, south=south,
                             sones_r=sones_r, osc_h=osc_h,
                             t_c=t_c, v_c=v_c, s_c=s_c, g_c=g_c,
                             first_waits=((v_c, bias_done), (s_c, kn_done_s)),
                             qs_range=[qs], qne_col=lambda qs: 0)
        attn_out_dma = g_c.v

        for (s0, w) in strips(T_OWN, AW):
            v_snap = v_c.v
            t_snap = t_c.v
            for i in range(NCH):
                p.emit("sync", lambda e, i=i, s0=s0, w=w: e.dma_start(
                    out=sxf[:, i, :w], in_=xf_h[i * 128:(i + 1) * 128, s0:s0 + w]),
                       waits=((v_c, v_snap),) if i == 0 else (), inc=lx, inc_n=16)
            p.emit("sync", lambda e, s0=s0, w=w: e.dma_start(
                out=sres[:, :, :w],
                in_=osc_h.rearrange("(c p) t -> p c t", p=128)[:, :, s0:s0 + w]),
                   waits=((g_c, attn_out_dma),), inc=lx, inc_n=16)
            c_ld = lx.v
            ow_free = [v_c.v, v_c.v]
            ow_free_t = [t_c.v, t_c.v]
            for o in range(NCH):
                for i in range(NCH):
                    ib = i % 2
                    p.emit("sync", lambda e, i=i, o=o, ib=ib: e.dma_start(
                        out=swf[:, ib, :], in_=ow_h[i * 128:(i + 1) * 128,
                                                    o * 128:(o + 1) * 128]),
                           waits=((v_c, ow_free[ib]), (t_c, ow_free_t[ib])),
                           inc=lw[ib], inc_n=16)
                    owld = lw[ib].v
                    p.emit("vector", lambda e, ib=ib: e.tensor_copy(
                        out=sow[:, 0, :], in_=swf[:, ib, :]),
                           waits=((lw[ib], owld), (t_c, ow_free_t[1 - ib])), inc=v_c)
                    cast_done = v_c.v
                    ow_free[ib] = cast_done
                    p.emit("tensor", lambda e, i=i, w=w: e.matmul(
                        psc[:, 0, 0:512][:, :w], sow[:, 0, :], sres[:, i, :w],
                        start=(i == 0), stop=(i == NCH - 1)),
                           waits=((v_c, cast_done),) + (((lx, c_ld), (s_c, s_c.v))
                                                        if i == 0 else ()),
                           inc=t_c)
                    ow_free_t[ib] = t_c.v
                o_mm = t_c.v
                p.emit("scalar", lambda e, o=o, w=w: e.activation(
                    out=ssq[:, :w], in_=psc[:, 0, 0:512][:, :w], func=AF.Identity,
                    bias=sob[:, o:o + 1]), waits=((t_c, o_mm),), inc=s_c)
                ob_done = s_c.v
                p.emit("vector", lambda e, o=o, w=w: e.tensor_add(
                    out=sy[:, :w], in0=ssq[:, :w], in1=sxf[:, o, :w]),
                       waits=((s_c, ob_done), (lx, c_ld)), inc=v_c)
                y_done = v_c.v
                p.wait("tensor", v_c, y_done)
                p.wait("scalar", v_c, y_done)
                p.emit("gpsimd", lambda e, o=o, s0=s0, w=w: e.dma_start(
                    out=y_h[o * 128:(o + 1) * 128, s0:s0 + w], in_=sy[:, :w]),
                       waits=((v_c, y_done),), inc=st, inc_n=16)
                p.wait("vector", st, st.v)
        p.wait("gpsimd", st, st.v)
        p.finalize(block)
    return nc



def _mel_tables():
    d2 = 32
    mel_max = 2595.0 * np.log10(1.0 + 4000.0 / 200.0)
    scale = np.power(10.0, np.linspace(0.0, mel_max, d2, dtype=np.float64) / 2595.0) - 1.0
    fb = (200.0 * scale / 1000.0).astype(np.float64)
    freqs = np.arange(SEQ, dtype=np.float64)[:, None] * fb[None, :]
    cos = np.cos(np.float32(freqs).astype(np.float64))
    sin = np.sin(np.float32(freqs).astype(np.float64))
    f32 = np.float32(freqs)
    cos = np.cos(f32.astype(np.float64)).astype(np.float32)
    sin = np.sin(f32.astype(np.float64)).astype(np.float32)
    return np.ascontiguousarray(cos.T), np.ascontiguousarray(sin.T)


_PERM = np.concatenate([np.arange(0, HD, 2), np.arange(1, HD, 2)])


def l0_attn_weight_maps(ins):
    q_w = np.asarray(ins["L0_attn_q_w"], np.float32)
    kv_w = np.asarray(ins["L0_attn_kv_w"], np.float32)
    q_b = np.asarray(ins["L0_attn_q_b"], np.float32)
    kv_b = np.asarray(ins["L0_attn_kv_b"], np.float32)
    wq = np.stack([np.ascontiguousarray(q_w[:, h * HD:(h + 1) * HD][:, _PERM])
                   for h in range(NHEAD)])
    wkv = np.stack([np.ascontiguousarray(np.concatenate(
        [kv_w[:, h * HD:(h + 1) * HD][:, _PERM],
         kv_w[:, D + h * HD:D + (h + 1) * HD]], axis=1)) for h in range(NHEAD)])
    qb = np.stack([q_b[h * HD:(h + 1) * HD][_PERM] for h in range(NHEAD)], axis=1)
    kvb = np.stack([np.concatenate([kv_b[h * HD:(h + 1) * HD][_PERM],
                                    kv_b[D + h * HD:D + (h + 1) * HD]])
                    for h in range(NHEAD)], axis=1)
    lnwb = np.stack([np.asarray(ins["L0_attn_ln_w"], np.float32)[_PERM],
                     np.asarray(ins["L0_attn_ln_b"], np.float32)[_PERM]], axis=1)
    chunked = lambda v: np.ascontiguousarray(
        np.asarray(v, np.float32).reshape(NCH, 128).T)
    return dict(
        wq=wq, wkv=wkv, qb=np.ascontiguousarray(qb),
        kvb=np.ascontiguousarray(kvb), lnwb=np.ascontiguousarray(lnwb),
        qnw=chunked(ins["L0_attn_qn_w"]), qnb=chunked(ins["L0_attn_qn_b"]),
        kvnw=chunked(ins["L0_attn_kvn_w"]), kvnb=chunked(ins["L0_attn_kvn_b"]),
        rh=chunked(ins["L0_attn_rh_w"][:, 0]),
        rhb=np.asarray(ins["L0_attn_rh_b"], np.float32).reshape(1, 1),
        ow=np.ascontiguousarray(np.asarray(ins["L0_attn_out_w"], np.float32)),
        ob=chunked(ins["L0_attn_out_b"]),
        ident=np.eye(128, dtype=BF))


def l0_attn_in_maps(x):
    cos_t, sin_t = _mel_tables()
    maps = []
    for c in range(NCORES):
        b, s = divmod(c, 4)
        r = s * T_OWN
        xf = np.ascontiguousarray(np.roll(x[b], -r, axis=0).T)
        maps.append({"xf": xf,
                     "costab": np.ascontiguousarray(np.roll(cos_t, -r, axis=1)),
                     "sintab": np.ascontiguousarray(np.roll(sin_t, -r, axis=1))})
    return maps



T_IN = 2560
Q0 = 512
NTILE = 16
KV_LO = 128
KV_W = T_IN - KV_LO


def build_l0f_attn(debug=False):
    nc = bass.Bass("TRN2", target_bir_lowering=False, debug=False)
    x_h = nc.dram_tensor("x", [D, T_IN], F32, kind="ExternalInput").ap()
    hm_h = nc.dram_tensor("hm", [1, T_IN], BF16, kind="ExternalInput").ap()
    wq_h = nc.dram_tensor("wqkvh", [NHEAD, D, 192], F32, kind="ExternalInput").ap()
    qsks_h = nc.dram_tensor("qsks", [HD, 2], F32, kind="ExternalInput").ap()
    wo_h = nc.dram_tensor("wo", [D, D], F32, kind="ExternalInput").ap()
    mtri_h = nc.dram_tensor("mtri", [128, 384], BF16, kind="ExternalInput").ap()
    mtri2_h = nc.dram_tensor("mtri2", [128, 384], BF16, kind="ExternalInput").ap()
    tq_a = nc.dram_tensor("tqA", [HD, T_IN], F32, kind="ExternalInput").ap()
    tq_b = nc.dram_tensor("tqB", [HD, T_IN], F32, kind="ExternalInput").ap()
    tk_a = nc.dram_tensor("tkA", [HD, T_IN], F32, kind="ExternalInput").ap()
    tk_b = nc.dram_tensor("tkB", [HD, T_IN], F32, kind="ExternalInput").ap()
    ident_h = nc.dram_tensor("ident", [128, 128], BF16, kind="ExternalInput").ap()
    y_h = nc.dram_tensor("y", [D, T_OWN], F32, kind="ExternalOutput").ap()
    if debug:
        dkne = nc.dram_tensor("dkne", [66, KV_W], BF16, kind="ExternalOutput").ap()
        dq2 = nc.dram_tensor("dq2", [66, T_OWN], BF16, kind="ExternalOutput").ap()
        dvtm = nc.dram_tensor("dvtm", [128, 19 * 65], BF16, kind="ExternalOutput").ap()
        dassem = nc.dram_tensor("dassem", [D, T_OWN], BF16, kind="ExternalOutput").ap()

    es = ExitStack()
    sb = lambda n, s, d=F32: es.enter_context(nc.sbuf_tensor(n, s, d)).ap()
    psb = lambda n, s, d=F32: es.enter_context(nc.psum_tensor(n, s, d)).ap()
    with es:
        xb = sb("xb", [128, NCH, T_IN], BF16)
        assem = sb("assem", [128, NCH, T_OWN], BF16)
        skne2 = sb("skne2", [66, KV_W], BF16)
        svtm2 = sb("svtm2", [128, 19, 65], BF16)
        sq2 = sb("sq2", [66, T_OWN], BF16)
        swh = sb("swh", [128, 2, NCH, 192], BF16)
        swf = sb("swf", [128, 2, 192])
        swo = sb("swo", [128, NCH, D], BF16)
        sqsks = sb("sqsks", [HD, 2])
        sqs8 = sb("sqs8", [HD, 1])
        skr = sb("skr", [64, 512])
        skrot = sb("skrot", [64, 512])
        sswap = sb("sswap", [64, 512])
        svfm = sb("svfm", [64, 512], BF16)
        stabA = sb("stabA", [64, 512])
        stabB = sb("stabB", [64, 512])
        ssq = sb("ssq", [128, 512])
        ssq2 = sb("ssq2", [128, 512])
        srow1 = sb("srow1", [1, 512])
        snrm = sb("snrm", [1, 512])
        smaxneg = sb("smaxneg", [128, NTILE], BF16)
        smaxT = sb("smaxT", [NTILE, 128], BF16)
        sp = sb("sp", [128, 2, 384], BF16)
        sto = sb("sto", [128, 128], BF16)
        smtri = sb("smtri", [128, 384], BF16)
        smtri2 = sb("smtri2", [128, 384], BF16)
        sxf = sb("sxf", [128, NCH, 512])
        sy = sb("sy", [128, 512])
        sident = sb("sident", [128, 128], BF16)
        sones_r = sb("sones_r", [1, 128])
        sones = sb("sones", [128, 1])
        seps = sb("seps", [1, 1])
        pkv = psb("pkv", [128, 512])
        pbc = psb("pbc", [64, 512])
        psc1 = psb("psc1", [128, 384])
        psc2 = psb("psc2", [128, 2, 384])
        ppv = psb("ppv", [65, 128])
        ptr = psb("ptr", [NTILE, 128], BF16)
        pdbc = psb("pdbc", [128, 128])

        p = Prog(nc, es)
        li = p.sem("li")
        lx = p.sem("lx")
        lt = p.sem("lt")
        lw = [p.sem("lw0"), p.sem("lw1")]
        t_c = p.sem("t_c")
        v_c = p.sem("v_c")
        s_c = p.sem("s_c")
        g_c = p.sem("g_c")
        st = p.sem("st")

        block = es.enter_context(nc.Block())

        p.emit("sync", lambda e: e.dma_start(out=sident, in_=ident_h), inc=li, inc_n=16)
        p.emit("sync", lambda e: e.dma_start(out=smtri, in_=mtri_h), inc=li, inc_n=16)
        p.emit("sync", lambda e: e.dma_start(out=smtri2, in_=mtri2_h), inc=li, inc_n=16)
        p.emit("sync", lambda e: e.dma_start(out=sqsks, in_=qsks_h), inc=li, inc_n=16)
        p.emit("sync", lambda e: e.dma_start(
            out=skne2[65:66, :], in_=hm_h[:, KV_LO:]),
               waits=((v_c, 1),), inc=li, inc_n=16)
        init_ld = li.v
        p.emit("vector", lambda e: e.memset(sones_r, 1.0))
        p.emit("vector", lambda e: e.memset(sones, 1.0))
        p.emit("vector", lambda e: e.memset(seps, EPS))
        p.emit("vector", lambda e: e.memset(skne2[64:66, :], 1.0), inc=v_c)
        p.emit("vector", lambda e: e.memset(sq2[64:66, :], 1.0))
        p.emit("vector", lambda e: e.memset(svtm2[:, :, 64], 1.0))
        p.emit("vector", lambda e: e.tensor_scalar_mul(
            out=sqs8, in0=sqsks[:, 0:1], scalar1=8.0),
               waits=((li, init_ld),), inc=v_c)
        init_done = v_c.v
        wo_free = [v_c.v, v_c.v]
        for i in range(NCH):
            for oc in range(NCH):
                ib = (i * NCH + oc) % 2
                p.emit("sync", lambda e, i=i, oc=oc, ib=ib: e.dma_start(
                    out=swf[:, ib, 0:128],
                    in_=wo_h[i * 128:(i + 1) * 128, oc * 128:(oc + 1) * 128]),
                       waits=((v_c, wo_free[ib]),), inc=lw[ib], inc_n=16)
                wld = lw[ib].v
                p.emit("vector", lambda e, i=i, oc=oc, ib=ib: e.tensor_copy(
                    out=swo[:, i, oc * 128:(oc + 1) * 128], in_=swf[:, ib, 0:128]),
                       waits=((lw[ib], wld),), inc=v_c)
                wo_free[ib] = v_c.v
        wo_done = v_c.v

        for (s0, w) in strips(T_IN, 512):
            v_snap = v_c.v
            for i in range(NCH):
                p.emit("sync", lambda e, i=i, s0=s0, w=w: e.dma_start(
                    out=sxf[:, i, :w], in_=x_h[i * 128:(i + 1) * 128, s0:s0 + w]),
                       waits=((v_c, v_snap),) if i == 0 else (), inc=lx, inc_n=16)
            x_ld = lx.v
            for i in range(NCH):
                p.emit("vector", lambda e, i=i, s0=s0, w=w: e.tensor_copy(
                    out=xb[:, i, s0:s0 + w], in_=sxf[:, i, :w]),
                       waits=((lx, x_ld),) if i == 0 else (), inc=v_c)
        xb_done = v_c.v

        def shifted_rhs(i, c0, w):
            if i < 4:
                return xb[:, i, c0:c0 + w]
            return xb[:, i, c0 - 1:c0 - 1 + w]

        def l2rot_pipeline(c0, w, tabA_h, tabB_h, scal_ap, out_slice, src_rows):
            mm_done = t_c.v
            p.emit("scalar", lambda e, w=w: e.activation(
                out=skr[:, :w], in_=pkv[0:64, :w], func=AF.Copy),
                   waits=((t_c, mm_done),), inc=s_c)
            kr_evac = s_c.v
            p.emit("scalar", lambda e, w=w: e.activation(
                out=ssq2[0:64, :w], in_=pkv[0:64, :w], func=AF.Square), inc=s_c)
            sq_done = s_c.v
            p.emit("tensor", lambda e, w=w: e.matmul(
                pbc[0:1, :w], sones[0:64], ssq2[0:64, :w],
                start=True, stop=True), waits=((s_c, sq_done),), inc=t_c)
            nrm_mm = t_c.v
            p.emit("scalar", lambda e, w=w: e.activation(
                out=srow1[:, :w], in_=pbc[0:1, :w], func=AF.Ln),
                   waits=((t_c, nrm_mm),))
            p.emit("scalar", lambda e, w=w: e.activation(
                out=srow1[:, :w], in_=srow1[:, :w], func=AF.Exp, scale=0.5), inc=s_c)
            nrm_done = s_c.v
            p.emit("vector", lambda e, w=w: e.tensor_scalar_max(
                out=srow1[:, :w], in0=srow1[:, :w], scalar1=1e-12),
                   waits=((s_c, nrm_done),))
            p.emit("vector", lambda e, w=w: e.reciprocal(
                out=snrm[:, :w], in_=srow1[:, :w]), inc=v_c)
            rinv_done = v_c.v
            p.emit("tensor", lambda e, w=w: e.matmul(
                pbc[:, :w], sones_r[:, 0:64], snrm[:, :w], start=True, stop=True),
                   waits=((v_c, rinv_done),), inc=t_c)
            rbc_done = t_c.v
            p.emit("vector", lambda e, w=w: e.tensor_mul(
                out=ssq[0:64, :w], in0=skr[:, :w], in1=pbc[:, :w]),
                   waits=((t_c, rbc_done), (s_c, kr_evac)))
            p.emit("vector", lambda e, w=w, scal_ap=scal_ap: e.tensor_scalar_mul(
                out=skr[:, :w], in0=ssq[0:64, :w], scalar1=scal_ap), inc=v_c)
            scaled_done = v_c.v
            tab_guard = v_c.v
            p.emit("sync", lambda e, c0=c0, w=w, tabA_h=tabA_h: e.dma_start(
                out=stabA[:, :w], in_=tabA_h[:, c0:c0 + w]),
                   waits=((v_c, tab_guard),), inc=lt, inc_n=16)
            p.emit("sync", lambda e, c0=c0, w=w, tabB_h=tabB_h: e.dma_start(
                out=stabB[:, :w], in_=tabB_h[:, c0:c0 + w]), inc=lt, inc_n=16)
            p.emit("sync", lambda e, w=w: e.dma_start(
                out=sswap[0:32, :w], in_=skr[32:64, :w]),
                   waits=((v_c, scaled_done),), inc=lt, inc_n=16)
            p.emit("sync", lambda e, w=w: e.dma_start(
                out=sswap[32:64, :w], in_=skr[0:32, :w]), inc=lt, inc_n=16)
            tab_ld = lt.v
            p.emit("vector", lambda e, w=w: e.tensor_mul(
                out=ssq[0:32, :w], in0=skr[0:32, :w], in1=stabA[0:32, :w]),
                   waits=((lt, tab_ld),))
            p.emit("vector", lambda e, w=w: e.tensor_mul(
                out=ssq2[0:32, :w], in0=sswap[0:32, :w], in1=stabB[0:32, :w]))
            p.emit("vector", lambda e, w=w: e.tensor_sub(
                out=skrot[0:32, :w], in0=ssq[0:32, :w], in1=ssq2[0:32, :w]))
            p.emit("vector", lambda e, w=w: e.tensor_mul(
                out=ssq[32:64, :w], in0=sswap[32:64, :w], in1=stabA[32:64, :w]))
            p.emit("vector", lambda e, w=w: e.tensor_mul(
                out=ssq2[32:64, :w], in0=skr[32:64, :w], in1=stabB[32:64, :w]))
            p.emit("vector", lambda e, w=w: e.tensor_add(
                out=skrot[32:64, :w], in0=ssq[32:64, :w], in1=ssq2[32:64, :w]))
            p.emit("vector", lambda e, w=w, out_slice=out_slice: e.tensor_copy(
                out=out_slice, in_=skrot[:, :w]), inc=v_c)
            return v_c.v

        for h in range(NHEAD):
            hb = h % 2
            wprev = [v_c.v, v_c.v]
            for i in range(NCH):
                ib = i % 2
                p.emit("sync", lambda e, h=h, i=i, ib=ib: e.dma_start(
                    out=swf[:, ib, :], in_=wq_h[h, i * 128:(i + 1) * 128, :]),
                       waits=((v_c, wprev[ib]),), inc=lw[ib], inc_n=16)
                wld = lw[ib].v
                p.emit("vector", lambda e, hb=hb, i=i, ib=ib: e.tensor_copy(
                    out=swh[:, hb, i, :], in_=swf[:, ib, :]),
                       waits=((lw[ib], wld),), inc=v_c)
                wprev[ib] = v_c.v
            w_done = v_c.v

            for (s0, w) in strips(KV_W, 512, start=KV_LO):
                strip_barrier(p, t_c, v_c, s_c)
                for i in range(NCH):
                    p.emit("tensor", lambda e, hb=hb, i=i, s0=s0, w=w: e.matmul(
                        pkv[:, :w], swh[:, hb, i, 64:192], shifted_rhs(i, s0, w),
                        start=(i == 0), stop=(i == NCH - 1)),
                           waits=((v_c, max(w_done, xb_done)),) if i == 0 else (),
                           inc=t_c if i == NCH - 1 else None)
                kvmm = t_c.v
                p.emit("scalar", lambda e, w=w: e.activation(
                    out=svfm[:, :w], in_=pkv[64:128, :w], func=AF.Copy),
                       waits=((t_c, kvmm),), inc=s_c)
                v_evac = s_c.v
                kdone = l2rot_pipeline(s0, w, tk_a, tk_b, sqsks[:, 1:2],
                                       skne2[0:64, s0 - KV_LO:s0 - KV_LO + w], None)
                for j in range(w // 128):
                    kchunk = (s0 - KV_LO) // 128 + j
                    p.emit("tensor", lambda e, j=j: e.transpose(
                        pdbc.bitcast(BF16)[:, 0:64],
                        svfm[:, j * 128:(j + 1) * 128], sident[0:64, 0:64]),
                           waits=((s_c, v_evac),) if j == 0 else (), inc=t_c)
                    trd = t_c.v
                    p.emit("vector", lambda e, kchunk=kchunk: e.tensor_copy(
                        out=svtm2[:, kchunk, 0:64], in_=pdbc.bitcast(BF16)[:, 0:64]),
                           waits=((t_c, trd),), inc=v_c)
                    p.wait("tensor", v_c, v_c.v)
                p.wait("scalar", t_c, t_c.v)
            kn_done_s = s_c.v
            kn_done_v = v_c.v

            for (s0, w) in strips(T_OWN, 512, start=Q0):
                strip_barrier(p, t_c, v_c, s_c)
                for i in range(NCH):
                    p.emit("tensor", lambda e, hb=hb, i=i, s0=s0, w=w: e.matmul(
                        pkv[0:64, :w], swh[:, hb, i, 0:64], shifted_rhs(i, s0, w),
                        start=(i == 0), stop=(i == NCH - 1)),
                           inc=t_c if i == NCH - 1 else None)
                qdone = l2rot_pipeline(s0, w, tq_a, tq_b, sqs8,
                                       sq2[0:64, s0 - Q0:s0 - Q0 + w], None)

            if debug and h == 0:
                p.emit("gpsimd", lambda e: e.dma_start(out=dkne, in_=skne2),
                       waits=((v_c, v_c.v), (s_c, s_c.v)), inc=st, inc_n=16)
                p.emit("gpsimd", lambda e: e.dma_start(
                    out=dvtm, in_=svtm2.rearrange("p c n -> p (c n)")),
                       inc=st, inc_n=16)
            for t in range(NTILE):
                c0 = Q0 + t * 128
                klo = c0 - 256 - KV_LO
                p.emit("tensor", lambda e, t=t, klo=klo: e.matmul(
                    psc1[:, :], sq2[0:64, t * 128:(t + 1) * 128],
                    skne2[0:64, klo:klo + 384], start=True, stop=True),
                       waits=((v_c, v_c.v),) if t == 0 else (), inc=t_c)
                p1mm = t_c.v
                p.emit("vector", lambda e: e.tensor_add(
                    out=psc1[:, :], in0=psc1[:, :], in1=smtri),
                       waits=((t_c, p1mm), (li, init_ld)))
                p.emit("vector", lambda e, t=t: e.reduce_max(
                    out=smaxneg[:, t:t + 1], in_=psc1[:, :], axis=AX.X, negate=True),
                       inc=v_c)
                p.wait("tensor", v_c, v_c.v)
            p.emit("tensor", lambda e: e.transpose(ptr[:, :], smaxneg, sident),
                   waits=((v_c, v_c.v),), inc=t_c)
            trmax = t_c.v
            p.emit("vector", lambda e: e.tensor_copy(out=smaxT, in_=ptr[:, :]),
                   waits=((t_c, trmax),), inc=v_c)
            maxT_done = v_c.v
            for t in range(NTILE):
                p.emit("gpsimd", lambda e, t=t: e.dma_start(
                    out=sq2[64:65, t * 128:(t + 1) * 128], in_=smaxT[t:t + 1, :]),
                       waits=((v_c, maxT_done),) if t == 0 else (), inc=g_c, inc_n=16)
            brow_ld = g_c.v

            if debug and h == 0:
                p.emit("gpsimd", lambda e: e.dma_start(out=dq2, in_=sq2),
                       waits=((g_c, brow_ld), (v_c, v_c.v)), inc=st, inc_n=16)
            exp_done = [0] * NTILE
            for t in range(NTILE):
                gb = t % 2
                c0 = Q0 + t * 128
                klo = c0 - 256 - KV_LO
                for j in range(3):
                    p.emit("tensor", lambda e, t=t, gb=gb, klo=klo, j=j: e.matmul(
                        psc2[:, gb, j * 128:(j + 1) * 128],
                        skne2[:, klo + j * 128:klo + (j + 1) * 128],
                        sq2[:, t * 128:(t + 1) * 128], start=True, stop=True),
                           waits=tuple(
                               ([(g_c, brow_ld), (s_c, kn_done_s)] if (t == 0 and j == 0) else [])
                               + ([(s_c, exp_done[t - 2])] if (t >= 2 and j == 0) else [])
                               + ([(v_c, v_c.v)] if (t == 0 and j == 0) else [])),
                           inc=t_c if j == 2 else None)
                p2mm = t_c.v
                p.emit("vector", lambda e, gb=gb: e.tensor_add(
                    out=psc2[:, gb, :], in0=psc2[:, gb, :], in1=smtri2),
                       waits=((t_c, p2mm),), inc=v_c)
                mask_done = v_c.v
                p.emit("scalar", lambda e, gb=gb: e.activation(
                    out=sp[:, gb, :], in_=psc2[:, gb, :], func=AF.Exp),
                       waits=((v_c, mask_done),), inc=s_c)
                exp_done[t] = s_c.v
                for j in range(3):
                    kchunk = (klo // 128) + j
                    p.emit("tensor", lambda e, gb=gb, j=j, kchunk=kchunk: e.matmul(
                        ppv[:, :], svtm2[:, kchunk, :], sp[:, gb, j * 128:(j + 1) * 128],
                        start=(j == 0), stop=(j == 2)),
                           waits=((s_c, exp_done[t]),) if j == 0 else (),
                           inc=t_c if j == 2 else None)
                pv_done = t_c.v
                p.emit("vector", lambda e: e.reciprocal(
                    out=snrm[:, 0:128], in_=ppv[64:65, :]),
                       waits=((t_c, pv_done),), inc=v_c)
                den_done = v_c.v
                ob = (h % 2) * 64
                p.emit("scalar", lambda e, ob=ob: e.activation(
                    out=sto[ob:ob + 64, :], in_=ppv[0:64, :], func=AF.Copy),
                       waits=((t_c, pv_done),), inc=s_c)
                oevac = s_c.v
                p.emit("tensor", lambda e, ob=ob: e.matmul(
                    pdbc[ob:ob + 64, :], sones_r[:, 0:64], snrm[:, 0:128],
                    start=True, stop=True,
                    tile_position=(0, ob) if ob else None),
                       waits=((v_c, den_done),), inc=t_c)
                dbc = t_c.v
                p.emit("vector", lambda e, h=h, t=t, ob=ob: e.tensor_mul(
                    out=assem[ob:ob + 64, h // 2, t * 128:(t + 1) * 128],
                    in0=sto[ob:ob + 64, :], in1=pdbc[ob:ob + 64, :]),
                       waits=((t_c, dbc), (s_c, oevac)), inc=v_c)
                adone = v_c.v
                p.wait("tensor", v_c, adone)
                p.wait("scalar", v_c, adone)
        heads_done_v = v_c.v
        if debug:
            p.wait("gpsimd", v_c, heads_done_v)
            for ci in range(NCH):
                p.emit("gpsimd", lambda e, ci=ci: e.dma_start(
                    out=dassem[ci * 128:(ci + 1) * 128, :], in_=assem[:, ci, :]),
                       inc=st, inc_n=16)

        for (s0, w) in strips(T_OWN, 512):
            v_snap = v_c.v
            for i in range(NCH):
                p.emit("sync", lambda e, i=i, s0=s0, w=w: e.dma_start(
                    out=sxf[:, i, :w], in_=x_h[i * 128:(i + 1) * 128, Q0 + s0:Q0 + s0 + w]),
                       waits=((v_c, v_snap),) if i == 0 else (), inc=lx, inc_n=16)
            x_ld = lx.v
            for o in range(NCH):
                for i in range(NCH):
                    p.emit("tensor", lambda e, i=i, o=o, s0=s0, w=w: e.matmul(
                        pkv[:, :w], swo[:, i, o * 128:(o + 1) * 128],
                        assem[:, i, s0:s0 + w], start=(i == 0), stop=(i == NCH - 1)),
                           waits=((v_c, max(heads_done_v, wo_done)),)
                           if (s0 == 0 and o == 0 and i == 0) else (),
                           inc=t_c if i == NCH - 1 else None)
                omm = t_c.v
                p.emit("scalar", lambda e, w=w: e.activation(
                    out=ssq[:, :w], in_=pkv[:, :w], func=AF.Copy),
                       waits=((t_c, omm),), inc=s_c)
                oevac = s_c.v
                p.emit("vector", lambda e, o=o, w=w: e.tensor_add(
                    out=sy[:, :w], in0=ssq[:, :w], in1=sxf[:, o, :w]),
                       waits=((s_c, oevac), (lx, x_ld)), inc=v_c)
                ydone = v_c.v
                p.wait("tensor", v_c, ydone)
                p.wait("scalar", v_c, ydone)
                p.emit("gpsimd", lambda e, o=o, s0=s0, w=w: e.dma_start(
                    out=y_h[o * 128:(o + 1) * 128, s0:s0 + w], in_=sy[:, :w]),
                       waits=((v_c, ydone),), inc=st, inc_n=16)
                p.wait("vector", st, st.v)
        p.wait("gpsimd", st, st.v)
        p.finalize(block)
    return nc


def _xpos_tables():
    half = np.arange(0, HD, 2, dtype=np.float64)
    inv_freq = 1.0 / 10000.0 ** (half / HD)
    base = (half + 0.4 * HD) / (1.4 * HD)
    pos = np.arange(-512, SEQ, dtype=np.float64)
    f32 = np.float32
    freqs = f32(pos[None, :]) * f32(inv_freq[:, None])
    cos = np.cos(freqs.astype(np.float64)).astype(f32)
    sin = np.sin(freqs.astype(np.float64)).astype(f32)
    power = f32((pos - SEQ // 2) / 128.0)
    scale = f32(base[:, None]) ** power[None, :]
    inv = (1.0 / scale).astype(f32)
    cs, ss = cos * scale, sin * scale
    ci, si = cos * inv, sin * inv
    stk = lambda a, b: np.ascontiguousarray(np.concatenate([a, b], axis=0))
    return (stk(cs, ss), stk(ss, cs), stk(ci, si), stk(si, ci))


def _l0f_mtri():
    i = np.arange(128)[:, None]
    m = np.zeros((128, 384), np.float32)
    jj0 = np.arange(128)[None, :]
    m[:, 0:128] = np.where(jj0 < i, -1e30, 0.0)
    m[:, 256:384] = np.where(jj0 > i, -1e30, 0.0)
    return m.astype(BF)


def _l0f_mtri2():
    m = np.asarray(_l0f_mtri(), np.float32)
    m2 = np.zeros((128, 384), np.float32)
    for j in range(3):
        m2[:, j * 128:(j + 1) * 128] = m[:, j * 128:(j + 1) * 128].T
    return m2.astype(BF)


def l0f_attn_layer_maps(ins, layer):
    Wqkv = np.asarray(ins["L0_ffn_Wqkv"], np.float32)[layer]
    wq = np.stack([np.ascontiguousarray(np.concatenate(
        [Wqkv[:, h * HD:(h + 1) * HD],
         Wqkv[:, D + h * HD:D + (h + 1) * HD],
         Wqkv[:, 2 * D + h * HD:2 * D + (h + 1) * HD]], axis=1))
        for h in range(NHEAD)])
    qsks = np.stack([np.asarray(ins["L0_ffn_q_scale"], np.float32)[layer],
                     np.asarray(ins["L0_ffn_k_scale"], np.float32)[layer]], axis=1)
    return dict(wqkvh=wq, qsks=np.ascontiguousarray(qsks),
                wo=np.ascontiguousarray(np.asarray(ins["L0_ffn_Wo"], np.float32)[layer]),
                mtri=_l0f_mtri(), mtri2=_l0f_mtri2(), ident=np.eye(128, dtype=BF))


def l0f_attn_in_maps(xcur):
    tqA, tqB, tkA, tkB = _XPOS_TABS
    maps = []
    for c in range(NCORES):
        b, s = divmod(c, 4)
        s0 = s * T_OWN
        xw = np.zeros((D, T_IN), np.float32)
        lo = s0 - 512
        src_lo = max(lo, 0)
        xw[:, src_lo - lo:] = xcur[b, src_lo:s0 + T_OWN, :].T
        hm = np.zeros((1, T_IN), np.float32)
        if lo < 0:
            hm[0, :-lo] = -1e30
        tab = lambda t: np.ascontiguousarray(t[:, 512 + lo:512 + lo + T_IN])
        maps.append(dict(x=np.ascontiguousarray(xw), hm=hm.astype(BF),
                         tqA=tab(tqA), tqB=tab(tqB), tkA=tab(tkA), tkB=tab(tkB)))
    return maps


_XPOS_TABS = _xpos_tables()



NHC = 22
T_F = T_OWN + 1


def build_l0f_ffn(debug=False):
    nc = bass.Bass("TRN2", target_bir_lowering=False, debug=False)
    x_h = nc.dram_tensor("x", [D, T_F], F32, kind="ExternalInput").ap()
    w1_h = nc.dram_tensor("w1", [D, 5460], F32, kind="ExternalInput").ap()
    w2_h = nc.dram_tensor("w2", [NHC * 128, D], F32, kind="ExternalInput").ap()
    b1_h = nc.dram_tensor("b1", [128, 2 * NHC], F32, kind="ExternalInput").ap()
    b2_h = nc.dram_tensor("b2", [128, NCH], F32, kind="ExternalInput").ap()
    lng_h = nc.dram_tensor("lng", [128, NCH], F32, kind="ExternalInput").ap()
    lnb_h = nc.dram_tensor("lnb", [128, NCH], F32, kind="ExternalInput").ap()
    y_h = nc.dram_tensor("y", [D, T_OWN], F32, kind="ExternalOutput").ap()
    if debug:
        dxs = nc.dram_tensor("dxs", [D, 256], BF16, kind="ExternalOutput").ap()
        dag = nc.dram_tensor("dag", [128, 256], BF16, kind="ExternalOutput").ap()
        db1 = nc.dram_tensor("db1", [128, 2 * NHC], F32, kind="ExternalOutput").ap()
        dw1 = nc.dram_tensor("dw1", [128, 256], BF16, kind="ExternalOutput").ap()

    HCW = [128] * (NHC - 1) + [42]

    es = ExitStack()
    sb = lambda n, s, d=F32: es.enter_context(nc.sbuf_tensor(n, s, d)).ap()
    psb = lambda n, s, d=F32: es.enter_context(nc.psum_tensor(n, s, d)).ap()
    with es:
        sx = sb("sx", [128, NCH, T_F])
        sw1 = sb("sw1", [128, NCH, 5460], BF16)
        sw2 = sb("sw2", [128, NHC, D], BF16)
        swf = sb("swf", [128, 2, 256])
        sxs = sb("sxs", [128, NCH, 256], BF16)
        sag = sb("sag", [128, 256], BF16)
        sgel = sb("sgel", [128, 256])
        ssq = sb("ssq", [128, 256])
        ssq2 = sb("ssq2", [128, 256])
        sy = sb("sy", [128, 256])
        sb1 = sb("sb1", [128, 2 * NHC])
        sb2 = sb("sb2", [128, NCH])
        slng = sb("slng", [128, NCH])
        slnb = sb("slnb", [128, NCH])
        srow1 = sb("srow1", [1, 256])
        srow2 = sb("srow2", [1, 256])
        srow3 = sb("srow3", [1, 256])
        sones = sb("sones", [128, 1])
        sones_r = sb("sones_r", [1, 128])
        seps = sb("seps", [1, 1])
        pout = psb("pout", [128, NCH, 256])
        pha = psb("pha", [128, 256])
        phg = psb("phg", [128, 256])
        prow_m = psb("prow_m", [1, 256])
        prow_v = psb("prow_v", [1, 256])

        p = Prog(nc, es)
        li = p.sem("li")
        lx = p.sem("lx")
        lw = [p.sem("lw0"), p.sem("lw1")]
        t_c = p.sem("t_c")
        v_c = p.sem("v_c")
        s_c = p.sem("s_c")
        st = p.sem("st")

        block = es.enter_context(nc.Block())

        for name, dst, srch in (("b1", sb1, b1_h), ("b2", sb2, b2_h),
                                ("lng", slng, lng_h), ("lnb", slnb, lnb_h)):
            p.emit("sync", lambda e, dst=dst, srch=srch: e.dma_start(out=dst, in_=srch),
                   inc=li, inc_n=16)
        init_ld = li.v
        p.emit("vector", lambda e: e.memset(sones, 1.0))
        p.emit("vector", lambda e: e.memset(sones_r, 1.0))
        p.emit("vector", lambda e: e.memset(seps, EPS))
        p.emit("vector", lambda e: e.memset(sag, 0.0), inc=v_c)
        init_done = v_c.v

        for i in range(NCH):
            p.emit("sync", lambda e, i=i: e.dma_start(
                out=sx[:, i, :], in_=x_h[i * 128:(i + 1) * 128, :]), inc=lx, inc_n=16)
        x_ld = lx.v

        swf_free = [init_done, init_done]
        swf_free_t = [0, 0]
        for i in range(NCH):
            for cg in range(22):
                cw = min(256, 5460 - cg * 256)
                ib = (i * 22 + cg) % 2
                p.emit("sync", lambda e, i=i, cg=cg, cw=cw, ib=ib: e.dma_start(
                    out=swf[:, ib, :cw], in_=w1_h[i * 128:(i + 1) * 128,
                                                  cg * 256:cg * 256 + cw]),
                       waits=((v_c, swf_free[ib]), (t_c, swf_free_t[ib])),
                       inc=lw[ib], inc_n=16)
                wld = lw[ib].v
                p.emit("vector", lambda e, i=i, cg=cg, cw=cw, ib=ib: e.tensor_scalar_mul(
                    out=sw1[:, i, cg * 256:cg * 256 + cw], in0=swf[:, ib, :cw],
                    scalar1=slng[:, i:i + 1]),
                       waits=((lw[ib], wld), (li, init_ld)), inc=v_c)
                swf_free[ib] = v_c.v
        w1_done = v_c.v
        for z in range(2 * NHC):
            half, k = divmod(z, NHC)
            wk = min(128, 2730 - 128 * k)
            j0 = (2730 if half else 0) + 128 * k
            for i in range(NCH):
                ib = i % 2
                if wk < 128:
                    p.emit("vector", lambda e, ib=ib, wk=wk: e.memset(
                        swf[:, ib, wk:128], 0.0),
                           waits=((t_c, swf_free_t[ib]),), inc=v_c)
                    swf_free[ib] = v_c.v
                p.emit("sync", lambda e, i=i, ib=ib, j0=j0, wk=wk: e.dma_start(
                    out=swf[:, ib, :wk], in_=w1_h[i * 128:(i + 1) * 128, j0:j0 + wk]),
                       waits=((v_c, swf_free[ib]), (t_c, swf_free_t[ib])),
                       inc=lw[ib], inc_n=16)
                wld = lw[ib].v
                p.emit("tensor", lambda e, i=i, ib=ib, z=z, wk=wk: e.matmul(
                    pha[0:128, z:z + 1], swf[:, ib, :128], slnb[:, i:i + 1],
                    start=(i == 0), stop=(i == NCH - 1)),
                       waits=((lw[ib], wld), (li, init_ld)), inc=t_c)
                swf_free_t[ib] = t_c.v
        b1mm_done = t_c.v
        p.emit("vector", lambda e: e.tensor_add(
            out=sb1, in0=sb1, in1=pha[:, 0:2 * NHC]),
               waits=((t_c, b1mm_done), (li, init_ld)), inc=v_c)
        b1_done = v_c.v
        p.wait("tensor", v_c, b1_done)

        for k in range(NHC):
            for cg in range(4):
                ib = (k * 4 + cg) % 2
                p.emit("sync", lambda e, k=k, cg=cg, ib=ib: e.dma_start(
                    out=swf[:, ib, :], in_=w2_h[k * 128:(k + 1) * 128,
                                                cg * 256:(cg + 1) * 256]),
                       waits=((v_c, swf_free[ib]), (t_c, swf_free_t[ib])),
                       inc=lw[ib], inc_n=16)
                wld = lw[ib].v
                p.emit("vector", lambda e, k=k, cg=cg, ib=ib: e.tensor_copy(
                    out=sw2[:, k, cg * 256:(cg + 1) * 256], in_=swf[:, ib, :]),
                       waits=((lw[ib], wld),), inc=v_c)
                swf_free[ib] = v_c.v
        w2_done = v_c.v

        for si in range(8):
            c0 = 1 + si * 256
            w = 256
            strip_barrier(p, t_c, v_c, s_c)

            def xsh(i, a, b):
                if i < 4:
                    return sx[:, i, a:b]
                return sx[:, i, a - 1:b - 1]

            for i in range(NCH):
                p.emit("tensor", lambda e, i=i, c0=c0: e.matmul(
                    prow_m[:, :], sones, xsh(i, c0, c0 + 256),
                    start=(i == 0), stop=(i == NCH - 1)),
                       waits=((lx, x_ld), (v_c, init_done)) if i == 0 else (),
                       inc=t_c if i == NCH - 1 else None)
            m_done = t_c.v
            for i in range(NCH):
                p.emit("scalar", lambda e, i=i, c0=c0: e.activation(
                    out=ssq2, in_=xsh(i, c0, c0 + 256), func=AF.Square),
                       waits=((lx, x_ld),) if i == 0 else ((t_c, t_c.v),), inc=s_c)
                sqd = s_c.v
                p.emit("tensor", lambda e, i=i: e.matmul(
                    prow_v[:, :], sones, ssq2, start=(i == 0), stop=(i == NCH - 1)),
                       waits=((s_c, sqd),), inc=t_c)
            v_done = t_c.v
            p.emit("vector", lambda e: e.tensor_scalar_mul(
                out=srow1, in0=prow_m, scalar1=1.0 / D), waits=((t_c, v_done),))
            p.emit("vector", lambda e: e.tensor_scalar_mul(
                out=srow2, in0=prow_v, scalar1=1.0 / D), waits=((t_c, v_done),))
            p.emit("vector", lambda e: e.tensor_mul(out=srow3, in0=srow1, in1=srow1))
            p.emit("vector", lambda e: e.tensor_sub(out=srow2, in0=srow2, in1=srow3),
                   inc=v_c)
            var_done = v_c.v
            p.emit("scalar", lambda e: e.activation(out=srow2, in_=srow2, func=AF.Ln,
                                                    bias=seps),
                   waits=((v_c, var_done),))
            p.emit("scalar", lambda e: e.activation(out=srow2, in_=srow2, func=AF.Exp,
                                                    scale=-0.5), inc=s_c)
            rstd_done = s_c.v
            p.emit("vector", lambda e: e.tensor_mul(out=srow1, in0=srow1, in1=srow2),
                   waits=((s_c, rstd_done),))
            p.emit("vector", lambda e: e.tensor_scalar_mul(out=srow1, in0=srow1,
                                                           scalar1=-1.0), inc=v_c)
            rows_done = v_c.v
            p.emit("tensor", lambda e: e.matmul(pha[:, :], sones_r, srow2,
                                                start=True, stop=True),
                   waits=((v_c, rows_done),))
            p.emit("tensor", lambda e: e.matmul(phg[:, :], sones_r, srow1,
                                                start=True, stop=True), inc=t_c)
            bc_done = t_c.v
            for i in range(NCH):
                p.emit("vector", lambda e, i=i, c0=c0: e.tensor_mul(
                    out=ssq, in0=xsh(i, c0, c0 + 256), in1=pha[:, :]),
                       waits=((t_c, bc_done),) if i == 0 else ())
                p.emit("vector", lambda e, i=i: e.tensor_add(
                    out=sxs[:, i, :], in0=ssq, in1=phg[:, :]), inc=v_c)
            xs_done = v_c.v
            p.wait("tensor", v_c, xs_done)
            if debug and si == 0:
                for i in range(NCH):
                    p.emit("gpsimd", lambda e, i=i: e.dma_start(
                        out=dxs[i * 128:(i + 1) * 128, :], in_=sxs[:, i, :]),
                           waits=((v_c, xs_done),), inc=st, inc_n=16)
                p.emit("gpsimd", lambda e: e.dma_start(out=db1, in_=sb1),
                       inc=st, inc_n=16)
                p.emit("gpsimd", lambda e: e.dma_start(out=dw1, in_=sw1[:, 0, 0:256]),
                       inc=st, inc_n=16)

            prev_ag_done = v_c.v
            for k in range(NHC):
                wk = HCW[k]
                a0 = 128 * k
                g0 = 2730 + 128 * k
                for i in range(NCH):
                    p.emit("tensor", lambda e, i=i, a0=a0, wk=wk: e.matmul(
                        pha[0:wk, :], sw1[:, i, a0:a0 + wk], sxs[:, i, :],
                        start=(i == 0), stop=(i == NCH - 1)),
                           waits=((v_c, max(w1_done, xs_done)),)
                           if (si == 0 and k == 0 and i == 0) else (),
                           inc=t_c if i == NCH - 1 else None)
                for i in range(NCH):
                    p.emit("tensor", lambda e, i=i, g0=g0, wk=wk: e.matmul(
                        phg[0:wk, :], sw1[:, i, g0:g0 + wk], sxs[:, i, :],
                        start=(i == 0), stop=(i == NCH - 1)),
                           inc=t_c if i == NCH - 1 else None)
                hg_done = t_c.v
                p.emit("scalar", lambda e, k=k, wk=wk: e.activation(
                    out=sgel[0:wk, :], in_=phg[0:wk, :], func=AF.Gelu,
                    bias=sb1[0:wk, NHC + k:NHC + k + 1]),
                       waits=((t_c, hg_done), (v_c, prev_ag_done)), inc=s_c)
                gel_done = s_c.v
                p.emit("vector", lambda e, k=k, wk=wk: e.scalar_tensor_tensor(
                    out=sag[0:wk, :], in0=pha[0:wk, :],
                    scalar=sb1[0:wk, k:k + 1], in1=sgel[0:wk, :],
                    op0=ALU.add, op1=ALU.mult),
                       waits=((s_c, gel_done),), inc=v_c)
                ag_done = v_c.v
                prev_ag_done = ag_done
                p.wait("tensor", v_c, ag_done)
                if debug and si == 0 and k == 0:
                    p.emit("gpsimd", lambda e: e.dma_start(out=dag, in_=sag),
                           waits=((v_c, ag_done),), inc=st, inc_n=16)
                    p.wait("tensor", st, st.v)
                for oc in range(NCH):
                    p.emit("tensor", lambda e, k=k, oc=oc: e.matmul(
                        pout[:, oc, :], sw2[:, k, oc * 128:(oc + 1) * 128], sag,
                        start=(k == 0 and oc % 2 == 0), stop=(k == NHC - 1)),
                           waits=((v_c, max(ag_done, w2_done)),) if oc == 0 else (),
                           inc=t_c if oc == NCH - 1 else None)
            ffn_done = t_c.v
            for oc in range(NCH):
                p.emit("scalar", lambda e, oc=oc: e.activation(
                    out=ssq2, in_=pout[:, oc, :], func=AF.Identity,
                    bias=sb2[:, oc:oc + 1]),
                       waits=((t_c, ffn_done),) if oc == 0 else (), inc=s_c)
                ev_done = s_c.v
                p.emit("vector", lambda e, oc=oc, c0=c0: e.tensor_add(
                    out=sy, in0=ssq2, in1=sx[:, oc, c0:c0 + 256]),
                       waits=((s_c, ev_done),), inc=v_c)
                y_done = v_c.v
                p.wait("scalar", v_c, y_done)
                p.emit("gpsimd", lambda e, oc=oc, si=si: e.dma_start(
                    out=y_h[oc * 128:(oc + 1) * 128, si * 256:(si + 1) * 256],
                    in_=sy), waits=((v_c, y_done),), inc=st, inc_n=16)
                p.wait("vector", st, st.v)
        p.wait("gpsimd", st, st.v)
        p.finalize(block)
    return nc


def l0f_ffn_layer_maps(ins, layer):
    w1 = np.ascontiguousarray(np.asarray(ins["L0_ffn_W1"], np.float32)[layer])
    w2r = np.asarray(ins["L0_ffn_W2"], np.float32)[layer]
    w2 = np.zeros((NHC * 128, D), np.float32)
    w2[:2730] = w2r
    b1r = np.asarray(ins["L0_ffn_b1"], np.float32)[layer]
    b1 = np.zeros((128, 2 * NHC), np.float32)
    for k in range(NHC):
        wk = min(128, 2730 - 128 * k)
        b1[:wk, k] = b1r[128 * k:128 * k + wk]
        b1[:wk, NHC + k] = b1r[2730 + 128 * k:2730 + 128 * k + wk]
    chunked = lambda v: np.ascontiguousarray(
        np.asarray(v, np.float32).reshape(NCH, 128).T)
    return dict(w1=w1, w2=np.ascontiguousarray(w2), b1=b1,
                b2=chunked(ins["L0_ffn_b2"][layer]),
                lng=chunked(ins["L0_ffn_ln_g"][layer]),
                lnb=chunked(ins["L0_ffn_ln_b"][layer]))


def l0f_ffn_in_maps(xmid):
    maps = []
    for c in range(NCORES):
        b, s = divmod(c, 4)
        s0 = s * T_OWN
        xw = np.zeros((D, T_F), np.float32)
        lo = s0 - 1
        src_lo = max(lo, 0)
        xw[:, src_lo - lo:] = xmid[b, src_lo:s0 + T_OWN, :].T
        maps.append({"x": np.ascontiguousarray(xw)})
    return maps



def build_ln():
    nc = bass.Bass("TRN2", target_bir_lowering=False, debug=False)
    x_h = nc.dram_tensor("x", [D, T_OWN], F32, kind="ExternalInput").ap()
    y_h = nc.dram_tensor("y", [D, T_OWN], F32, kind="ExternalOutput").ap()
    es = ExitStack()
    sb = lambda n, s, d=F32: es.enter_context(nc.sbuf_tensor(n, s, d)).ap()
    psb = lambda n, s, d=F32: es.enter_context(nc.psum_tensor(n, s, d)).ap()
    with es:
        sxf = sb("sxf", [128, NCH, 512])
        ssq = sb("ssq", [128, 512])
        ssq2 = sb("ssq2", [128, 512])
        sy = sb("sy", [128, 512])
        srow_a = sb("srow_a", [1, 512])
        srow_b = sb("srow_b", [1, 512])
        sones = sb("sones", [128, 1])
        sones_r = sb("sones_r", [1, 128])
        seps = sb("seps", [1, 1])
        prow_m = psb("prow_m", [1, 512])
        prow_v = psb("prow_v", [1, 512])
        pbcA = psb("pbcA", [128, 512])
        pbcB = psb("pbcB", [128, 512])
        p = Prog(nc, es)
        lx = p.sem("lx")
        t_c = p.sem("t_c")
        v_c = p.sem("v_c")
        s_c = p.sem("s_c")
        st = p.sem("st")
        block = es.enter_context(nc.Block())
        p.emit("vector", lambda e: e.memset(sones, 1.0))
        p.emit("vector", lambda e: e.memset(sones_r, 1.0))
        p.emit("vector", lambda e: e.memset(seps, EPS), inc=v_c)
        init_done = v_c.v
        for (s0, w) in strips(T_OWN, 512):
            t_snap, v_snap, s_snap = t_c.v, v_c.v, s_c.v
            for i in range(NCH):
                p.emit("sync", lambda e, i=i, s0=s0, w=w: e.dma_start(
                    out=sxf[:, i, :w], in_=x_h[i * 128:(i + 1) * 128, s0:s0 + w]),
                       waits=((v_c, max(v_snap, init_done)), (st, st.v))
                       if i == 0 else (),
                       inc=lx, inc_n=16)
            x_ld = lx.v
            emit_ln_rows(
                p, x_chunks=[sxf[:, i, :w] for i in range(NCH)], w=w,
                prow_m=prow_m[:, :w], prow_v=prow_v[:, :w], sq=ssq2[:, :w],
                row_a=srow_a[:, :w], row_b=srow_b[:, :w],
                ones=sones, t_c=t_c, v_c=v_c, s_c=s_c, eps_ap=seps,
                first_t_waits=((lx, x_ld), (v_c, max(init_done, v_snap))),
                first_s_waits=((lx, x_ld), (t_c, t_snap), (v_c, v_snap)))
            rows_done = v_c.v
            p.emit("tensor", lambda e, w=w: e.matmul(
                pbcA[:, :w], sones_r, srow_a[:, :w], start=True, stop=True),
                   waits=((v_c, rows_done),))
            p.emit("tensor", lambda e, w=w: e.matmul(
                pbcB[:, :w], sones_r, srow_b[:, :w], start=True, stop=True), inc=t_c)
            bc_done = t_c.v
            for i in range(NCH):
                p.emit("vector", lambda e, i=i, w=w: e.tensor_mul(
                    out=ssq[:, :w], in0=sxf[:, i, :w], in1=pbcA[:, :w]),
                       waits=((t_c, bc_done),) if i == 0 else ())
                p.emit("vector", lambda e, i=i, w=w: e.tensor_add(
                    out=sy[:, :w], in0=ssq[:, :w], in1=pbcB[:, :w]), inc=v_c)
                y_done = v_c.v
                p.emit("gpsimd", lambda e, i=i, s0=s0, w=w: e.dma_start(
                    out=y_h[i * 128:(i + 1) * 128, s0:s0 + w], in_=sy[:, :w]),
                       waits=((v_c, y_done),), inc=st, inc_n=16)
                p.wait("vector", st, st.v)
            p.wait("tensor", v_c, v_c.v)
        p.wait("gpsimd", st, st.v)
        p.finalize(block)
    return nc


def build_add():
    nc = bass.Bass("TRN2", target_bir_lowering=False, debug=False)
    a_h = nc.dram_tensor("a", [D, T_OWN], F32, kind="ExternalInput").ap()
    b_h = nc.dram_tensor("b", [D, T_OWN], F32, kind="ExternalInput").ap()
    y_h = nc.dram_tensor("y", [D, T_OWN], F32, kind="ExternalOutput").ap()
    es = ExitStack()
    sb = lambda n, s, d=F32: es.enter_context(nc.sbuf_tensor(n, s, d)).ap()
    with es:
        sa = sb("sa", [128, NCH, 512])
        sbb = sb("sbb", [128, NCH, 512])
        sy = sb("sy", [128, 512])
        p = Prog(nc, es)
        lx = p.sem("lx")
        v_c = p.sem("v_c")
        st = p.sem("st")
        block = es.enter_context(nc.Block())
        for (s0, w) in strips(T_OWN, 512):
            v_snap = v_c.v
            for i in range(NCH):
                p.emit("sync", lambda e, i=i, s0=s0, w=w: e.dma_start(
                    out=sa[:, i, :w], in_=a_h[i * 128:(i + 1) * 128, s0:s0 + w]),
                       waits=((v_c, v_snap),) if i == 0 else (), inc=lx, inc_n=16)
                p.emit("sync", lambda e, i=i, s0=s0, w=w: e.dma_start(
                    out=sbb[:, i, :w], in_=b_h[i * 128:(i + 1) * 128, s0:s0 + w]),
                       inc=lx, inc_n=16)
            x_ld = lx.v
            for i in range(NCH):
                p.emit("vector", lambda e, i=i, w=w: e.tensor_add(
                    out=sy[:, :w], in0=sa[:, i, :w], in1=sbb[:, i, :w]),
                       waits=((lx, x_ld),) if i == 0 else ((st, st.v),), inc=v_c)
                y_done = v_c.v
                p.emit("gpsimd", lambda e, i=i, s0=s0, w=w: e.dma_start(
                    out=y_h[i * 128:(i + 1) * 128, s0:s0 + w], in_=sy[:, :w]),
                       waits=((v_c, y_done),), inc=st, inc_n=16)
        p.wait("gpsimd", st, st.v)
        p.finalize(block)
    return nc


def shard_own_fm(xfull):
    return [np.ascontiguousarray(xfull[c // 4, (c % 4) * T_OWN:((c % 4) + 1) * T_OWN, :].T)
            for c in range(NCORES)]



_NC_CACHE = {}


def _get_nc(name, builder):
    if name not in _NC_CACHE:
        _NC_CACHE[name] = builder()
    return _NC_CACHE[name]


def kernel(**inputs):
    ins = {k: np.asarray(v, np.float32) for k, v in inputs.items()}
    x0 = ins["L0_attn_x"]
    cores = list(range(NCORES))

    nc = _get_nc("l0_attn", build_l0_attn)
    wm = l0_attn_weight_maps(ins)
    maps = [dict(m, **wm) for m in l0_attn_in_maps(x0)]
    res = run_bass_kernel_spmd(nc, maps, core_ids=cores)
    x1 = assemble_l1_attn_out(res.results)

    nc = _get_nc("ln", build_ln)
    maps = [{"x": s} for s in shard_own_fm(x1)]
    res = run_bass_kernel_spmd(nc, maps, core_ids=cores)
    h = assemble_l1_attn_out(res.results)

    for layer in range(6):
        nc = _get_nc("l0f_attn", build_l0f_attn)
        wm = l0f_attn_layer_maps(ins, layer)
        maps = [dict(m, **wm) for m in l0f_attn_in_maps(h)]
        res = run_bass_kernel_spmd(nc, maps, core_ids=cores)
        hmid = assemble_l1_attn_out(res.results)

        nc = _get_nc("l0f_ffn", build_l0f_ffn)
        wm = l0f_ffn_layer_maps(ins, layer)
        maps = [dict(m, **wm) for m in l0f_ffn_in_maps(hmid)]
        res = run_bass_kernel_spmd(nc, maps, core_ids=cores)
        h = assemble_l1_attn_out(res.results)

    nc = _get_nc("add", build_add)
    a_sh, b_sh = shard_own_fm(x1), shard_own_fm(h)
    maps = [{"a": a_sh[c], "b": b_sh[c]} for c in cores]
    res = run_bass_kernel_spmd(nc, maps, core_ids=cores)
    x2 = assemble_l1_attn_out(res.results)

    nc = _get_nc("l1_attn", build_l1_attn)
    wm = l1_attn_weight_maps(ins["L1_attn_Wq"], ins["L1_attn_Wk"], ins["L1_attn_Wv"])
    maps = [dict(m, **wm) for m in l1_attn_in_maps(x2)]
    res = run_bass_kernel_spmd(nc, maps, core_ids=cores)
    x3 = assemble_l1_attn_out(res.results)

    nc = _get_nc("l1_ffn", build_l1_ffn)
    maps = l1_ffn_in_maps(x3, ins["L1_ffn_W_r"], ins["L1_ffn_W_k"],
                          ins["L1_ffn_W_v"], ins["L1_ffn_mu_r"], ins["L1_ffn_mu_k"])
    res = run_bass_kernel_spmd(nc, maps, core_ids=cores)
    return assemble_fm_out(res.results)


# version 02
# speedup vs baseline: 1.1089x; 1.1089x over previous
import numpy as np
import ml_dtypes
from contextlib import ExitStack

import concourse.bass as bass
import concourse.mybir as mybir
from concourse.bass_utils import run_bass_kernel_spmd

F32 = mybir.dt.float32
BF16 = mybir.dt.bfloat16
AF = mybir.ActivationFunctionType
ALU = mybir.AluOpType
AX = mybir.AxisListType

BF = ml_dtypes.bfloat16

NCORES = 8
D = 1024
NCH = D // 128
SEQ = 8192
BATCH = 2
T_OWN = 2048
T = T_OWN + 1
T_PAD = 2560
EPS = 1e-5



class Ctr:
    __slots__ = ("h", "v")

    def __init__(self, h):
        self.h = h
        self.v = 0

    def inc(self, n=1):
        self.v += n
        return self.v


class Prog:
    ENGINES = ("sync", "tensor", "vector", "scalar", "gpsimd")

    def __init__(self, nc, es):
        self.nc = nc
        self.es = es
        self.streams = {e: [] for e in self.ENGINES}

    def sem(self, name):
        h = self.es.enter_context(self.nc.semaphore(name))
        return Ctr(h)

    def emit(self, eng, fn, waits=(), inc=None, inc_n=1):
        waits = tuple((c, v) for (c, v) in waits if v > 0)

        def run(e, fn=fn, waits=waits, inc=inc, inc_n=inc_n):
            for c, v in waits:
                e.wait_ge(c.h, v)
            inst = fn(e)
            if inc is not None:
                assert inst is not None
                inst.then_inc(inc.h, inc_n)

        self.streams[eng].append(run)
        if inc is not None:
            return inc.inc(inc_n)
        return None

    def wait(self, eng, ctr, val):
        if val > 0:
            self.emit(eng, lambda e: None, waits=((ctr, val),))

    def finalize(self, block):
        for eng in self.ENGINES:
            fns = self.streams[eng]
            if not fns:
                continue
            getattr(block, eng)(lambda e, fns=fns: [f(e) for f in fns])


def strips(total, width, start=0):
    return [(s, min(width, total + start - s)) for s in range(start, total + start, width)]



def emit_ln_rows(p, *, x_chunks, w, prow_m, prow_v, sq, row_a, row_b, ones,
                 t_c, v_c, s_c, eps_ap=None, first_t_waits=(), first_s_waits=()):
    for i, xc in enumerate(x_chunks):
        p.emit("tensor",
               lambda e, xc=xc, i=i: e.matmul(prow_m, ones, xc,
                                              start=(i == 0), stop=(i == NCH - 1)),
               waits=first_t_waits if i == 0 else (),
               inc=t_c if i == NCH - 1 else None)
    mdone = t_c.v
    for i, xc in enumerate(x_chunks):
        p.emit("scalar",
               lambda e, xc=xc: e.activation(out=sq, in_=xc, func=AF.Square),
               waits=first_s_waits if i == 0 else (), inc=s_c)
        sdone = s_c.v
        p.emit("tensor",
               lambda e, i=i: e.matmul(prow_v, ones, sq,
                                       start=(i == 0), stop=(i == NCH - 1)),
               waits=((s_c, sdone),), inc=t_c)
        if i < NCH - 1:
            p.wait("scalar", t_c, t_c.v)
    vsum_done = t_c.v
    p.emit("vector", lambda e: e.tensor_scalar_mul(out=row_b, in0=prow_m, scalar1=1.0 / D),
           waits=((t_c, mdone),))
    p.emit("vector", lambda e: e.tensor_scalar_mul(out=row_a, in0=prow_v, scalar1=1.0 / D),
           waits=((t_c, vsum_done),))
    p.emit("vector", lambda e: e.tensor_mul(out=sq[0:1, :w], in0=row_b, in1=row_b))
    p.emit("vector", lambda e: e.tensor_sub(out=row_a, in0=row_a, in1=sq[0:1, :w]),
           inc=v_c)
    vdone = v_c.v
    p.emit("scalar", lambda e: e.activation(out=row_a, in_=row_a, func=AF.Ln, bias=eps_ap),
           waits=((v_c, vdone),))
    p.emit("scalar", lambda e: e.activation(out=row_a, in_=row_a, func=AF.Exp, scale=-0.5),
           inc=s_c)
    rstd_done = s_c.v
    p.emit("vector", lambda e: e.tensor_mul(out=row_b, in0=row_b, in1=row_a),
           waits=((s_c, rstd_done),))
    p.emit("vector", lambda e: e.tensor_scalar_mul(out=row_b, in0=row_b, scalar1=-1.0),
           inc=v_c)
    return v_c.v



def build_l1_ffn():
    nc = bass.Bass("TRN2", target_bir_lowering=False, debug=False)
    x_h = nc.dram_tensor("x", [D, T_PAD], F32, kind="ExternalInput").ap()
    w_h = {k: nc.dram_tensor(k, [D, D], F32, kind="ExternalInput").ap()
           for k in ("wr", "wk", "wv")}
    mu_h = {k: nc.dram_tensor(k, [128, NCH], F32, kind="ExternalInput").ap()
            for k in ("mur", "muk")}
    y_h = nc.dram_tensor("y", [D, T_OWN], F32, kind="ExternalOutput").ap()

    es = ExitStack()
    sb = lambda n, s, d=F32: es.enter_context(nc.sbuf_tensor(n, s, d)).ap()
    psb = lambda n, s, d=F32: es.enter_context(nc.psum_tensor(n, s, d)).ap()
    with es:
        sxs = sb("sxs", [128, NCH, 512])
        sxo = sb("sxo", [128, NCH, 512])
        swf = sb("swf", [128, 2, D])
        sw = {k: sb("sw_" + k, [128, NCH, D], BF16) for k in ("wr", "wk", "wv")}
        smu = {k: sb("smu_" + k, [128, NCH]) for k in ("mur", "muk")}
        som = {k: sb("som_" + k, [128, NCH]) for k in ("mur", "muk")}
        sxln = sb("sxln", [128, NCH, T_PAD], BF16)
        sxr = sb("sxr", [128, NCH, 512], BF16)
        sxk = sb("sxk", [128, NCH, 512], BF16)
        sr = sb("sr", [128, NCH, 512])
        sk2 = sb("sk2", [128, NCH, 512], BF16)
        sy = sb("sy", [128, NCH, 512])
        ssq = sb("ssq", [128, 512])
        ssq2 = sb("ssq2", [128, 512])
        srow_a = sb("srow_a", [1, 512])
        srow_b = sb("srow_b", [1, 512])
        sones = sb("sones", [128, 1])
        sones_r = sb("sones_r", [1, 128])
        seps = sb("seps", [1, 1])
        prow_m = psb("prow_m", [1, 512])
        prow_v = psb("prow_v", [1, 512])
        pbcA = psb("pbcA", [128, 512])
        pbcB = psb("pbcB", [128, 512])
        pmm = psb("pmm", [128, 512])
        pmm2 = psb("pmm2", [128, 512])

        p = Prog(nc, es)
        li = p.sem("li")
        lx = p.sem("lx")
        lw = [p.sem("lw0"), p.sem("lw1")]
        t_c = p.sem("t_c")
        v_c = p.sem("v_c")
        s_c = p.sem("s_c")
        st = p.sem("st")

        block = es.enter_context(nc.Block())

        for k in ("mur", "muk"):
            p.emit("sync", lambda e, k=k: e.dma_start(out=smu[k], in_=mu_h[k]),
                   inc=li, inc_n=16)
        mu_ld = li.v
        wcast_done = {}
        prev_cast = [0, 0]
        for k in ("wr", "wk", "wv"):
            for i in range(NCH):
                ib = i % 2
                p.emit("sync", lambda e, k=k, i=i, ib=ib: e.dma_start(
                    out=swf[:, ib, :], in_=w_h[k][i * 128:(i + 1) * 128, :]),
                       waits=((v_c, prev_cast[ib]),),
                       inc=lw[ib], inc_n=16)
                wl = lw[ib].v
                p.emit("vector", lambda e, k=k, i=i, ib=ib: e.tensor_copy(
                    out=sw[k][:, i, :], in_=swf[:, ib, :]),
                       waits=((lw[ib], wl),), inc=v_c)
                prev_cast[ib] = v_c.v
            wcast_done[k] = v_c.v
        p.emit("vector", lambda e: e.tensor_scalar(
            out=som["mur"], in0=smu["mur"], scalar1=-1.0, scalar2=1.0,
            op0=ALU.mult, op1=ALU.add), waits=((li, mu_ld),))
        p.emit("vector", lambda e: e.tensor_scalar(
            out=som["muk"], in0=smu["muk"], scalar1=-1.0, scalar2=1.0,
            op0=ALU.mult, op1=ALU.add), inc=v_c)
        p.emit("vector", lambda e: e.memset(sones, 1.0))
        p.emit("vector", lambda e: e.memset(sones_r, 1.0))
        p.emit("vector", lambda e: e.memset(seps, EPS), inc=v_c)
        ones_done = v_c.v

        for (s0, w) in strips(T_PAD, 512):
            t_snap, v_snap, s_snap = t_c.v, v_c.v, s_c.v
            for i in range(NCH):
                p.emit("sync", lambda e, i=i, s0=s0, w=w: e.dma_start(
                    out=sxs[:, i, :w], in_=x_h[i * 128:(i + 1) * 128, s0:s0 + w]),
                       waits=((v_c, v_snap),) if i == 0 else (),
                       inc=lx, inc_n=16)
            x_ld = lx.v
            emit_ln_rows(
                p, x_chunks=[sxs[:, i, :w] for i in range(NCH)], w=w,
                prow_m=prow_m[:, :w], prow_v=prow_v[:, :w], sq=ssq2[:, :w],
                row_a=srow_a[:, :w], row_b=srow_b[:, :w],
                ones=sones, t_c=t_c, v_c=v_c, s_c=s_c, eps_ap=seps,
                first_t_waits=((lx, x_ld), (v_c, max(ones_done, v_snap))),
                first_s_waits=((lx, x_ld), (t_c, t_snap), (v_c, v_snap)))
            rows_done = v_c.v
            p.emit("tensor", lambda e, w=w: e.matmul(
                pbcA[:, :w], sones_r, srow_a[:, :w], start=True, stop=True),
                   waits=((v_c, rows_done),))
            p.emit("tensor", lambda e, w=w: e.matmul(
                pbcB[:, :w], sones_r, srow_b[:, :w], start=True, stop=True),
                   inc=t_c)
            bc_done = t_c.v
            for i in range(NCH):
                p.emit("vector", lambda e, i=i, s0=s0, w=w: e.tensor_mul(
                    out=ssq[:, :w], in0=sxs[:, i, :w], in1=pbcA[:, :w]),
                       waits=((t_c, bc_done),) if i == 0 else ())
                p.emit("vector", lambda e, i=i, s0=s0, w=w: e.tensor_add(
                    out=sxln[:, i, s0:s0 + w], in0=ssq[:, :w], in1=pbcB[:, :w]),
                       inc=v_c)
            p.wait("tensor", v_c, v_c.v)
        xln_done = v_c.v

        for si, (s0, w) in enumerate(strips(T_OWN, 512)):
            c0 = s0 + 1
            t_snap = t_c.v
            v_snap2 = v_c.v
            for i in range(NCH):
                p.emit("sync", lambda e, i=i, c0=c0, w=w: e.dma_start(
                    out=sxo[:, i, :w], in_=x_h[i * 128:(i + 1) * 128, c0:c0 + w]),
                       waits=((v_c, v_snap2),) if i == 0 else (),
                       inc=lx, inc_n=16)
            xo_ld = lx.v
            for i in range(NCH):
                p.emit("vector", lambda e, i=i, c0=c0, w=w: e.tensor_scalar_mul(
                    out=ssq[:, :w], in0=sxln[:, i, c0:c0 + w],
                    scalar1=smu["mur"][:, i:i + 1]),
                       waits=((v_c, xln_done), (t_c, t_snap)) if i == 0 else ())
                p.emit("vector", lambda e, i=i, c0=c0, w=w: e.scalar_tensor_tensor(
                    out=sxr[:, i, :w], in0=sxln[:, i, c0 - 1:c0 - 1 + w],
                    scalar=som["mur"][:, i:i + 1], in1=ssq[:, :w],
                    op0=ALU.mult, op1=ALU.add))
                p.emit("vector", lambda e, i=i, c0=c0, w=w: e.tensor_scalar_mul(
                    out=ssq[:, :w], in0=sxln[:, i, c0:c0 + w],
                    scalar1=smu["muk"][:, i:i + 1]))
                p.emit("vector", lambda e, i=i, c0=c0, w=w: e.scalar_tensor_tensor(
                    out=sxk[:, i, :w], in0=sxln[:, i, c0 - 1:c0 - 1 + w],
                    scalar=som["muk"][:, i:i + 1], in1=ssq[:, :w],
                    op0=ALU.mult, op1=ALU.add), inc=v_c)
            mix_done = v_c.v

            for o in range(NCH):
                for i in range(NCH):
                    p.emit("tensor", lambda e, o=o, i=i, w=w: e.matmul(
                        pmm[:, :w], sw["wr"][:, i, o * 128:(o + 1) * 128],
                        sxr[:, i, :w], start=(i == 0), stop=(i == NCH - 1)),
                           waits=((v_c, max(mix_done, wcast_done["wr"])),)
                           if i == 0 else (),
                           inc=t_c if i == NCH - 1 else None)
                mmr = t_c.v
                p.emit("scalar", lambda e, o=o, w=w: e.activation(
                    out=sr[:, o, :w], in_=pmm[:, :w], func=AF.Sigmoid),
                       waits=((t_c, mmr),), inc=s_c)
                for i in range(NCH):
                    p.emit("tensor", lambda e, o=o, i=i, w=w: e.matmul(
                        pmm2[:, :w], sw["wk"][:, i, o * 128:(o + 1) * 128],
                        sxk[:, i, :w], start=(i == 0), stop=(i == NCH - 1)),
                           waits=((v_c, wcast_done["wv"]),) if (o == 0 and i == 0) else (),
                           inc=t_c if i == NCH - 1 else None)
                mmk = t_c.v
                p.emit("scalar", lambda e, o=o, w=w: e.activation(
                    out=ssq2[:, :w], in_=pmm2[:, :w], func=AF.Relu),
                       waits=((t_c, mmk),))
                p.emit("scalar", lambda e, o=o, w=w: e.activation(
                    out=sk2[:, o, :w], in_=ssq2[:, :w], func=AF.Square), inc=s_c)
                k2_done = s_c.v
                p.wait("tensor", s_c, k2_done)

            for o in range(NCH):
                for i in range(NCH):
                    p.emit("tensor", lambda e, o=o, i=i, w=w: e.matmul(
                        pmm[:, :w], sw["wv"][:, i, o * 128:(o + 1) * 128],
                        sk2[:, i, :w], start=(i == 0), stop=(i == NCH - 1)),
                           inc=t_c if i == NCH - 1 else None)
                mmv = t_c.v
                p.emit("vector", lambda e, o=o, w=w: e.tensor_mul(
                    out=sy[:, o, :w], in0=sr[:, o, :w], in1=pmm[:, :w]),
                       waits=((t_c, mmv),))
                p.emit("vector", lambda e, o=o, w=w: e.tensor_add(
                    out=sy[:, o, :w], in0=sy[:, o, :w], in1=sxo[:, o, :w]),
                       waits=((lx, xo_ld),) if o == 0 else (), inc=v_c)
                ydone = v_c.v
                p.wait("tensor", v_c, ydone)
                p.emit("gpsimd", lambda e, o=o, s0=s0, w=w: e.dma_start(
                    out=y_h[o * 128:(o + 1) * 128, s0:s0 + w], in_=sy[:, o, :w]),
                       waits=((v_c, ydone),), inc=st, inc_n=16)
            p.wait("vector", st, st.v)

        p.wait("gpsimd", st, st.v)
        p.finalize(block)
    return nc



def shard_tokens_fm(x_bt_d):
    out = []
    for c in range(NCORES):
        b, s = divmod(c, 4)
        s0 = s * T_OWN
        sl = np.zeros((D, T_PAD), dtype=np.float32)
        sl[:, 1:T] = x_bt_d[b, s0:s0 + T_OWN, :].T
        if s0 > 0:
            sl[:, 0] = x_bt_d[b, s0 - 1, :]
        out.append(np.ascontiguousarray(sl))
    return out


def mu_fm(mu):
    return np.ascontiguousarray(np.asarray(mu, np.float32).reshape(NCH, 128).T)


def l1_ffn_in_maps(x, W_r, W_k, W_v, mu_r, mu_k):
    wr = np.ascontiguousarray(np.asarray(W_r, np.float32).T)
    wk = np.ascontiguousarray(np.asarray(W_k, np.float32).T)
    wv = np.ascontiguousarray(np.asarray(W_v, np.float32).T)
    mur = mu_fm(mu_r)
    muk = mu_fm(mu_k)
    xs = shard_tokens_fm(x)
    return [dict(x=xs[c], wr=wr, wk=wk, wv=wv, mur=mur, muk=muk)
            for c in range(NCORES)]


def assemble_fm_out(results, key="y"):
    out = np.empty((BATCH, SEQ, D), dtype=np.float32)
    for c in range(NCORES):
        b, s = divmod(c, 4)
        out[b, s * T_OWN:(s + 1) * T_OWN, :] = results[c][key].T
    return out






def strip_barrier(p, t_c, v_c, s_c):
    ct, cv, cs = t_c.v, v_c.v, s_c.v
    p.wait("tensor", v_c, cv)
    p.wait("tensor", s_c, cs)
    p.wait("scalar", v_c, cv)
    p.wait("scalar", t_c, ct)
    p.wait("vector", t_c, ct)
    p.wait("vector", s_c, cs)


def emit_attn_b5(p, *, h, skne, sqne, svtm, psc, sp, pkv, snrm, sto, south,
                 sones_r, osc_h, t_c, v_c, s_c, g_c,
                 first_waits=(), qs_range=None, qne_col=None):
    if qs_range is None:
        qs_range = range(QSTRIPS)
    if qne_col is None:
        qne_col = lambda qs: qs * 512
    first = True
    for qs in qs_range:
        q0 = qs * 512
        qc = qne_col(qs)
        g_snap = g_c.v
        v_snap_b5 = v_c.v
        ngroups = KCH // 2
        fill_done = [0] * ngroups
        exp_done = [0] * ngroups
        for g in range(ngroups):
            gb = g % 2
            for cc in range(2):
                kchunk = g * 2 + cc
                k0 = kchunk * 128
                p.emit("tensor", lambda e, gb=gb, cc=cc, k0=k0, qc=qc: e.matmul(
                    psc[:, gb, cc * 512:(cc + 1) * 512],
                    skne[:, k0:k0 + 128], sqne[:, qc:qc + 512],
                    start=True, stop=True),
                       waits=tuple(
                           (list(first_waits) if (first and g == 0 and cc == 0) else [])
                           + ([(s_c, exp_done[g - 2])] if (g >= 2 and cc == 0) else [])),
                       inc=t_c if cc == 1 else None)
            fill_done[g] = t_c.v
            p.emit("scalar", lambda e, gb=gb: e.activation(
                out=sp[:, gb, :], in_=psc[:, gb, :], func=AF.Exp),
                   waits=((t_c, fill_done[g]),), inc=s_c)
            exp_done[g] = s_c.v
            for cc in range(2):
                kchunk = g * 2 + cc
                p.emit("tensor", lambda e, gb=gb, cc=cc, kchunk=kchunk: e.matmul(
                    pkv[0:65, :], svtm[:, kchunk, :], sp[:, gb, cc * 512:(cc + 1) * 512],
                    start=(kchunk == 0), stop=(kchunk == KCH - 1)),
                       waits=((s_c, exp_done[g]),) if cc == 0 else (),
                       inc=t_c if (cc == 1 and kchunk == KCH - 1) else None)
        pv_done = t_c.v
        p.emit("vector", lambda e: e.reciprocal(out=snrm[:, :512], in_=pkv[64:65, :512]),
               waits=((t_c, pv_done),), inc=v_c)
        den_done = v_c.v
        p.emit("scalar", lambda e: e.activation(
            out=sto[:, :], in_=pkv[0:64, :512], func=AF.Copy),
               waits=((t_c, pv_done), (v_c, v_snap_b5)), inc=s_c)
        out_evac = s_c.v
        p.emit("tensor", lambda e: e.matmul(
            psc[0:64, 0, 0:512],
            sones_r[:, 0:64], snrm[:, :512], start=True, stop=True),
               waits=((v_c, den_done), (s_c, exp_done[ngroups - 1]),), inc=t_c)
        den_bc = t_c.v
        p.emit("vector", lambda e: e.tensor_mul(
            out=south, in0=sto, in1=psc[0:64, 0, 0:512]),
               waits=((t_c, den_bc), (s_c, out_evac), (g_c, g_snap)), inc=v_c)
        south_done = v_c.v
        p.emit("gpsimd", lambda e, h=h, q0=q0: e.dma_start(
            out=osc_h[h * 64:(h + 1) * 64, q0:q0 + 512], in_=south),
               waits=((v_c, south_done),), inc=g_c, inc_n=16)
        p.wait("tensor", v_c, south_done)
        p.wait("scalar", v_c, south_done)
        first = False






NHEAD = 16
HD = 64
KCH = SEQ // 128
QSTRIPS = T_OWN // 512


def build_l1_attn():
    nc = bass.Bass("TRN2", target_bir_lowering=False, debug=False)
    xf_h = nc.dram_tensor("xf", [D, SEQ], F32, kind="ExternalInput").ap()
    wq_h = nc.dram_tensor("wq", [NHEAD, D, HD], F32, kind="ExternalInput").ap()
    wkv_h = nc.dram_tensor("wkv", [NHEAD, D, 128], F32, kind="ExternalInput").ap()
    ident_h = nc.dram_tensor("ident", [128, 128], BF16, kind="ExternalInput").ap()
    y_h = nc.dram_tensor("y", [D, T_OWN], F32, kind="ExternalOutput").ap()
    osc_h = nc.dram_tensor("osc", [D, T_OWN], BF16, kind="Internal").ap()

    es = ExitStack()
    sb = lambda n, s, d=F32: es.enter_context(nc.sbuf_tensor(n, s, d)).ap()
    psb = lambda n, s, d=F32: es.enter_context(nc.psum_tensor(n, s, d)).ap()
    with es:
        sxln = sb("sxln", [128, NCH, SEQ], BF16)
        skne = sb("skne", [65, SEQ], BF16)
        svtm = sb("svtm", [128, KCH, 65], BF16)
        sqne = sb("sqne", [65, T_OWN], BF16)
        swkv = sb("swkv", [128, 2, NCH, 128], BF16)
        swq = sb("swq", [128, 2, NCH, HD], BF16)
        swf = sb("swf", [128, 2, 128])
        sxf = sb("sxf", [128, NCH, 512])
        svfm = sb("svfm", [64, 512], BF16)
        sto = sb("sto", [64, 512], BF16)
        sp = sb("sp", [128, 2, 1024], BF16)
        snrm = sb("snrm", [1, 512])
        skmax = sb("skmax", [1, 1])
        srow_a = sb("srow_a", [1, 512])
        srow_b = sb("srow_b", [1, 512])
        ssq = sb("ssq", [128, 512])
        ssq2 = sb("ssq2", [128, 512])
        sident = sb("sident", [128, 128], BF16)
        sones = sb("sones", [128, 1])
        sones_r = sb("sones_r", [1, 128])
        seps = sb("seps", [1, 1])
        south = sb("south", [64, 512], BF16)
        sres = sb("sres", [128, NCH, 512], BF16)
        prow_m = psb("prow_m", [1, 512])
        prow_v = psb("prow_v", [1, 512])
        pkv = psb("pkv", [128, 512])
        ptr = psb("ptr", [128, 64], BF16)
        psc = psb("psc", [128, 2, 1024])

        p = Prog(nc, es)
        li = p.sem("li")
        lx = p.sem("lx")
        lw = [p.sem("lw0"), p.sem("lw1")]
        t_c = p.sem("t_c")
        v_c = p.sem("v_c")
        s_c = p.sem("s_c")
        g_c = p.sem("g_c")
        st = p.sem("st")

        block = es.enter_context(nc.Block())

        p.emit("sync", lambda e: e.dma_start(out=sident, in_=ident_h), inc=li, inc_n=16)
        p.emit("vector", lambda e: e.memset(sones, 1.0))
        p.emit("vector", lambda e: e.memset(sones_r, 1.0))
        p.emit("vector", lambda e: e.memset(seps, EPS))
        p.emit("vector", lambda e: e.memset(skne[64:65, :], 1.0))
        p.emit("vector", lambda e: e.memset(svtm[:, :, 64], 1.0), inc=v_c)
        init_done = v_c.v

        for (s0, w) in strips(SEQ, 512):
            t_snap, v_snap, s_snap = t_c.v, v_c.v, s_c.v
            for i in range(NCH):
                p.emit("sync", lambda e, i=i, s0=s0, w=w: e.dma_start(
                    out=sxf[:, i, :w], in_=xf_h[i * 128:(i + 1) * 128, s0:s0 + w]),
                       waits=((v_c, v_snap),) if i == 0 else (),
                       inc=lx, inc_n=16)
            x_ld = lx.v
            emit_ln_rows(
                p, x_chunks=[sxf[:, i, :w] for i in range(NCH)], w=w,
                prow_m=prow_m[:, :w], prow_v=prow_v[:, :w], sq=ssq2[:, :w],
                row_a=srow_a[:, :w], row_b=srow_b[:, :w],
                ones=sones, t_c=t_c, v_c=v_c, s_c=s_c, eps_ap=seps,
                first_t_waits=((lx, x_ld), (v_c, max(init_done, v_snap))),
                first_s_waits=((lx, x_ld), (t_c, t_snap), (v_c, v_snap)))
            rows_done = v_c.v
            p.emit("tensor", lambda e, w=w: e.matmul(
                pkv[:, :w], sones_r, srow_a[:, :w], start=True, stop=True),
                   waits=((v_c, rows_done),))
            p.emit("tensor", lambda e, w=w: e.matmul(
                psc[:, 0, :w], sones_r, srow_b[:, :w], start=True, stop=True),
                   inc=t_c)
            bc_done = t_c.v
            for i in range(NCH):
                p.emit("vector", lambda e, i=i, w=w: e.tensor_mul(
                    out=ssq[:, :w], in0=sxf[:, i, :w], in1=pkv[:, :w]),
                       waits=((t_c, bc_done),) if i == 0 else ())
                p.emit("vector", lambda e, i=i, s0=s0, w=w: e.tensor_add(
                    out=sxln[:, i, s0:s0 + w], in0=ssq[:, :w], in1=psc[:, 0, :w]),
                       inc=v_c)
            p.wait("tensor", v_c, v_c.v)
        xln_done = v_c.v

        for h in range(NHEAD):
            hb = h % 2
            cast_prev = [v_c.v, v_c.v]
            for i in range(NCH):
                ib = i % 2
                p.emit("sync", lambda e, h=h, i=i, ib=ib: e.dma_start(
                    out=swf[:, ib, :], in_=wkv_h[h, i * 128:(i + 1) * 128, :]),
                       waits=((v_c, cast_prev[ib]),), inc=lw[ib], inc_n=16)
                wld = lw[ib].v
                p.emit("vector", lambda e, hb=hb, i=i, ib=ib: e.tensor_copy(
                    out=swkv[:, hb, i, :], in_=swf[:, ib, :]),
                       waits=((lw[ib], wld),), inc=v_c)
                cast_prev[ib] = v_c.v
            kvw_done = v_c.v
            for i in range(NCH):
                ib = i % 2
                p.emit("sync", lambda e, h=h, i=i, ib=ib: e.dma_start(
                    out=swf[:, ib, :HD], in_=wq_h[h, i * 128:(i + 1) * 128, :]),
                       waits=((v_c, cast_prev[ib]),), inc=lw[ib], inc_n=16)
                wld = lw[ib].v
                p.emit("vector", lambda e, hb=hb, i=i, ib=ib: e.tensor_copy(
                    out=swq[:, hb, i, :], in_=swf[:, ib, :HD]),
                       waits=((lw[ib], wld),), inc=v_c)
                cast_prev[ib] = v_c.v
            qw_done = v_c.v

            p.emit("vector", lambda e: e.memset(skmax, 0.0), inc=v_c)
            prev_head_attn_t = t_c.v
            for (s0, w) in strips(SEQ, 512):
                strip_barrier(p, t_c, v_c, s_c)
                for i in range(NCH):
                    p.emit("tensor", lambda e, hb=hb, i=i, s0=s0, w=w: e.matmul(
                        pkv[:, :w], swkv[:, hb, i, :], sxln[:, i, s0:s0 + w],
                        start=(i == 0), stop=(i == NCH - 1)),
                           waits=((v_c, max(kvw_done, xln_done)),) if i == 0 else (),
                           inc=t_c if i == NCH - 1 else None)
                kv_mm = t_c.v
                p.emit("scalar", lambda e, s0=s0, w=w: e.activation(
                    out=skne[0:64, s0:s0 + w], in_=pkv[0:64, :w], func=AF.Copy),
                       waits=((t_c, kv_mm),))
                p.emit("scalar", lambda e, w=w: e.activation(
                    out=svfm[:, :w], in_=pkv[64:128, :w], func=AF.Copy), inc=s_c)
                kv_evac = s_c.v
                p.emit("scalar", lambda e, s0=s0, w=w: e.activation(
                    out=ssq2[0:64, :w], in_=pkv[0:64, :w], func=AF.Square), inc=s_c)
                ksq_done = s_c.v
                p.wait("tensor", s_c, kv_evac)
                p.emit("tensor", lambda e, w=w: e.matmul(
                    prow_v[:, :w], sones[0:64], ssq2[0:64, :w], start=True, stop=True),
                       waits=((s_c, ksq_done),), inc=t_c)
                knorm_mm = t_c.v
                p.emit("vector", lambda e, w=w: e.reduce_max(
                    out=snrm[:, 0:1], in_=prow_v[:, :w], axis=AX.X),
                       waits=((t_c, knorm_mm),))
                p.emit("vector", lambda e: e.tensor_max(
                    out=skmax, in0=skmax, in1=snrm[:, 0:1]), inc=v_c)
                for j in range(w // 128):
                    kchunk = s0 // 128 + j
                    p.emit("tensor", lambda e, j=j: e.transpose(
                        ptr[:, :], svfm[:, j * 128:(j + 1) * 128], sident[0:64, 0:64]),
                           waits=((s_c, kv_evac),) if j == 0 else (), inc=t_c)
                    tr_done = t_c.v
                    p.emit("vector", lambda e, kchunk=kchunk: e.tensor_copy(
                        out=svtm[:, kchunk, 0:64], in_=ptr[:, :]),
                           waits=((t_c, tr_done),), inc=v_c)
                    p.wait("tensor", v_c, v_c.v)
                p.wait("scalar", t_c, t_c.v)
            kn_done_v = v_c.v
            kn_done_s = s_c.v

            p.emit("scalar", lambda e: e.activation(out=skmax, in_=skmax, func=AF.Ln),
                   waits=((v_c, kn_done_v),))
            p.emit("scalar", lambda e: e.activation(out=skmax, in_=skmax, func=AF.Exp,
                                                    scale=0.5), inc=s_c)
            kmax_done = s_c.v

            for (s0, w) in strips(T_OWN, 512):
                strip_barrier(p, t_c, v_c, s_c)
                for i in range(NCH):
                    p.emit("tensor", lambda e, hb=hb, i=i, s0=s0, w=w: e.matmul(
                        pkv[0:64, :w], swq[:, hb, i, :], sxln[:, i, s0:s0 + w],
                        start=(i == 0), stop=(i == NCH - 1)),
                           waits=((v_c, qw_done),) if i == 0 else (),
                           inc=t_c if i == NCH - 1 else None)
                q_mm = t_c.v
                p.emit("scalar", lambda e, s0=s0, w=w: e.activation(
                    out=sqne[0:64, s0:s0 + w], in_=pkv[0:64, :w], func=AF.Copy,
                    scale=0.125), waits=((t_c, q_mm),))
                p.emit("scalar", lambda e, w=w: e.activation(
                    out=ssq2[0:64, :w], in_=pkv[0:64, :w], func=AF.Square), inc=s_c)
                qsq_done = s_c.v
                p.wait("tensor", s_c, qsq_done)
                p.emit("tensor", lambda e, w=w: e.matmul(
                    prow_v[:, :w], sones[0:64], ssq2[0:64, :w], start=True, stop=True),
                       inc=t_c)
                qn_mm = t_c.v
                p.emit("scalar", lambda e, w=w: e.activation(
                    out=snrm[:, :w], in_=prow_v[:, :w], func=AF.Ln),
                       waits=((t_c, qn_mm),))
                p.emit("scalar", lambda e, w=w: e.activation(
                    out=snrm[:, :w], in_=snrm[:, :w], func=AF.Exp,
                    scale=0.5), inc=s_c)
                qrow_done = s_c.v
                p.wait("tensor", s_c, qrow_done)
                p.emit("vector", lambda e, s0=s0, w=w: e.tensor_scalar(
                    out=sqne[64:65, s0:s0 + w], in0=snrm[:, :w], scalar1=skmax,
                    scalar2=-0.125, op0=ALU.mult, op1=ALU.mult),
                       waits=((s_c, max(qrow_done, kmax_done)),))
                p.emit("vector", lambda e, s0=s0, w=w: e.tensor_scalar_add(
                    out=sqne[64:65, s0:s0 + w], in0=sqne[64:65, s0:s0 + w],
                    scalar1=-0.05), inc=v_c)
                p.wait("scalar", v_c, v_c.v)
            bias_done = v_c.v

            emit_attn_b5(p, h=h, skne=skne, sqne=sqne, svtm=svtm, psc=psc,
                         sp=sp, pkv=pkv, snrm=snrm, sto=sto, south=south,
                         sones_r=sones_r, osc_h=osc_h,
                         t_c=t_c, v_c=v_c, s_c=s_c, g_c=g_c,
                         first_waits=((v_c, bias_done), (s_c, kn_done_s)))
        attn_out_dma = g_c.v

        for (s0, w) in strips(T_OWN, 512):
            v_snap = v_c.v
            for i in range(NCH):
                p.emit("sync", lambda e, i=i, s0=s0, w=w: e.dma_start(
                    out=sxf[:, i, :w], in_=xf_h[i * 128:(i + 1) * 128, s0:s0 + w]),
                       waits=((v_c, v_snap),) if i == 0 else (), inc=lx, inc_n=16)
            x_ld2 = lx.v
            p.emit("sync", lambda e, s0=s0, w=w: e.dma_start(
                out=sres[:, :, :w],
                in_=osc_h.rearrange("(c p) t -> p c t", p=128)[:, :, s0:s0 + w]),
                   waits=((g_c, attn_out_dma),), inc=lx, inc_n=16)
            o_ld = lx.v
            for i in range(NCH):
                p.emit("vector", lambda e, i=i, w=w: e.tensor_add(
                    out=ssq[:, :w], in0=sxf[:, i, :w], in1=sres[:, i, :w]),
                       waits=((lx, max(x_ld2, o_ld)),) if i == 0 else ())
                p.emit("vector", lambda e, i=i, w=w: e.tensor_copy(
                    out=sxf[:, i, :w], in_=ssq[:, :w]), inc=v_c)
                ydone = v_c.v
                p.emit("gpsimd", lambda e, i=i, s0=s0, w=w: e.dma_start(
                    out=y_h[i * 128:(i + 1) * 128, s0:s0 + w], in_=sxf[:, i, :w]),
                       waits=((v_c, ydone),), inc=st, inc_n=16)
            p.wait("sync", st, st.v)

        p.wait("gpsimd", st, st.v)
        p.finalize(block)
    return nc


def l1_attn_in_maps(x2):
    maps = []
    for c in range(NCORES):
        b, s = divmod(c, 4)
        xf = np.ascontiguousarray(np.roll(x2[b], -s * T_OWN, axis=0).T)
        maps.append({"xf": xf})
    return maps


def l1_attn_weight_maps(Wq, Wk, Wv):
    wqT = np.asarray(Wq, np.float32).T
    wkT = np.asarray(Wk, np.float32).T
    wvT = np.asarray(Wv, np.float32).T
    wq = np.stack([np.ascontiguousarray(wqT[:, h * HD:(h + 1) * HD])
                   for h in range(NHEAD)])
    wkv = np.stack([np.ascontiguousarray(np.concatenate(
        [wkT[:, h * HD:(h + 1) * HD], wvT[:, h * HD:(h + 1) * HD]], axis=1))
        for h in range(NHEAD)])
    ident = np.eye(128, dtype=BF)
    return dict(wq=wq, wkv=wkv, ident=ident)


def assemble_l1_attn_out(results):
    out = np.empty((BATCH, SEQ, D), dtype=np.float32)
    for c in range(NCORES):
        b, s = divmod(c, 4)
        out[b, s * T_OWN:(s + 1) * T_OWN, :] = results[c]["y"].T
    return out



def build_l0_attn():
    nc = bass.Bass("TRN2", target_bir_lowering=False, debug=False)
    xf_h = nc.dram_tensor("xf", [D, SEQ], F32, kind="ExternalInput").ap()
    cos_h = nc.dram_tensor("costab", [32, SEQ], F32, kind="ExternalInput").ap()
    sin_h = nc.dram_tensor("sintab", [32, SEQ], F32, kind="ExternalInput").ap()
    wq_h = nc.dram_tensor("wq", [NHEAD, D, HD], F32, kind="ExternalInput").ap()
    wkv_h = nc.dram_tensor("wkv", [NHEAD, D, 128], F32, kind="ExternalInput").ap()
    qb_h = nc.dram_tensor("qb", [HD, NHEAD], F32, kind="ExternalInput").ap()
    kvb_h = nc.dram_tensor("kvb", [128, NHEAD], F32, kind="ExternalInput").ap()
    lnwb_h = nc.dram_tensor("lnwb", [HD, 2], F32, kind="ExternalInput").ap()
    qnw_h = nc.dram_tensor("qnw", [128, NCH], F32, kind="ExternalInput").ap()
    qnb_h = nc.dram_tensor("qnb", [128, NCH], F32, kind="ExternalInput").ap()
    kvnw_h = nc.dram_tensor("kvnw", [128, NCH], F32, kind="ExternalInput").ap()
    kvnb_h = nc.dram_tensor("kvnb", [128, NCH], F32, kind="ExternalInput").ap()
    rh_h = nc.dram_tensor("rh", [128, NCH], F32, kind="ExternalInput").ap()
    rhb_h = nc.dram_tensor("rhb", [1, 1], F32, kind="ExternalInput").ap()
    ow_h = nc.dram_tensor("ow", [D, D], F32, kind="ExternalInput").ap()
    ob_h = nc.dram_tensor("ob", [128, NCH], F32, kind="ExternalInput").ap()
    ident_h = nc.dram_tensor("ident", [128, 128], BF16, kind="ExternalInput").ap()
    y_h = nc.dram_tensor("y", [D, T_OWN], F32, kind="ExternalOutput").ap()
    osc_h = nc.dram_tensor("osc", [D, T_OWN], BF16, kind="Internal").ap()

    AW = 128

    es = ExitStack()
    sb = lambda n, s, d=F32: es.enter_context(nc.sbuf_tensor(n, s, d)).ap()
    psb = lambda n, s, d=F32: es.enter_context(nc.psum_tensor(n, s, d)).ap()
    with es:
        st_ = sb("st_", [128, NCH, SEQ], BF16)
        srad = sb("srad", [1, SEQ], BF16)
        skne = sb("skne", [65, SEQ], BF16)
        svtm = sb("svtm", [128, KCH, 65], BF16)
        sqne = sb("sqne", [65, 512], BF16)
        stabA = sb("stabA", [64, 512])
        stabB = sb("stabB", [64, 512])
        sswap = sb("sswap", [64, 512])
        swkv = sb("swkv", [128, NCH, 128], BF16)
        swq = sb("swq", [128, NCH, HD], BF16)
        swf = sb("swf", [128, 2, 128])
        sbias = sb("sbias", [128, 1])
        sqbias = sb("sqbias", [64, 1])
        skr = sb("skr", [64, 512])
        skrot = sb("skrot", [64, 512])
        svfm = sb("svfm", [64, 512], BF16)
        sto = svfm
        south = svfm
        sp = sb("sp", [128, 2, 1024], BF16)
        sxf = sb("sxf", [128, NCH, AW])
        ssq = sb("ssq", [128, 512])
        ssq2 = sb("ssq2", [128, 512])
        srow1 = sb("srow1", [1, 512])
        srow2 = sb("srow2", [1, 512])
        srow3 = sb("srow3", [1, 512])
        snrm = sb("snrm", [1, 512])
        skmax = sb("skmax", [1, 1])
        ssmall = sb("ssmall", [128, 16])
        sqb_s = sb("sqb_s", [64, 16])
        slnwb = sb("slnwb", [64, 2])
        slnq = sb("slnq", [64, 2])
        sqnw = sb("sqnw", [128, NCH])
        sqnb = sb("sqnb", [128, NCH])
        skvnw = sb("skvnw", [128, NCH])
        skvnb = sb("skvnb", [128, NCH])
        srh = sb("srh", [128, NCH])
        srhb = sb("srhb", [1, 1])
        srhb01 = sb("srhb01", [1, 1])
        sob = sb("sob", [128, NCH])
        sow = sb("sow", [128, 1, 128], BF16)
        sident = sb("sident", [64, 64], BF16)
        sones = sb("sones", [128, 1])
        sones_r = sb("sones_r", [1, 128])
        seps = sb("seps", [1, 1])
        sy = sb("sy", [128, AW])
        prow_m = psb("prow_m", [1, 512])
        prow_v = psb("prow_v", [1, 512])
        pkv = psb("pkv", [128, 512])
        ptr = psb("ptr", [128, 64], BF16)
        psc = psb("psc", [128, 2, 1024])

        p = Prog(nc, es)
        li = p.sem("li")
        lx = p.sem("lx")
        lt = p.sem("lt")
        lw = [p.sem("lw0"), p.sem("lw1")]
        t_c = p.sem("t_c")
        v_c = p.sem("v_c")
        s_c = p.sem("s_c")
        g_c = p.sem("g_c")
        st = p.sem("st")

        sres = sp.rearrange("p a b -> p (a b)").rearrange(
            "p (c t) -> p c t", c=NCH)[:, :, :AW]

        block = es.enter_context(nc.Block())

        for name, dst, srch in (("ident", sident, ident_h), ("kvb", ssmall, kvb_h),
                                ("qb", sqb_s, qb_h), ("lnwb", slnwb, lnwb_h),
                                ("qnw", sqnw, qnw_h), ("qnb", sqnb, qnb_h),
                                ("kvnw", skvnw, kvnw_h), ("kvnb", skvnb, kvnb_h),
                                ("rh", srh, rh_h), ("rhb", srhb, rhb_h),
                                ("ob", sob, ob_h)):
            p.emit("sync", lambda e, dst=dst, srch=srch: e.dma_start(
                out=dst, in_=srch[0:64, 0:64] if dst is sident else srch),
                   inc=li, inc_n=16)
        init_ld = li.v
        p.emit("vector", lambda e: e.memset(sones, 1.0))
        p.emit("vector", lambda e: e.memset(sones_r, 1.0))
        p.emit("vector", lambda e: e.memset(seps, EPS))
        p.emit("vector", lambda e: e.memset(skne[64:65, :], 1.0))
        p.emit("vector", lambda e: e.memset(svtm[:, :, 64], 1.0))
        p.emit("vector", lambda e: e.tensor_scalar_mul(out=srhb01, in0=srhb, scalar1=0.01),
               waits=((li, init_ld),))
        p.emit("vector", lambda e: e.tensor_scalar_mul(
            out=slnq[:, :], in0=slnwb[:, :], scalar1=0.125), inc=v_c)
        init_done = v_c.v

        p.emit("tensor", lambda e: e.matmul(prow_m[:, 0:NCH], sones, srh,
                                            start=True, stop=True),
               waits=((li, init_ld), (v_c, init_done)), inc=t_c)
        rhs_mm = t_c.v
        p.emit("vector", lambda e: e.reduce_sum(out=snrm[:, 0:1], in_=prow_m[:, 0:NCH],
                                                axis=AX.X), waits=((t_c, rhs_mm),))
        srhsum = sb("srhsum", [1, 1])
        p.emit("vector", lambda e: e.tensor_copy(out=srhsum, in_=snrm[:, 0:1]), inc=v_c)
        pre_done = v_c.v

        prev_t_end, prev_v_end, prev_s_end = t_c.v, pre_done, s_c.v
        for (s0, w) in strips(SEQ, AW):
            for i in range(NCH):
                p.emit("sync", lambda e, i=i, s0=s0, w=w: e.dma_start(
                    out=sxf[:, i, :w], in_=xf_h[i * 128:(i + 1) * 128, s0:s0 + w]),
                       waits=((v_c, prev_v_end), (t_c, prev_t_end), (s_c, prev_s_end))
                       if i == 0 else (),
                       inc=lx, inc_n=16)
            x_ld = lx.v
            for i in range(NCH):
                p.emit("tensor", lambda e, i=i, w=w: e.matmul(
                    prow_m[:, :w], sones, sxf[:, i, :w],
                    start=(i == 0), stop=(i == NCH - 1)),
                       waits=((lx, x_ld), (v_c, prev_v_end)) if i == 0 else (),
                       inc=t_c if i == NCH - 1 else None)
            m_done = t_c.v
            for i in range(NCH):
                p.emit("tensor", lambda e, i=i, w=w: e.matmul(
                    psc[0:1, 1, :w], srh[:, i:i + 1], sxf[:, i, :w],
                    start=(i == 0), stop=(i == NCH - 1)),
                       inc=t_c if i == NCH - 1 else None)
            r_done = t_c.v
            for i in range(NCH):
                p.emit("scalar", lambda e, i=i, w=w: e.activation(
                    out=ssq2[:, :w], in_=sxf[:, i, :w], func=AF.Square),
                       waits=((lx, x_ld), (t_c, prev_t_end), (v_c, prev_v_end))
                       if i == 0 else ((t_c, t_c.v),), inc=s_c)
                sq_i = s_c.v
                p.emit("tensor", lambda e, i=i, w=w: e.matmul(
                    prow_v[:, :w], sones, ssq2[:, :w],
                    start=(i == 0), stop=(i == NCH - 1)),
                       waits=((s_c, sq_i),), inc=t_c)
            v_done = t_c.v
            p.emit("vector", lambda e, w=w: e.tensor_scalar_mul(
                out=srow1[:, :w], in0=prow_m[:, :w], scalar1=1.0 / D),
                   waits=((t_c, m_done),))
            p.emit("vector", lambda e, w=w: e.tensor_scalar_mul(
                out=srow2[:, :w], in0=prow_v[:, :w], scalar1=1.0 / D),
                   waits=((t_c, v_done),))
            p.emit("vector", lambda e, w=w: e.tensor_mul(
                out=srow3[:, :w], in0=srow1[:, :w], in1=srow1[:, :w]))
            p.emit("vector", lambda e, w=w: e.tensor_sub(
                out=srow2[:, :w], in0=srow2[:, :w], in1=srow3[:, :w]), inc=v_c)
            var_done = v_c.v
            p.emit("scalar", lambda e, w=w: e.activation(
                out=srow3[:, :w], in_=srow2[:, :w], func=AF.Ln, bias=seps),
                   waits=((v_c, var_done),))
            p.emit("scalar", lambda e, w=w: e.activation(
                out=srow3[:, :w], in_=srow3[:, :w], func=AF.Exp, scale=-0.5), inc=s_c)
            rstd1_done = s_c.v
            p.emit("vector", lambda e, w=w: e.tensor_scalar_mul(
                out=snrm[:, :w], in0=srow1[:, :w], scalar1=srhsum),
                   waits=((s_c, rstd1_done), (t_c, r_done)))
            p.emit("vector", lambda e, w=w: e.tensor_sub(
                out=snrm[:, :w], in0=psc[0:1, 1, :w], in1=snrm[:, :w]))
            p.emit("vector", lambda e, w=w: e.tensor_mul(
                out=snrm[:, :w], in0=snrm[:, :w], in1=srow3[:, :w]))
            p.emit("vector", lambda e, w=w: e.tensor_scalar(
                out=snrm[:, :w], in0=snrm[:, :w], scalar1=0.01, scalar2=srhb01,
                op0=ALU.mult, op1=ALU.add))
            p.emit("vector", lambda e, w=w: e.tensor_scalar_max(
                out=snrm[:, :w], in0=snrm[:, :w], scalar1=0.5))
            p.emit("vector", lambda e, w=w: e.tensor_scalar_min(
                out=snrm[:, :w], in0=snrm[:, :w], scalar1=2.0))
            p.emit("vector", lambda e, s0=s0, w=w: e.tensor_copy(
                out=srad[0:1, s0:s0 + w], in_=snrm[:, :w]), inc=v_c)
            p.emit("vector", lambda e, w=w: e.tensor_mul(
                out=snrm[:, :w], in0=srow2[:, :w], in1=srow3[:, :w]))
            p.emit("vector", lambda e, w=w: e.tensor_mul(
                out=snrm[:, :w], in0=snrm[:, :w], in1=srow3[:, :w]), inc=v_c)
            var2_done = v_c.v
            p.emit("scalar", lambda e, w=w: e.activation(
                out=snrm[:, :w], in_=snrm[:, :w], func=AF.Ln, bias=seps),
                   waits=((v_c, var2_done),))
            p.emit("scalar", lambda e, w=w: e.activation(
                out=snrm[:, :w], in_=snrm[:, :w], func=AF.Exp, scale=-0.5), inc=s_c)
            rstd2_done = s_c.v
            p.emit("vector", lambda e, w=w: e.tensor_mul(
                out=srow3[:, :w], in0=srow3[:, :w], in1=snrm[:, :w]),
                   waits=((s_c, rstd2_done),))
            p.emit("vector", lambda e, w=w: e.tensor_mul(
                out=srow1[:, :w], in0=srow1[:, :w], in1=srow3[:, :w]))
            p.emit("vector", lambda e, w=w: e.tensor_scalar_mul(
                out=srow1[:, :w], in0=srow1[:, :w], scalar1=-1.0), inc=v_c)
            rows_done = v_c.v
            p.emit("tensor", lambda e, w=w: e.matmul(
                psc[:, 0, 0:512][:, :w], sones_r, srow3[:, :w], start=True, stop=True),
                   waits=((v_c, rows_done),))
            p.emit("tensor", lambda e, w=w: e.matmul(
                psc[:, 0, 512:1024][:, :w], sones_r, srow1[:, :w], start=True, stop=True),
                   inc=t_c)
            bc_done = t_c.v
            for i in range(NCH):
                p.emit("vector", lambda e, i=i, w=w: e.tensor_mul(
                    out=ssq[:, :w], in0=sxf[:, i, :w], in1=psc[:, 0, 0:512][:, :w]),
                       waits=((t_c, bc_done),) if i == 0 else ())
                p.emit("vector", lambda e, i=i, s0=s0, w=w: e.tensor_add(
                    out=st_[:, i, s0:s0 + w], in0=ssq[:, :w],
                    in1=psc[:, 0, 512:1024][:, :w]), inc=v_c)
            prev_t_end, prev_v_end, prev_s_end = t_c.v, v_c.v, s_c.v
        a_done_v, a_done_t, a_done_s = v_c.v, t_c.v, s_c.v

        sones_rb = sb("sones_rb", [1, 128], BF16)
        p.emit("vector", lambda e: e.memset(sones_rb, 1.0), inc=v_c)

        for h in range(NHEAD):
            p.emit("vector", lambda e: e.memset(skmax, 0.0), inc=v_c)
            swf_free = [v_c.v, v_c.v]
            swf_free_t = [t_c.v, t_c.v]
            for i in range(NCH):
                ib = i % 2
                p.emit("sync", lambda e, h=h, i=i, ib=ib: e.dma_start(
                    out=swf[:, ib, :], in_=wkv_h[h, i * 128:(i + 1) * 128, :]),
                       waits=((v_c, swf_free[ib]), (t_c, swf_free_t[ib])),
                       inc=lw[ib], inc_n=16)
                wld = lw[ib].v
                p.emit("tensor", lambda e, i=i, ib=ib: e.matmul(
                    pkv[:, 0:1], swf[:, ib, :], skvnb[:, i:i + 1],
                    start=(i == 0), stop=(i == NCH - 1)),
                       waits=((lw[ib], wld), (li, init_ld)), inc=t_c)
                swf_free_t[ib] = t_c.v
                p.emit("vector", lambda e, i=i, ib=ib: e.tensor_scalar_mul(
                    out=swkv[:, i, :], in0=swf[:, ib, :], scalar1=skvnw[:, i:i + 1]),
                       waits=((lw[ib], wld),), inc=v_c)
                swf_free[ib] = v_c.v
            kvb_mm = t_c.v
            p.emit("vector", lambda e, h=h: e.tensor_add(
                out=sbias, in0=pkv[:, 0:1], in1=ssmall[:, h:h + 1]),
                   waits=((t_c, kvb_mm),), inc=v_c)
            kvprep = v_c.v
            for i in range(NCH):
                ib = i % 2
                p.emit("sync", lambda e, h=h, i=i, ib=ib: e.dma_start(
                    out=swf[:, ib, 0:HD], in_=wq_h[h, i * 128:(i + 1) * 128, :]),
                       waits=((v_c, max(swf_free[ib], kvprep)), (t_c, swf_free_t[ib])),
                       inc=lw[ib], inc_n=16)
                wld = lw[ib].v
                p.emit("tensor", lambda e, i=i, ib=ib: e.matmul(
                    pkv[0:64, 0:1], swf[:, ib, 0:HD], sqnb[:, i:i + 1],
                    start=(i == 0), stop=(i == NCH - 1)),
                       waits=((lw[ib], wld), (v_c, kvprep)), inc=t_c)
                swf_free_t[ib] = t_c.v
                p.emit("vector", lambda e, i=i, ib=ib: e.tensor_scalar_mul(
                    out=swq[:, i, :], in0=swf[:, ib, 0:HD], scalar1=sqnw[:, i:i + 1]),
                       waits=((lw[ib], wld),), inc=v_c)
                swf_free[ib] = v_c.v
            qb_mm = t_c.v
            p.emit("vector", lambda e, h=h: e.tensor_add(
                out=sqbias, in0=pkv[0:64, 0:1], in1=sqb_s[:, h:h + 1]),
                   waits=((t_c, qb_mm),), inc=v_c)
            qprep = v_c.v

            def head_vec(side, s0, w, out_slice, lnw_pair, bias_ap):
                kv_mm = t_c.v
                p.emit("scalar", lambda e, w=w, bias_ap=bias_ap: e.activation(
                    out=skr[:, :w], in_=pkv[0:64, :w], func=AF.Identity, bias=bias_ap),
                       waits=((t_c, kv_mm),), inc=s_c)
                kr_evac = s_c.v
                tab_v_guard = v_c.v
                p.emit("sync", lambda e, s0=s0, w=w: e.dma_start(
                    out=stabA[0:32, :w], in_=cos_h[:, s0:s0 + w]),
                       waits=((v_c, tab_v_guard),), inc=lt, inc_n=16)
                p.emit("sync", lambda e, s0=s0, w=w: e.dma_start(
                    out=stabA[32:64, :w], in_=sin_h[:, s0:s0 + w]), inc=lt, inc_n=16)
                p.emit("sync", lambda e, s0=s0, w=w: e.dma_start(
                    out=stabB[0:32, :w], in_=sin_h[:, s0:s0 + w]), inc=lt, inc_n=16)
                p.emit("sync", lambda e, s0=s0, w=w: e.dma_start(
                    out=stabB[32:64, :w], in_=cos_h[:, s0:s0 + w]), inc=lt, inc_n=16)
                p.emit("sync", lambda e, w=w: e.dma_start(
                    out=sswap[0:32, :w], in_=skr[32:64, :w]),
                       waits=((s_c, kr_evac),), inc=lt, inc_n=16)
                p.emit("sync", lambda e, w=w: e.dma_start(
                    out=sswap[32:64, :w], in_=skr[0:32, :w]), inc=lt, inc_n=16)
                tab_ld = lt.v
                p.emit("tensor", lambda e, s0=s0, w=w: e.matmul(
                    psc[0:64, 0, 0:512][:, :w], sones_rb[:, 0:64],
                    srad[0:1, s0:s0 + w], start=True, stop=True), inc=t_c)
                rad_bc = t_c.v
                p.emit("vector", lambda e, w=w: e.tensor_mul(
                    out=ssq[0:32, :w], in0=skr[0:32, :w], in1=stabA[0:32, :w]),
                       waits=((s_c, kr_evac), (lt, tab_ld)))
                p.emit("vector", lambda e, w=w: e.tensor_mul(
                    out=ssq2[0:32, :w], in0=sswap[0:32, :w], in1=stabB[0:32, :w]))
                p.emit("vector", lambda e, w=w: e.tensor_sub(
                    out=skrot[0:32, :w], in0=ssq[0:32, :w], in1=ssq2[0:32, :w]))
                p.emit("vector", lambda e, w=w: e.tensor_mul(
                    out=ssq[32:64, :w], in0=sswap[32:64, :w], in1=stabA[32:64, :w]))
                p.emit("vector", lambda e, w=w: e.tensor_mul(
                    out=ssq2[32:64, :w], in0=skr[32:64, :w], in1=stabB[32:64, :w]))
                p.emit("vector", lambda e, w=w: e.tensor_add(
                    out=skrot[32:64, :w], in0=ssq[32:64, :w], in1=ssq2[32:64, :w]))
                p.emit("vector", lambda e, w=w: e.tensor_mul(
                    out=skrot[:, :w], in0=skrot[:, :w], in1=psc[0:64, 0, 0:512][:, :w]),
                       waits=((t_c, rad_bc),), inc=v_c)
                rot_done = v_c.v
                p.emit("tensor", lambda e, w=w: e.matmul(
                    prow_m[:, :w], sones[0:64], skrot[:, :w], start=True, stop=True),
                       waits=((v_c, rot_done),), inc=t_c)
                hm_done = t_c.v
                p.emit("scalar", lambda e, w=w: e.activation(
                    out=ssq2[0:64, :w], in_=skrot[:, :w], func=AF.Square),
                       waits=((v_c, rot_done),), inc=s_c)
                hsq_done = s_c.v
                p.emit("tensor", lambda e, w=w: e.matmul(
                    prow_v[:, :w], sones[0:64], ssq2[0:64, :w], start=True, stop=True),
                       waits=((s_c, hsq_done),), inc=t_c)
                hv_done = t_c.v
                p.emit("vector", lambda e, w=w: e.tensor_scalar_mul(
                    out=srow1[:, :w], in0=prow_m[:, :w], scalar1=1.0 / HD),
                       waits=((t_c, hm_done),))
                p.emit("vector", lambda e, w=w: e.tensor_scalar_mul(
                    out=srow2[:, :w], in0=prow_v[:, :w], scalar1=1.0 / HD),
                       waits=((t_c, hv_done),))
                p.emit("vector", lambda e, w=w: e.tensor_mul(
                    out=srow3[:, :w], in0=srow1[:, :w], in1=srow1[:, :w]))
                p.emit("vector", lambda e, w=w: e.tensor_sub(
                    out=srow2[:, :w], in0=srow2[:, :w], in1=srow3[:, :w]), inc=v_c)
                hvar_done = v_c.v
                p.emit("scalar", lambda e, w=w: e.activation(
                    out=srow2[:, :w], in_=srow2[:, :w], func=AF.Ln, bias=seps),
                       waits=((v_c, hvar_done),))
                p.emit("scalar", lambda e, w=w: e.activation(
                    out=srow2[:, :w], in_=srow2[:, :w], func=AF.Exp, scale=-0.5),
                       inc=s_c)
                hr_done = s_c.v
                p.emit("vector", lambda e, w=w: e.tensor_mul(
                    out=srow1[:, :w], in0=srow1[:, :w], in1=srow2[:, :w]),
                       waits=((s_c, hr_done),))
                p.emit("vector", lambda e, w=w: e.tensor_scalar_mul(
                    out=srow1[:, :w], in0=srow1[:, :w], scalar1=-1.0), inc=v_c)
                hrows_done = v_c.v
                p.emit("tensor", lambda e, w=w: e.matmul(
                    psc[0:64, 0, 512:1024][:, :w], sones_r[:, 0:64], srow2[:, :w],
                    start=True, stop=True), waits=((v_c, hrows_done),))
                p.emit("tensor", lambda e, w=w: e.matmul(
                    psc[0:64, 1, 0:512][:, :w], sones_r[:, 0:64], srow1[:, :w],
                    start=True, stop=True), inc=t_c)
                hbc_done = t_c.v
                p.emit("vector", lambda e, w=w: e.tensor_mul(
                    out=ssq[0:64, :w], in0=skrot[:, :w],
                    in1=psc[0:64, 0, 512:1024][:, :w]),
                       waits=((t_c, hbc_done),))
                p.emit("vector", lambda e, w=w: e.tensor_add(
                    out=ssq[0:64, :w], in0=ssq[0:64, :w],
                    in1=psc[0:64, 1, 0:512][:, :w]))
                p.emit("vector", lambda e, out_slice=out_slice, w=w,
                       lnw_pair=lnw_pair: e.tensor_scalar(
                    out=out_slice, in0=ssq[0:64, :w],
                    scalar1=lnw_pair[:, 0:1], scalar2=lnw_pair[:, 1:2],
                    op0=ALU.mult, op1=ALU.add), inc=v_c)
                return v_c.v

            for (s0, w) in strips(SEQ, 512):
                strip_barrier(p, t_c, v_c, s_c)
                for i in range(NCH):
                    p.emit("tensor", lambda e, i=i, s0=s0, w=w: e.matmul(
                        pkv[:, :w], swkv[:, i, :], st_[:, i, s0:s0 + w],
                        start=(i == 0), stop=(i == NCH - 1)),
                           waits=((v_c, max(kvprep, qprep, a_done_v)),)
                           if i == 0 else (),
                           inc=t_c if i == NCH - 1 else None)
                kv_mm = t_c.v
                p.emit("scalar", lambda e, w=w: e.activation(
                    out=svfm[:, :w], in_=pkv[64:128, :w], func=AF.Identity,
                    bias=sbias[64:128]), waits=((t_c, kv_mm),), inc=s_c)
                vfm_evac = s_c.v
                kn_done = head_vec("k", s0, w, skne[0:64, s0:s0 + w], slnwb,
                                   sbias[0:64])
                p.emit("scalar", lambda e, s0=s0, w=w: e.activation(
                    out=ssq2[0:64, :w], in_=skne[0:64, s0:s0 + w], func=AF.Square),
                       waits=((v_c, kn_done),), inc=s_c)
                knsq_done = s_c.v
                p.emit("tensor", lambda e, w=w: e.matmul(
                    prow_v[:, :w], sones[0:64], ssq2[0:64, :w], start=True, stop=True),
                       waits=((s_c, knsq_done),), inc=t_c)
                knn_mm = t_c.v
                p.emit("vector", lambda e, w=w: e.reduce_max(
                    out=snrm[:, 0:1], in_=prow_v[:, :w], axis=AX.X),
                       waits=((t_c, knn_mm),))
                p.emit("vector", lambda e: e.tensor_max(
                    out=skmax, in0=skmax, in1=snrm[:, 0:1]), inc=v_c)
                for j in range(w // 128):
                    kchunk = s0 // 128 + j
                    p.emit("tensor", lambda e, j=j: e.transpose(
                        ptr[:, :], svfm[:, j * 128:(j + 1) * 128],
                        sident),
                           waits=((s_c, vfm_evac),) if j == 0 else (), inc=t_c)
                    tr_done = t_c.v
                    p.emit("vector", lambda e, kchunk=kchunk: e.tensor_copy(
                        out=svtm[:, kchunk, 0:64], in_=ptr[:, :]),
                           waits=((t_c, tr_done),), inc=v_c)
                    p.wait("tensor", v_c, v_c.v)
                p.wait("scalar", t_c, t_c.v)
            kn_done_s = s_c.v
            p.emit("scalar", lambda e: e.activation(out=skmax, in_=skmax, func=AF.Ln),
                   waits=((v_c, v_c.v),))
            p.emit("scalar", lambda e: e.activation(out=skmax, in_=skmax, func=AF.Exp,
                                                    scale=0.5), inc=s_c)
            kmax_done = s_c.v

            for qs in range(QSTRIPS):
                q0 = qs * 512
                w = 512
                strip_barrier(p, t_c, v_c, s_c)
                for i in range(NCH):
                    p.emit("tensor", lambda e, i=i, q0=q0, w=w: e.matmul(
                        pkv[0:64, :w], swq[:, i, :], st_[:, i, q0:q0 + w],
                        start=(i == 0), stop=(i == NCH - 1)),
                           inc=t_c if i == NCH - 1 else None)
                q_done = head_vec("q", q0, w, sqne[0:64, 0:w], slnq, sqbias)
                p.emit("scalar", lambda e, w=w: e.activation(
                    out=ssq2[0:64, :w], in_=sqne[0:64, 0:w], func=AF.Square),
                       waits=((v_c, q_done),), inc=s_c)
                qsq_done = s_c.v
                p.emit("tensor", lambda e, w=w: e.matmul(
                    prow_v[:, :w], sones[0:64], ssq2[0:64, :w], start=True, stop=True),
                       waits=((s_c, qsq_done),), inc=t_c)
                qn_mm = t_c.v
                p.emit("scalar", lambda e, w=w: e.activation(
                    out=snrm[:, :w], in_=prow_v[:, :w], func=AF.Ln),
                       waits=((t_c, qn_mm),))
                p.emit("scalar", lambda e, w=w: e.activation(
                    out=snrm[:, :w], in_=snrm[:, :w], func=AF.Exp, scale=0.5),
                       inc=s_c)
                qrow_done = s_c.v
                p.wait("tensor", s_c, qrow_done)
                p.emit("vector", lambda e, w=w: e.tensor_scalar(
                    out=sqne[64:65, 0:w], in0=snrm[:, :w], scalar1=skmax,
                    scalar2=-1.0, op0=ALU.mult, op1=ALU.mult),
                       waits=((s_c, max(qrow_done, kmax_done)),))
                p.emit("vector", lambda e, w=w: e.tensor_scalar_add(
                    out=sqne[64:65, 0:w], in0=sqne[64:65, 0:w],
                    scalar1=-0.05), inc=v_c)
                p.wait("scalar", v_c, v_c.v)
                bias_done = v_c.v
                emit_attn_b5(p, h=h, skne=skne, sqne=sqne, svtm=svtm, psc=psc,
                             sp=sp, pkv=pkv, snrm=snrm, sto=sto, south=south,
                             sones_r=sones_r, osc_h=osc_h,
                             t_c=t_c, v_c=v_c, s_c=s_c, g_c=g_c,
                             first_waits=((v_c, bias_done), (s_c, kn_done_s)),
                             qs_range=[qs], qne_col=lambda qs: 0)
        attn_out_dma = g_c.v

        for (s0, w) in strips(T_OWN, AW):
            v_snap = v_c.v
            t_snap = t_c.v
            for i in range(NCH):
                p.emit("sync", lambda e, i=i, s0=s0, w=w: e.dma_start(
                    out=sxf[:, i, :w], in_=xf_h[i * 128:(i + 1) * 128, s0:s0 + w]),
                       waits=((v_c, v_snap),) if i == 0 else (), inc=lx, inc_n=16)
            p.emit("sync", lambda e, s0=s0, w=w: e.dma_start(
                out=sres[:, :, :w],
                in_=osc_h.rearrange("(c p) t -> p c t", p=128)[:, :, s0:s0 + w]),
                   waits=((g_c, attn_out_dma),), inc=lx, inc_n=16)
            c_ld = lx.v
            ow_free = [v_c.v, v_c.v]
            ow_free_t = [t_c.v, t_c.v]
            for o in range(NCH):
                for i in range(NCH):
                    ib = i % 2
                    p.emit("sync", lambda e, i=i, o=o, ib=ib: e.dma_start(
                        out=swf[:, ib, :], in_=ow_h[i * 128:(i + 1) * 128,
                                                    o * 128:(o + 1) * 128]),
                           waits=((v_c, ow_free[ib]), (t_c, ow_free_t[ib])),
                           inc=lw[ib], inc_n=16)
                    owld = lw[ib].v
                    p.emit("vector", lambda e, ib=ib: e.tensor_copy(
                        out=sow[:, 0, :], in_=swf[:, ib, :]),
                           waits=((lw[ib], owld), (t_c, ow_free_t[1 - ib])), inc=v_c)
                    cast_done = v_c.v
                    ow_free[ib] = cast_done
                    p.emit("tensor", lambda e, i=i, w=w: e.matmul(
                        psc[:, 0, 0:512][:, :w], sow[:, 0, :], sres[:, i, :w],
                        start=(i == 0), stop=(i == NCH - 1)),
                           waits=((v_c, cast_done),) + (((lx, c_ld), (s_c, s_c.v))
                                                        if i == 0 else ()),
                           inc=t_c)
                    ow_free_t[ib] = t_c.v
                o_mm = t_c.v
                p.emit("scalar", lambda e, o=o, w=w: e.activation(
                    out=ssq[:, :w], in_=psc[:, 0, 0:512][:, :w], func=AF.Identity,
                    bias=sob[:, o:o + 1]), waits=((t_c, o_mm),), inc=s_c)
                ob_done = s_c.v
                p.emit("vector", lambda e, o=o, w=w: e.tensor_add(
                    out=sy[:, :w], in0=ssq[:, :w], in1=sxf[:, o, :w]),
                       waits=((s_c, ob_done), (lx, c_ld)), inc=v_c)
                y_done = v_c.v
                p.wait("tensor", v_c, y_done)
                p.wait("scalar", v_c, y_done)
                p.emit("gpsimd", lambda e, o=o, s0=s0, w=w: e.dma_start(
                    out=y_h[o * 128:(o + 1) * 128, s0:s0 + w], in_=sy[:, :w]),
                       waits=((v_c, y_done),), inc=st, inc_n=16)
                p.wait("vector", st, st.v)
        p.wait("gpsimd", st, st.v)
        p.finalize(block)
    return nc



def _mel_tables():
    d2 = 32
    mel_max = 2595.0 * np.log10(1.0 + 4000.0 / 200.0)
    scale = np.power(10.0, np.linspace(0.0, mel_max, d2, dtype=np.float64) / 2595.0) - 1.0
    fb = (200.0 * scale / 1000.0).astype(np.float64)
    freqs = np.arange(SEQ, dtype=np.float64)[:, None] * fb[None, :]
    cos = np.cos(np.float32(freqs).astype(np.float64))
    sin = np.sin(np.float32(freqs).astype(np.float64))
    f32 = np.float32(freqs)
    cos = np.cos(f32.astype(np.float64)).astype(np.float32)
    sin = np.sin(f32.astype(np.float64)).astype(np.float32)
    return np.ascontiguousarray(cos.T), np.ascontiguousarray(sin.T)


_PERM = np.concatenate([np.arange(0, HD, 2), np.arange(1, HD, 2)])


def l0_attn_weight_maps(ins):
    q_w = np.asarray(ins["L0_attn_q_w"], np.float32)
    kv_w = np.asarray(ins["L0_attn_kv_w"], np.float32)
    q_b = np.asarray(ins["L0_attn_q_b"], np.float32)
    kv_b = np.asarray(ins["L0_attn_kv_b"], np.float32)
    wq = np.stack([np.ascontiguousarray(q_w[:, h * HD:(h + 1) * HD][:, _PERM])
                   for h in range(NHEAD)])
    wkv = np.stack([np.ascontiguousarray(np.concatenate(
        [kv_w[:, h * HD:(h + 1) * HD][:, _PERM],
         kv_w[:, D + h * HD:D + (h + 1) * HD]], axis=1)) for h in range(NHEAD)])
    qb = np.stack([q_b[h * HD:(h + 1) * HD][_PERM] for h in range(NHEAD)], axis=1)
    kvb = np.stack([np.concatenate([kv_b[h * HD:(h + 1) * HD][_PERM],
                                    kv_b[D + h * HD:D + (h + 1) * HD]])
                    for h in range(NHEAD)], axis=1)
    lnwb = np.stack([np.asarray(ins["L0_attn_ln_w"], np.float32)[_PERM],
                     np.asarray(ins["L0_attn_ln_b"], np.float32)[_PERM]], axis=1)
    chunked = lambda v: np.ascontiguousarray(
        np.asarray(v, np.float32).reshape(NCH, 128).T)
    return dict(
        wq=wq, wkv=wkv, qb=np.ascontiguousarray(qb),
        kvb=np.ascontiguousarray(kvb), lnwb=np.ascontiguousarray(lnwb),
        qnw=chunked(ins["L0_attn_qn_w"]), qnb=chunked(ins["L0_attn_qn_b"]),
        kvnw=chunked(ins["L0_attn_kvn_w"]), kvnb=chunked(ins["L0_attn_kvn_b"]),
        rh=chunked(ins["L0_attn_rh_w"][:, 0]),
        rhb=np.asarray(ins["L0_attn_rh_b"], np.float32).reshape(1, 1),
        ow=np.ascontiguousarray(np.asarray(ins["L0_attn_out_w"], np.float32)),
        ob=chunked(ins["L0_attn_out_b"]),
        ident=np.eye(128, dtype=BF))


def l0_attn_in_maps(x):
    cos_t, sin_t = _mel_tables()
    maps = []
    for c in range(NCORES):
        b, s = divmod(c, 4)
        r = s * T_OWN
        xf = np.ascontiguousarray(np.roll(x[b], -r, axis=0).T)
        maps.append({"xf": xf,
                     "costab": np.ascontiguousarray(np.roll(cos_t, -r, axis=1)),
                     "sintab": np.ascontiguousarray(np.roll(sin_t, -r, axis=1))})
    return maps



T_IN = 2560
Q0 = 512
NTILE = 16
KV_LO = 128
KV_W = T_IN - KV_LO


def build_l0f_attn(debug=False):
    nc = bass.Bass("TRN2", target_bir_lowering=False, debug=False)
    x_h = nc.dram_tensor("x", [D, T_IN], F32, kind="ExternalInput").ap()
    hm_h = nc.dram_tensor("hm", [1, T_IN], BF16, kind="ExternalInput").ap()
    wq_h = nc.dram_tensor("wqkvh", [NHEAD, D, 192], F32, kind="ExternalInput").ap()
    qsks_h = nc.dram_tensor("qsks", [HD, 2], F32, kind="ExternalInput").ap()
    wo_h = nc.dram_tensor("wo", [D, D], F32, kind="ExternalInput").ap()
    mtri_h = nc.dram_tensor("mtri", [128, 384], BF16, kind="ExternalInput").ap()
    mtri2_h = nc.dram_tensor("mtri2", [128, 384], BF16, kind="ExternalInput").ap()
    tq_a = nc.dram_tensor("tqA", [HD, T_IN], F32, kind="ExternalInput").ap()
    tq_b = nc.dram_tensor("tqB", [HD, T_IN], F32, kind="ExternalInput").ap()
    tk_a = nc.dram_tensor("tkA", [HD, T_IN], F32, kind="ExternalInput").ap()
    tk_b = nc.dram_tensor("tkB", [HD, T_IN], F32, kind="ExternalInput").ap()
    ident_h = nc.dram_tensor("ident", [128, 128], BF16, kind="ExternalInput").ap()
    y_h = nc.dram_tensor("y", [D, T_OWN], F32, kind="ExternalOutput").ap()
    if debug:
        dkne = nc.dram_tensor("dkne", [66, KV_W], BF16, kind="ExternalOutput").ap()
        dq2 = nc.dram_tensor("dq2", [66, T_OWN], BF16, kind="ExternalOutput").ap()
        dvtm = nc.dram_tensor("dvtm", [128, 19 * 65], BF16, kind="ExternalOutput").ap()
        dassem = nc.dram_tensor("dassem", [D, T_OWN], BF16, kind="ExternalOutput").ap()

    es = ExitStack()
    sb = lambda n, s, d=F32: es.enter_context(nc.sbuf_tensor(n, s, d)).ap()
    psb = lambda n, s, d=F32: es.enter_context(nc.psum_tensor(n, s, d)).ap()
    with es:
        xb = sb("xb", [128, NCH, T_IN], BF16)
        assem = sb("assem", [128, NCH, T_OWN], BF16)
        skne2 = sb("skne2", [66, KV_W], BF16)
        svtm2 = sb("svtm2", [128, 19, 65], BF16)
        sq2 = sb("sq2", [66, T_OWN], BF16)
        swh = sb("swh", [128, 2, NCH, 192], BF16)
        swf = sb("swf", [128, 2, 192])
        swo = sb("swo", [128, NCH, D], BF16)
        sqsks = sb("sqsks", [HD, 2])
        sqs8 = sb("sqs8", [HD, 1])
        skr = sb("skr", [64, KV_W])
        sswap = sb("sswap", [64, KV_W])
        svfm = sb("svfm", [64, 512], BF16)
        stabA = sb("stabA", [64, KV_W])
        stabB = sb("stabB", [64, KV_W])
        srowf = sb("srowf", [1, KV_W])
        sout_h = sb("sout_h", [128, T_OWN], BF16)
        sden = sb("sden", [1, T_OWN])
        ssq = sb("ssq", [128, 512])
        ssq2 = sb("ssq2", [128, 512])
        snrm = sb("snrm", [1, 512])
        smaxneg = sb("smaxneg", [128, NTILE], BF16)
        smaxT = sb("smaxT", [NTILE, 128], BF16)
        sp = sb("sp", [128, 2, 384], BF16)
        sto = sb("sto", [128, 128], BF16)
        smtri = sb("smtri", [128, 384], BF16)
        smtri2 = sb("smtri2", [128, 384], BF16)
        sxf = sb("sxf", [128, NCH, 512])
        sy = sb("sy", [128, 512])
        sident = sb("sident", [128, 128], BF16)
        sones_r = sb("sones_r", [1, 128])
        sones = sb("sones", [128, 1])
        seps = sb("seps", [1, 1])
        pkv = psb("pkv", [128, 512])
        pbc = psb("pbc", [64, 512])
        psc1 = psb("psc1", [128, 2, 512])
        psc2 = psb("psc2", [128, 2, 384])
        ppv = psb("ppv", [65, 128])
        pdbc = psb("pdbc", [128, 512])

        p = Prog(nc, es)
        li = p.sem("li")
        lx = p.sem("lx")
        lt = p.sem("lt")
        lw = [p.sem("lw0"), p.sem("lw1")]
        t_c = p.sem("t_c")
        v_c = p.sem("v_c")
        s_c = p.sem("s_c")
        g_c = p.sem("g_c")
        st = p.sem("st")

        block = es.enter_context(nc.Block())

        p.emit("sync", lambda e: e.dma_start(out=sident, in_=ident_h), inc=li, inc_n=16)
        p.emit("sync", lambda e: e.dma_start(out=smtri, in_=mtri_h), inc=li, inc_n=16)
        p.emit("sync", lambda e: e.dma_start(out=smtri2, in_=mtri2_h), inc=li, inc_n=16)
        p.emit("sync", lambda e: e.dma_start(out=sqsks, in_=qsks_h), inc=li, inc_n=16)
        p.emit("sync", lambda e: e.dma_start(
            out=skne2[65:66, :], in_=hm_h[:, KV_LO:]),
               waits=((v_c, 1),), inc=li, inc_n=16)
        init_ld = li.v
        p.emit("vector", lambda e: e.memset(sones_r, 1.0))
        p.emit("vector", lambda e: e.memset(sones, 1.0))
        p.emit("vector", lambda e: e.memset(seps, EPS))
        p.emit("vector", lambda e: e.memset(skne2[64:66, :], 1.0), inc=v_c)
        p.emit("vector", lambda e: e.memset(sq2[64:66, :], 1.0))
        p.emit("vector", lambda e: e.memset(svtm2[:, :, 64], 1.0))
        p.emit("vector", lambda e: e.tensor_scalar_mul(
            out=sqs8, in0=sqsks[:, 0:1], scalar1=8.0),
               waits=((li, init_ld),), inc=v_c)
        init_done = v_c.v
        wo_free = [v_c.v, v_c.v]
        for i in range(NCH):
            for oc in range(NCH):
                ib = (i * NCH + oc) % 2
                p.emit("sync", lambda e, i=i, oc=oc, ib=ib: e.dma_start(
                    out=swf[:, ib, 0:128],
                    in_=wo_h[i * 128:(i + 1) * 128, oc * 128:(oc + 1) * 128]),
                       waits=((v_c, wo_free[ib]),), inc=lw[ib], inc_n=16)
                wld = lw[ib].v
                p.emit("vector", lambda e, i=i, oc=oc, ib=ib: e.tensor_copy(
                    out=swo[:, i, oc * 128:(oc + 1) * 128], in_=swf[:, ib, 0:128]),
                       waits=((lw[ib], wld),), inc=v_c)
                wo_free[ib] = v_c.v
        wo_done = v_c.v

        for (s0, w) in strips(T_IN, 512):
            v_snap = v_c.v
            for i in range(NCH):
                p.emit("sync", lambda e, i=i, s0=s0, w=w: e.dma_start(
                    out=sxf[:, i, :w], in_=x_h[i * 128:(i + 1) * 128, s0:s0 + w]),
                       waits=((v_c, v_snap),) if i == 0 else (), inc=lx, inc_n=16)
            x_ld = lx.v
            for i in range(NCH):
                p.emit("vector", lambda e, i=i, s0=s0, w=w: e.tensor_copy(
                    out=xb[:, i, s0:s0 + w], in_=sxf[:, i, :w]),
                       waits=((lx, x_ld),) if i == 0 else (), inc=v_c)
        xb_done = v_c.v

        def shifted_rhs(i, c0, w):
            if i < 4:
                return xb[:, i, c0:c0 + w]
            return xb[:, i, c0 - 1:c0 - 1 + w]

        def l2rot_full(lo, width, tabA_h, tabB_h, scal_ap, out_slice,
                       proj_cols, prev_done):
            p.emit("sync", lambda e, width=width, tabA_h=tabA_h, lo=lo: e.dma_start(
                out=stabA[:, :width], in_=tabA_h[:, lo:lo + width]),
                   waits=((v_c, prev_done),), inc=lt, inc_n=16)
            tab_ld = lt.v
            prev_sq = s_c.v
            for (s0, w) in strips(width, 512, start=lo):
                r0 = s0 - lo
                for i in range(NCH):
                    p.emit("tensor", lambda e, i=i, s0=s0, w=w, hb=h % 2,
                           proj_cols=proj_cols: e.matmul(
                        pkv[0:64, :w], swh[:, hb, i, proj_cols],
                        shifted_rhs(i, s0, w),
                        start=(i == 0), stop=(i == NCH - 1)),
                           waits=((s_c, prev_sq), (v_c, prev_done))
                           if i == 0 else (),
                           inc=t_c if i == NCH - 1 else None)
                mm = t_c.v
                p.emit("scalar", lambda e, r0=r0, w=w: e.activation(
                    out=skr[:, r0:r0 + w], in_=pkv[0:64, :w], func=AF.Copy),
                       waits=((t_c, mm),), inc=s_c)
                prev_sq = s_c.v
            proj_done = s_c.v
            p.emit("scalar", lambda e, width=width: e.activation(
                out=sswap[:, :width], in_=skr[:, :width], func=AF.Square),
                   inc=s_c)
            sq_done = s_c.v
            for (r0, w) in strips(width, 512):
                p.emit("tensor", lambda e, r0=r0, w=w: e.matmul(
                    pbc[0:1, :w], sones[0:64], sswap[:, r0:r0 + w],
                    start=True, stop=True),
                       waits=((s_c, sq_done),) if r0 == 0 else (), inc=t_c)
                nm = t_c.v
                p.emit("vector", lambda e, r0=r0, w=w: e.tensor_copy(
                    out=srowf[:, r0:r0 + w], in_=pbc[0:1, :w]),
                       waits=((t_c, nm),), inc=v_c)
                p.wait("tensor", v_c, v_c.v)
            ssq_rows = v_c.v
            p.emit("vector", lambda e, width=width: e.tensor_scalar_max(
                out=srowf[:, :width], in0=srowf[:, :width], scalar1=1e-24),
                   waits=((v_c, ssq_rows),), inc=v_c)
            clamp_done = v_c.v
            p.emit("scalar", lambda e, width=width: e.activation(
                out=srowf[:, :width], in_=srowf[:, :width], func=AF.Ln),
                   waits=((v_c, clamp_done),))
            p.emit("scalar", lambda e, width=width: e.activation(
                out=srowf[:, :width], in_=srowf[:, :width], func=AF.Exp,
                scale=-0.5), inc=s_c)
            rinv_done = s_c.v
            for (r0, w) in strips(width, 512):
                p.emit("tensor", lambda e, r0=r0, w=w: e.matmul(
                    pbc[:, :w], sones_r[:, 0:64], srowf[:, r0:r0 + w],
                    start=True, stop=True),
                       waits=((s_c, rinv_done),) if r0 == 0 else (), inc=t_c)
                bc = t_c.v
                p.emit("vector", lambda e, r0=r0, w=w: e.tensor_mul(
                    out=skr[:, r0:r0 + w], in0=skr[:, r0:r0 + w], in1=pbc[:, :w]),
                       waits=((t_c, bc),), inc=v_c)
                p.wait("tensor", v_c, v_c.v)
            p.emit("vector", lambda e, width=width, scal_ap=scal_ap:
                   e.tensor_scalar_mul(
                out=skr[:, :width], in0=skr[:, :width], scalar1=scal_ap), inc=v_c)
            scaled = v_c.v
            p.emit("sync", lambda e, width=width: e.dma_start(
                out=sswap[0:32, :width], in_=skr[32:64, :width]),
                   waits=((v_c, scaled),), inc=lt, inc_n=16)
            p.emit("sync", lambda e, width=width: e.dma_start(
                out=sswap[32:64, :width], in_=skr[0:32, :width]), inc=lt, inc_n=16)
            p.emit("sync", lambda e, width=width, tabB_h=tabB_h, lo=lo: e.dma_start(
                out=stabB[:, :width], in_=tabB_h[:, lo:lo + width]), inc=lt, inc_n=16)
            swap_ld = lt.v
            for (r0, w) in strips(width, 512):
                p.emit("vector", lambda e, r0=r0, w=w: e.tensor_mul(
                    out=ssq[0:32, :w], in0=skr[0:32, r0:r0 + w],
                    in1=stabA[0:32, r0:r0 + w]),
                       waits=((lt, swap_ld),) if r0 == 0 else ())
                p.emit("vector", lambda e, r0=r0, w=w: e.tensor_mul(
                    out=ssq2[0:32, :w], in0=sswap[0:32, r0:r0 + w],
                    in1=stabB[0:32, r0:r0 + w]))
                p.emit("vector", lambda e, r0=r0, w=w: e.tensor_sub(
                    out=ssq[0:32, :w], in0=ssq[0:32, :w], in1=ssq2[0:32, :w]))
                p.emit("vector", lambda e, r0=r0, w=w: e.tensor_mul(
                    out=ssq[32:64, :w], in0=sswap[32:64, r0:r0 + w],
                    in1=stabA[32:64, r0:r0 + w]))
                p.emit("vector", lambda e, r0=r0, w=w: e.tensor_mul(
                    out=ssq2[32:64, :w], in0=skr[32:64, r0:r0 + w],
                    in1=stabB[32:64, r0:r0 + w]))
                p.emit("vector", lambda e, r0=r0, w=w: e.tensor_add(
                    out=ssq[32:64, :w], in0=ssq[32:64, :w], in1=ssq2[32:64, :w]))
                p.emit("vector", lambda e, r0=r0, w=w, out_slice=out_slice:
                       e.tensor_copy(out=out_slice[:, r0:r0 + w], in_=ssq[0:64, :w]),
                       inc=v_c)
            return v_c.v

        for h in range(NHEAD):
            hb = h % 2
            wprev = [v_c.v, v_c.v]
            for i in range(NCH):
                ib = i % 2
                p.emit("sync", lambda e, h=h, i=i, ib=ib: e.dma_start(
                    out=swf[:, ib, :], in_=wq_h[h, i * 128:(i + 1) * 128, :]),
                       waits=((v_c, wprev[ib]),), inc=lw[ib], inc_n=16)
                wld = lw[ib].v
                p.emit("vector", lambda e, hb=hb, i=i, ib=ib: e.tensor_copy(
                    out=swh[:, hb, i, :], in_=swf[:, ib, :]),
                       waits=((lw[ib], wld),), inc=v_c)
                wprev[ib] = v_c.v
            w_done = v_c.v

            kdone = l2rot_full(KV_LO, KV_W, tk_a, tk_b, sqsks[:, 1:2],
                               skne2[0:64, :], slice(64, 128),
                               max(w_done, xb_done))
            prev_sq_v = s_c.v
            for (s0, w) in strips(KV_W, 512, start=KV_LO):
                for i in range(NCH):
                    p.emit("tensor", lambda e, hb=hb, i=i, s0=s0, w=w: e.matmul(
                        pkv[0:64, :w], swh[:, hb, i, 128:192],
                        shifted_rhs(i, s0, w),
                        start=(i == 0), stop=(i == NCH - 1)),
                           waits=((s_c, prev_sq_v),) if i == 0 else (),
                           inc=t_c if i == NCH - 1 else None)
                vmm = t_c.v
                p.emit("scalar", lambda e, w=w: e.activation(
                    out=svfm[:, :w], in_=pkv[0:64, :w], func=AF.Copy),
                       waits=((t_c, vmm),), inc=s_c)
                prev_sq_v = s_c.v
                v_evac = s_c.v
                for j in range(w // 128):
                    kchunk = (s0 - KV_LO) // 128 + j
                    p.emit("tensor", lambda e, j=j: e.transpose(
                        pdbc.bitcast(BF16)[:, 0:64],
                        svfm[:, j * 128:(j + 1) * 128], sident[0:64, 0:64]),
                           waits=((s_c, v_evac),) if j == 0 else (), inc=t_c)
                    trd = t_c.v
                    p.emit("vector", lambda e, kchunk=kchunk: e.tensor_copy(
                        out=svtm2[:, kchunk, 0:64], in_=pdbc.bitcast(BF16)[:, 0:64]),
                           waits=((t_c, trd),), inc=v_c)
                    p.wait("tensor", v_c, v_c.v)
            kn_done_s = s_c.v

            qdone = l2rot_full(Q0, T_OWN, tq_a, tq_b, sqs8,
                               sq2[0:64, :], slice(0, 64), kdone)

            p1_red = [0] * NTILE
            for t in range(NTILE):
                gb = t % 2
                c0 = Q0 + t * 128
                klo = c0 - 256 - KV_LO
                p.emit("tensor", lambda e, t=t, gb=gb, klo=klo: e.matmul(
                    psc1[:, gb, 0:384], sq2[0:64, t * 128:(t + 1) * 128],
                    skne2[0:64, klo:klo + 384], start=True, stop=True),
                       waits=tuple(([(v_c, v_c.v)] if t == 0 else [])
                                   + ([(v_c, p1_red[t - 2])] if t >= 2 else [])),
                       inc=t_c)
                p1mm = t_c.v
                p.emit("vector", lambda e, gb=gb: e.tensor_add(
                    out=psc1[:, gb, 0:384], in0=psc1[:, gb, 0:384], in1=smtri),
                       waits=((t_c, p1mm), (li, init_ld)))
                p.emit("vector", lambda e, t=t, gb=gb: e.reduce_max(
                    out=smaxneg[:, t:t + 1], in_=psc1[:, gb, 0:384], axis=AX.X,
                    negate=True), inc=v_c)
                p1_red[t] = v_c.v
            p.emit("tensor", lambda e: e.transpose(
                pdbc.bitcast(BF16)[0:NTILE, 0:128], smaxneg, sident),
                   waits=((v_c, v_c.v),), inc=t_c)
            trmax = t_c.v
            p.emit("vector", lambda e: e.tensor_copy(
                out=smaxT, in_=pdbc.bitcast(BF16)[0:NTILE, 0:128]),
                   waits=((t_c, trmax),), inc=v_c)
            maxT_done = v_c.v
            for t in range(NTILE):
                p.emit("gpsimd", lambda e, t=t: e.dma_start(
                    out=sq2[64:65, t * 128:(t + 1) * 128], in_=smaxT[t:t + 1, :]),
                       waits=((v_c, maxT_done),) if t == 0 else (), inc=g_c, inc_n=16)
            brow_ld = g_c.v

            exp_done = [0] * NTILE
            ev_done = [0] * NTILE
            for t in range(NTILE):
                gb = t % 2
                c0 = Q0 + t * 128
                klo = c0 - 256 - KV_LO
                for j in range(3):
                    p.emit("tensor", lambda e, t=t, gb=gb, klo=klo, j=j: e.matmul(
                        psc2[:, gb, j * 128:(j + 1) * 128],
                        skne2[:, klo + j * 128:klo + (j + 1) * 128],
                        sq2[:, t * 128:(t + 1) * 128], start=True, stop=True),
                           waits=tuple(
                               ([(g_c, brow_ld), (s_c, kn_done_s)] if (t == 0 and j == 0) else [])
                               + ([(s_c, exp_done[t - 2])] if (t >= 2 and j == 0) else [])
                               + ([(v_c, v_c.v)] if (t == 0 and j == 0) else [])),
                           inc=t_c if j == 2 else None)
                p2mm = t_c.v
                p.emit("vector", lambda e, gb=gb: e.tensor_add(
                    out=psc2[:, gb, :], in0=psc2[:, gb, :], in1=smtri2),
                       waits=((t_c, p2mm),), inc=v_c)
                mask_done = v_c.v
                p.emit("scalar", lambda e, gb=gb: e.activation(
                    out=sp[:, gb, :], in_=psc2[:, gb, :], func=AF.Exp),
                       waits=((v_c, mask_done),), inc=s_c)
                exp_done[t] = s_c.v
                for j in range(3):
                    kchunk = (klo // 128) + j
                    p.emit("tensor", lambda e, gb=gb, j=j, kchunk=kchunk: e.matmul(
                        ppv[:, :], svtm2[:, kchunk, :], sp[:, gb, j * 128:(j + 1) * 128],
                        start=(j == 0), stop=(j == 2)),
                           waits=tuple(([(s_c, exp_done[t])] if j == 0 else [])
                                       + ([(s_c, ev_done[t - 1])] if (j == 0 and t >= 1) else [])),
                           inc=t_c if j == 2 else None)
                pv_done = t_c.v
                p.emit("scalar", lambda e, t=t, ob=(h % 2) * 64: e.activation(
                    out=sout_h[ob:ob + 64, t * 128:(t + 1) * 128], in_=ppv[0:64, :],
                    func=AF.Copy), waits=((t_c, pv_done),))
                p.emit("scalar", lambda e, t=t: e.activation(
                    out=sden[:, t * 128:(t + 1) * 128], in_=ppv[64:65, :],
                    func=AF.Copy), inc=s_c)
                ev_done[t] = s_c.v
            p.emit("scalar", lambda e: e.activation(
                out=sden, in_=sden, func=AF.Ln), waits=((s_c, ev_done[-1]),))
            p.emit("scalar", lambda e: e.activation(
                out=sden, in_=sden, func=AF.Exp, scale=-1.0), inc=s_c)
            rden = s_c.v
            ob = (h % 2) * 64
            for (r0, w) in strips(T_OWN, 512):
                p.emit("tensor", lambda e, r0=r0, w=w, ob=ob: e.matmul(
                    pdbc[ob:ob + 64, :w],
                    sones_r[:, 0:64], sden[:, r0:r0 + w],
                    start=True, stop=True,
                    tile_position=(0, ob) if ob else None),
                       waits=((s_c, rden),) if r0 == 0 else (), inc=t_c)
                bcd = t_c.v
                p.emit("vector", lambda e, r0=r0, w=w, ob=ob, h=h: e.tensor_mul(
                    out=assem[ob:ob + 64, h // 2, r0:r0 + w],
                    in0=sout_h[ob:ob + 64, r0:r0 + w],
                    in1=pdbc[ob:ob + 64, :w]),
                       waits=((t_c, bcd),), inc=v_c)
                p.wait("tensor", v_c, v_c.v)
        heads_done_v = v_c.v
        if debug:
            p.wait("gpsimd", v_c, heads_done_v)
            for ci in range(NCH):
                p.emit("gpsimd", lambda e, ci=ci: e.dma_start(
                    out=dassem[ci * 128:(ci + 1) * 128, :], in_=assem[:, ci, :]),
                       inc=st, inc_n=16)

        for (s0, w) in strips(T_OWN, 512):
            v_snap = v_c.v
            for i in range(NCH):
                p.emit("sync", lambda e, i=i, s0=s0, w=w: e.dma_start(
                    out=sxf[:, i, :w], in_=x_h[i * 128:(i + 1) * 128, Q0 + s0:Q0 + s0 + w]),
                       waits=((v_c, v_snap),) if i == 0 else (), inc=lx, inc_n=16)
            x_ld = lx.v
            for o in range(NCH):
                for i in range(NCH):
                    p.emit("tensor", lambda e, i=i, o=o, s0=s0, w=w: e.matmul(
                        pkv[:, :w], swo[:, i, o * 128:(o + 1) * 128],
                        assem[:, i, s0:s0 + w], start=(i == 0), stop=(i == NCH - 1)),
                           waits=((v_c, max(heads_done_v, wo_done)),)
                           if (s0 == 0 and o == 0 and i == 0) else (),
                           inc=t_c if i == NCH - 1 else None)
                omm = t_c.v
                p.emit("scalar", lambda e, w=w: e.activation(
                    out=ssq[:, :w], in_=pkv[:, :w], func=AF.Copy),
                       waits=((t_c, omm),), inc=s_c)
                oevac = s_c.v
                p.emit("vector", lambda e, o=o, w=w: e.tensor_add(
                    out=sy[:, :w], in0=ssq[:, :w], in1=sxf[:, o, :w]),
                       waits=((s_c, oevac), (lx, x_ld)), inc=v_c)
                ydone = v_c.v
                p.wait("tensor", v_c, ydone)
                p.wait("scalar", v_c, ydone)
                p.emit("gpsimd", lambda e, o=o, s0=s0, w=w: e.dma_start(
                    out=y_h[o * 128:(o + 1) * 128, s0:s0 + w], in_=sy[:, :w]),
                       waits=((v_c, ydone),), inc=st, inc_n=16)
                p.wait("vector", st, st.v)
        p.wait("gpsimd", st, st.v)
        p.finalize(block)
    return nc


def _xpos_tables():
    half = np.arange(0, HD, 2, dtype=np.float64)
    inv_freq = 1.0 / 10000.0 ** (half / HD)
    base = (half + 0.4 * HD) / (1.4 * HD)
    pos = np.arange(-512, SEQ, dtype=np.float64)
    f32 = np.float32
    freqs = f32(pos[None, :]) * f32(inv_freq[:, None])
    cos = np.cos(freqs.astype(np.float64)).astype(f32)
    sin = np.sin(freqs.astype(np.float64)).astype(f32)
    power = f32((pos - SEQ // 2) / 128.0)
    scale = f32(base[:, None]) ** power[None, :]
    inv = (1.0 / scale).astype(f32)
    cs, ss = cos * scale, sin * scale
    ci, si = cos * inv, sin * inv
    stk = lambda a, b: np.ascontiguousarray(np.concatenate([a, b], axis=0))
    return (stk(cs, ss), stk(ss, cs), stk(ci, si), stk(si, ci))


def _l0f_mtri():
    i = np.arange(128)[:, None]
    m = np.zeros((128, 384), np.float32)
    jj0 = np.arange(128)[None, :]
    m[:, 0:128] = np.where(jj0 < i, -1e30, 0.0)
    m[:, 256:384] = np.where(jj0 > i, -1e30, 0.0)
    return m.astype(BF)


def _l0f_mtri2():
    m = np.asarray(_l0f_mtri(), np.float32)
    m2 = np.zeros((128, 384), np.float32)
    for j in range(3):
        m2[:, j * 128:(j + 1) * 128] = m[:, j * 128:(j + 1) * 128].T
    return m2.astype(BF)


def l0f_attn_layer_maps(ins, layer):
    Wqkv = np.asarray(ins["L0_ffn_Wqkv"], np.float32)[layer]
    wq = np.stack([np.ascontiguousarray(np.concatenate(
        [Wqkv[:, h * HD:(h + 1) * HD],
         Wqkv[:, D + h * HD:D + (h + 1) * HD],
         Wqkv[:, 2 * D + h * HD:2 * D + (h + 1) * HD]], axis=1))
        for h in range(NHEAD)])
    qsks = np.stack([np.asarray(ins["L0_ffn_q_scale"], np.float32)[layer],
                     np.asarray(ins["L0_ffn_k_scale"], np.float32)[layer]], axis=1)
    return dict(wqkvh=wq, qsks=np.ascontiguousarray(qsks),
                wo=np.ascontiguousarray(np.asarray(ins["L0_ffn_Wo"], np.float32)[layer]),
                mtri=_l0f_mtri(), mtri2=_l0f_mtri2(), ident=np.eye(128, dtype=BF))


def l0f_attn_in_maps(xcur):
    tqA, tqB, tkA, tkB = _XPOS_TABS
    maps = []
    for c in range(NCORES):
        b, s = divmod(c, 4)
        s0 = s * T_OWN
        xw = np.zeros((D, T_IN), np.float32)
        lo = s0 - 512
        src_lo = max(lo, 0)
        xw[:, src_lo - lo:] = xcur[b, src_lo:s0 + T_OWN, :].T
        hm = np.zeros((1, T_IN), np.float32)
        if lo < 0:
            hm[0, :-lo] = -1e30
        tab = lambda t: np.ascontiguousarray(t[:, 512 + lo:512 + lo + T_IN])
        maps.append(dict(x=np.ascontiguousarray(xw), hm=hm.astype(BF),
                         tqA=tab(tqA), tqB=tab(tqB), tkA=tab(tkA), tkB=tab(tkB)))
    return maps


_XPOS_TABS = _xpos_tables()



NHC = 22
T_F = T_OWN + 1


def build_l0f_ffn(debug=False):
    nc = bass.Bass("TRN2", target_bir_lowering=False, debug=False)
    x_h = nc.dram_tensor("x", [D, T_F], F32, kind="ExternalInput").ap()
    w1_h = nc.dram_tensor("w1", [D, 5460], F32, kind="ExternalInput").ap()
    w2_h = nc.dram_tensor("w2", [NHC * 128, D], F32, kind="ExternalInput").ap()
    b1_h = nc.dram_tensor("b1", [128, 2 * NHC], F32, kind="ExternalInput").ap()
    b2_h = nc.dram_tensor("b2", [128, NCH], F32, kind="ExternalInput").ap()
    lng_h = nc.dram_tensor("lng", [128, NCH], F32, kind="ExternalInput").ap()
    lnb_h = nc.dram_tensor("lnb", [128, NCH], F32, kind="ExternalInput").ap()
    y_h = nc.dram_tensor("y", [D, T_OWN], F32, kind="ExternalOutput").ap()
    if debug:
        dxs = nc.dram_tensor("dxs", [D, 256], BF16, kind="ExternalOutput").ap()
        dag = nc.dram_tensor("dag", [128, 256], BF16, kind="ExternalOutput").ap()
        db1 = nc.dram_tensor("db1", [128, 2 * NHC], F32, kind="ExternalOutput").ap()
        dw1 = nc.dram_tensor("dw1", [128, 256], BF16, kind="ExternalOutput").ap()

    HCW = [128] * (NHC - 1) + [42]

    es = ExitStack()
    sb = lambda n, s, d=F32: es.enter_context(nc.sbuf_tensor(n, s, d)).ap()
    psb = lambda n, s, d=F32: es.enter_context(nc.psum_tensor(n, s, d)).ap()
    with es:
        sx = sb("sx", [128, NCH, T_F])
        sw1 = sb("sw1", [128, NCH, 5460], BF16)
        sw2 = sb("sw2", [128, NHC, D], BF16)
        swf = sb("swf", [128, 2, 256])
        sxs = sb("sxs", [128, NCH, 256], BF16)
        sag = sb("sag", [128, 256], BF16)
        sgel = sb("sgel", [128, 256])
        ssq = sb("ssq", [128, 256])
        ssq2 = sb("ssq2", [128, 256])
        sy = sb("sy", [128, 256])
        sb1 = sb("sb1", [128, 2 * NHC])
        sb2 = sb("sb2", [128, NCH])
        slng = sb("slng", [128, NCH])
        slnb = sb("slnb", [128, NCH])
        srow1 = sb("srow1", [1, 256])
        srow2 = sb("srow2", [1, 256])
        srow3 = sb("srow3", [1, 256])
        sones = sb("sones", [128, 1])
        sones_r = sb("sones_r", [1, 128])
        seps = sb("seps", [1, 1])
        pout = psb("pout", [128, NCH, 256])
        pha = psb("pha", [128, 256])
        phg = psb("phg", [128, 256])
        prow_m = psb("prow_m", [1, 256])
        prow_v = psb("prow_v", [1, 256])

        p = Prog(nc, es)
        li = p.sem("li")
        lx = p.sem("lx")
        lw = [p.sem("lw0"), p.sem("lw1")]
        t_c = p.sem("t_c")
        v_c = p.sem("v_c")
        s_c = p.sem("s_c")
        st = p.sem("st")

        block = es.enter_context(nc.Block())

        for name, dst, srch in (("b1", sb1, b1_h), ("b2", sb2, b2_h),
                                ("lng", slng, lng_h), ("lnb", slnb, lnb_h)):
            p.emit("sync", lambda e, dst=dst, srch=srch: e.dma_start(out=dst, in_=srch),
                   inc=li, inc_n=16)
        init_ld = li.v
        p.emit("vector", lambda e: e.memset(sones, 1.0))
        p.emit("vector", lambda e: e.memset(sones_r, 1.0))
        p.emit("vector", lambda e: e.memset(seps, EPS))
        p.emit("vector", lambda e: e.memset(sag, 0.0), inc=v_c)
        init_done = v_c.v

        for i in range(NCH):
            p.emit("sync", lambda e, i=i: e.dma_start(
                out=sx[:, i, :], in_=x_h[i * 128:(i + 1) * 128, :]), inc=lx, inc_n=16)
        x_ld = lx.v

        swf_free = [init_done, init_done]
        swf_free_t = [0, 0]
        for i in range(NCH):
            for cg in range(22):
                cw = min(256, 5460 - cg * 256)
                ib = (i * 22 + cg) % 2
                p.emit("sync", lambda e, i=i, cg=cg, cw=cw, ib=ib: e.dma_start(
                    out=swf[:, ib, :cw], in_=w1_h[i * 128:(i + 1) * 128,
                                                  cg * 256:cg * 256 + cw]),
                       waits=((v_c, swf_free[ib]), (t_c, swf_free_t[ib])),
                       inc=lw[ib], inc_n=16)
                wld = lw[ib].v
                p.emit("vector", lambda e, i=i, cg=cg, cw=cw, ib=ib: e.tensor_scalar_mul(
                    out=sw1[:, i, cg * 256:cg * 256 + cw], in0=swf[:, ib, :cw],
                    scalar1=slng[:, i:i + 1]),
                       waits=((lw[ib], wld), (li, init_ld)), inc=v_c)
                swf_free[ib] = v_c.v
        w1_done = v_c.v
        for z in range(2 * NHC):
            half, k = divmod(z, NHC)
            wk = min(128, 2730 - 128 * k)
            j0 = (2730 if half else 0) + 128 * k
            for i in range(NCH):
                ib = i % 2
                if wk < 128:
                    p.emit("vector", lambda e, ib=ib, wk=wk: e.memset(
                        swf[:, ib, wk:128], 0.0),
                           waits=((t_c, swf_free_t[ib]),), inc=v_c)
                    swf_free[ib] = v_c.v
                p.emit("sync", lambda e, i=i, ib=ib, j0=j0, wk=wk: e.dma_start(
                    out=swf[:, ib, :wk], in_=w1_h[i * 128:(i + 1) * 128, j0:j0 + wk]),
                       waits=((v_c, swf_free[ib]), (t_c, swf_free_t[ib])),
                       inc=lw[ib], inc_n=16)
                wld = lw[ib].v
                p.emit("tensor", lambda e, i=i, ib=ib, z=z, wk=wk: e.matmul(
                    pha[0:128, z:z + 1], swf[:, ib, :128], slnb[:, i:i + 1],
                    start=(i == 0), stop=(i == NCH - 1)),
                       waits=((lw[ib], wld), (li, init_ld)), inc=t_c)
                swf_free_t[ib] = t_c.v
        b1mm_done = t_c.v
        p.emit("vector", lambda e: e.tensor_add(
            out=sb1, in0=sb1, in1=pha[:, 0:2 * NHC]),
               waits=((t_c, b1mm_done), (li, init_ld)), inc=v_c)
        b1_done = v_c.v
        p.wait("tensor", v_c, b1_done)

        for k in range(NHC):
            for cg in range(4):
                ib = (k * 4 + cg) % 2
                p.emit("sync", lambda e, k=k, cg=cg, ib=ib: e.dma_start(
                    out=swf[:, ib, :], in_=w2_h[k * 128:(k + 1) * 128,
                                                cg * 256:(cg + 1) * 256]),
                       waits=((v_c, swf_free[ib]), (t_c, swf_free_t[ib])),
                       inc=lw[ib], inc_n=16)
                wld = lw[ib].v
                p.emit("vector", lambda e, k=k, cg=cg, ib=ib: e.tensor_copy(
                    out=sw2[:, k, cg * 256:(cg + 1) * 256], in_=swf[:, ib, :]),
                       waits=((lw[ib], wld),), inc=v_c)
                swf_free[ib] = v_c.v
        w2_done = v_c.v

        for si in range(8):
            c0 = 1 + si * 256
            w = 256
            strip_barrier(p, t_c, v_c, s_c)

            def xsh(i, a, b):
                if i < 4:
                    return sx[:, i, a:b]
                return sx[:, i, a - 1:b - 1]

            for i in range(NCH):
                p.emit("tensor", lambda e, i=i, c0=c0: e.matmul(
                    prow_m[:, :], sones, xsh(i, c0, c0 + 256),
                    start=(i == 0), stop=(i == NCH - 1)),
                       waits=((lx, x_ld), (v_c, init_done)) if i == 0 else (),
                       inc=t_c if i == NCH - 1 else None)
            m_done = t_c.v
            for i in range(NCH):
                p.emit("scalar", lambda e, i=i, c0=c0: e.activation(
                    out=ssq2, in_=xsh(i, c0, c0 + 256), func=AF.Square),
                       waits=((lx, x_ld),) if i == 0 else ((t_c, t_c.v),), inc=s_c)
                sqd = s_c.v
                p.emit("tensor", lambda e, i=i: e.matmul(
                    prow_v[:, :], sones, ssq2, start=(i == 0), stop=(i == NCH - 1)),
                       waits=((s_c, sqd),), inc=t_c)
            v_done = t_c.v
            p.emit("vector", lambda e: e.tensor_scalar_mul(
                out=srow1, in0=prow_m, scalar1=1.0 / D), waits=((t_c, v_done),))
            p.emit("vector", lambda e: e.tensor_scalar_mul(
                out=srow2, in0=prow_v, scalar1=1.0 / D), waits=((t_c, v_done),))
            p.emit("vector", lambda e: e.tensor_mul(out=srow3, in0=srow1, in1=srow1))
            p.emit("vector", lambda e: e.tensor_sub(out=srow2, in0=srow2, in1=srow3),
                   inc=v_c)
            var_done = v_c.v
            p.emit("scalar", lambda e: e.activation(out=srow2, in_=srow2, func=AF.Ln,
                                                    bias=seps),
                   waits=((v_c, var_done),))
            p.emit("scalar", lambda e: e.activation(out=srow2, in_=srow2, func=AF.Exp,
                                                    scale=-0.5), inc=s_c)
            rstd_done = s_c.v
            p.emit("vector", lambda e: e.tensor_mul(out=srow1, in0=srow1, in1=srow2),
                   waits=((s_c, rstd_done),))
            p.emit("vector", lambda e: e.tensor_scalar_mul(out=srow1, in0=srow1,
                                                           scalar1=-1.0), inc=v_c)
            rows_done = v_c.v
            p.emit("tensor", lambda e: e.matmul(pha[:, :], sones_r, srow2,
                                                start=True, stop=True),
                   waits=((v_c, rows_done),))
            p.emit("tensor", lambda e: e.matmul(phg[:, :], sones_r, srow1,
                                                start=True, stop=True), inc=t_c)
            bc_done = t_c.v
            for i in range(NCH):
                p.emit("vector", lambda e, i=i, c0=c0: e.tensor_mul(
                    out=ssq, in0=xsh(i, c0, c0 + 256), in1=pha[:, :]),
                       waits=((t_c, bc_done),) if i == 0 else ())
                p.emit("vector", lambda e, i=i: e.tensor_add(
                    out=sxs[:, i, :], in0=ssq, in1=phg[:, :]), inc=v_c)
            xs_done = v_c.v
            p.wait("tensor", v_c, xs_done)
            if debug and si == 0:
                for i in range(NCH):
                    p.emit("gpsimd", lambda e, i=i: e.dma_start(
                        out=dxs[i * 128:(i + 1) * 128, :], in_=sxs[:, i, :]),
                           waits=((v_c, xs_done),), inc=st, inc_n=16)
                p.emit("gpsimd", lambda e: e.dma_start(out=db1, in_=sb1),
                       inc=st, inc_n=16)
                p.emit("gpsimd", lambda e: e.dma_start(out=dw1, in_=sw1[:, 0, 0:256]),
                       inc=st, inc_n=16)

            prev_ag_done = v_c.v
            for k in range(NHC):
                wk = HCW[k]
                a0 = 128 * k
                g0 = 2730 + 128 * k
                for i in range(NCH):
                    p.emit("tensor", lambda e, i=i, a0=a0, wk=wk: e.matmul(
                        pha[0:wk, :], sw1[:, i, a0:a0 + wk], sxs[:, i, :],
                        start=(i == 0), stop=(i == NCH - 1)),
                           waits=((v_c, max(w1_done, xs_done)),)
                           if (si == 0 and k == 0 and i == 0) else (),
                           inc=t_c if i == NCH - 1 else None)
                for i in range(NCH):
                    p.emit("tensor", lambda e, i=i, g0=g0, wk=wk: e.matmul(
                        phg[0:wk, :], sw1[:, i, g0:g0 + wk], sxs[:, i, :],
                        start=(i == 0), stop=(i == NCH - 1)),
                           inc=t_c if i == NCH - 1 else None)
                hg_done = t_c.v
                p.emit("scalar", lambda e, k=k, wk=wk: e.activation(
                    out=sgel[0:wk, :], in_=phg[0:wk, :], func=AF.Gelu,
                    bias=sb1[0:wk, NHC + k:NHC + k + 1]),
                       waits=((t_c, hg_done), (v_c, prev_ag_done)), inc=s_c)
                gel_done = s_c.v
                p.emit("vector", lambda e, k=k, wk=wk: e.scalar_tensor_tensor(
                    out=sag[0:wk, :], in0=pha[0:wk, :],
                    scalar=sb1[0:wk, k:k + 1], in1=sgel[0:wk, :],
                    op0=ALU.add, op1=ALU.mult),
                       waits=((s_c, gel_done),), inc=v_c)
                ag_done = v_c.v
                prev_ag_done = ag_done
                p.wait("tensor", v_c, ag_done)
                if debug and si == 0 and k == 0:
                    p.emit("gpsimd", lambda e: e.dma_start(out=dag, in_=sag),
                           waits=((v_c, ag_done),), inc=st, inc_n=16)
                    p.wait("tensor", st, st.v)
                for oc in range(NCH):
                    p.emit("tensor", lambda e, k=k, oc=oc: e.matmul(
                        pout[:, oc, :], sw2[:, k, oc * 128:(oc + 1) * 128], sag,
                        start=(k == 0 and oc % 2 == 0), stop=(k == NHC - 1)),
                           waits=((v_c, max(ag_done, w2_done)),) if oc == 0 else (),
                           inc=t_c if oc == NCH - 1 else None)
            ffn_done = t_c.v
            for oc in range(NCH):
                p.emit("scalar", lambda e, oc=oc: e.activation(
                    out=ssq2, in_=pout[:, oc, :], func=AF.Identity,
                    bias=sb2[:, oc:oc + 1]),
                       waits=((t_c, ffn_done),) if oc == 0 else (), inc=s_c)
                ev_done = s_c.v
                p.emit("vector", lambda e, oc=oc, c0=c0: e.tensor_add(
                    out=sy, in0=ssq2, in1=sx[:, oc, c0:c0 + 256]),
                       waits=((s_c, ev_done),), inc=v_c)
                y_done = v_c.v
                p.wait("scalar", v_c, y_done)
                p.emit("gpsimd", lambda e, oc=oc, si=si: e.dma_start(
                    out=y_h[oc * 128:(oc + 1) * 128, si * 256:(si + 1) * 256],
                    in_=sy), waits=((v_c, y_done),), inc=st, inc_n=16)
                p.wait("vector", st, st.v)
        p.wait("gpsimd", st, st.v)
        p.finalize(block)
    return nc


def l0f_ffn_layer_maps(ins, layer):
    w1 = np.ascontiguousarray(np.asarray(ins["L0_ffn_W1"], np.float32)[layer])
    w2r = np.asarray(ins["L0_ffn_W2"], np.float32)[layer]
    w2 = np.zeros((NHC * 128, D), np.float32)
    w2[:2730] = w2r
    b1r = np.asarray(ins["L0_ffn_b1"], np.float32)[layer]
    b1 = np.zeros((128, 2 * NHC), np.float32)
    for k in range(NHC):
        wk = min(128, 2730 - 128 * k)
        b1[:wk, k] = b1r[128 * k:128 * k + wk]
        b1[:wk, NHC + k] = b1r[2730 + 128 * k:2730 + 128 * k + wk]
    chunked = lambda v: np.ascontiguousarray(
        np.asarray(v, np.float32).reshape(NCH, 128).T)
    return dict(w1=w1, w2=np.ascontiguousarray(w2), b1=b1,
                b2=chunked(ins["L0_ffn_b2"][layer]),
                lng=chunked(ins["L0_ffn_ln_g"][layer]),
                lnb=chunked(ins["L0_ffn_ln_b"][layer]))


def l0f_ffn_in_maps(xmid):
    maps = []
    for c in range(NCORES):
        b, s = divmod(c, 4)
        s0 = s * T_OWN
        xw = np.zeros((D, T_F), np.float32)
        lo = s0 - 1
        src_lo = max(lo, 0)
        xw[:, src_lo - lo:] = xmid[b, src_lo:s0 + T_OWN, :].T
        maps.append({"x": np.ascontiguousarray(xw)})
    return maps



def build_ln():
    nc = bass.Bass("TRN2", target_bir_lowering=False, debug=False)
    x_h = nc.dram_tensor("x", [D, T_OWN], F32, kind="ExternalInput").ap()
    y_h = nc.dram_tensor("y", [D, T_OWN], F32, kind="ExternalOutput").ap()
    es = ExitStack()
    sb = lambda n, s, d=F32: es.enter_context(nc.sbuf_tensor(n, s, d)).ap()
    psb = lambda n, s, d=F32: es.enter_context(nc.psum_tensor(n, s, d)).ap()
    with es:
        sxf = sb("sxf", [128, NCH, 512])
        ssq = sb("ssq", [128, 512])
        ssq2 = sb("ssq2", [128, 512])
        sy = sb("sy", [128, 512])
        srow_a = sb("srow_a", [1, 512])
        srow_b = sb("srow_b", [1, 512])
        sones = sb("sones", [128, 1])
        sones_r = sb("sones_r", [1, 128])
        seps = sb("seps", [1, 1])
        prow_m = psb("prow_m", [1, 512])
        prow_v = psb("prow_v", [1, 512])
        pbcA = psb("pbcA", [128, 512])
        pbcB = psb("pbcB", [128, 512])
        p = Prog(nc, es)
        lx = p.sem("lx")
        t_c = p.sem("t_c")
        v_c = p.sem("v_c")
        s_c = p.sem("s_c")
        st = p.sem("st")
        block = es.enter_context(nc.Block())
        p.emit("vector", lambda e: e.memset(sones, 1.0))
        p.emit("vector", lambda e: e.memset(sones_r, 1.0))
        p.emit("vector", lambda e: e.memset(seps, EPS), inc=v_c)
        init_done = v_c.v
        for (s0, w) in strips(T_OWN, 512):
            t_snap, v_snap, s_snap = t_c.v, v_c.v, s_c.v
            for i in range(NCH):
                p.emit("sync", lambda e, i=i, s0=s0, w=w: e.dma_start(
                    out=sxf[:, i, :w], in_=x_h[i * 128:(i + 1) * 128, s0:s0 + w]),
                       waits=((v_c, max(v_snap, init_done)), (st, st.v))
                       if i == 0 else (),
                       inc=lx, inc_n=16)
            x_ld = lx.v
            emit_ln_rows(
                p, x_chunks=[sxf[:, i, :w] for i in range(NCH)], w=w,
                prow_m=prow_m[:, :w], prow_v=prow_v[:, :w], sq=ssq2[:, :w],
                row_a=srow_a[:, :w], row_b=srow_b[:, :w],
                ones=sones, t_c=t_c, v_c=v_c, s_c=s_c, eps_ap=seps,
                first_t_waits=((lx, x_ld), (v_c, max(init_done, v_snap))),
                first_s_waits=((lx, x_ld), (t_c, t_snap), (v_c, v_snap)))
            rows_done = v_c.v
            p.emit("tensor", lambda e, w=w: e.matmul(
                pbcA[:, :w], sones_r, srow_a[:, :w], start=True, stop=True),
                   waits=((v_c, rows_done),))
            p.emit("tensor", lambda e, w=w: e.matmul(
                pbcB[:, :w], sones_r, srow_b[:, :w], start=True, stop=True), inc=t_c)
            bc_done = t_c.v
            for i in range(NCH):
                p.emit("vector", lambda e, i=i, w=w: e.tensor_mul(
                    out=ssq[:, :w], in0=sxf[:, i, :w], in1=pbcA[:, :w]),
                       waits=((t_c, bc_done),) if i == 0 else ())
                p.emit("vector", lambda e, i=i, w=w: e.tensor_add(
                    out=sy[:, :w], in0=ssq[:, :w], in1=pbcB[:, :w]), inc=v_c)
                y_done = v_c.v
                p.emit("gpsimd", lambda e, i=i, s0=s0, w=w: e.dma_start(
                    out=y_h[i * 128:(i + 1) * 128, s0:s0 + w], in_=sy[:, :w]),
                       waits=((v_c, y_done),), inc=st, inc_n=16)
                p.wait("vector", st, st.v)
            p.wait("tensor", v_c, v_c.v)
        p.wait("gpsimd", st, st.v)
        p.finalize(block)
    return nc


def build_add():
    nc = bass.Bass("TRN2", target_bir_lowering=False, debug=False)
    a_h = nc.dram_tensor("a", [D, T_OWN], F32, kind="ExternalInput").ap()
    b_h = nc.dram_tensor("b", [D, T_OWN], F32, kind="ExternalInput").ap()
    y_h = nc.dram_tensor("y", [D, T_OWN], F32, kind="ExternalOutput").ap()
    es = ExitStack()
    sb = lambda n, s, d=F32: es.enter_context(nc.sbuf_tensor(n, s, d)).ap()
    with es:
        sa = sb("sa", [128, NCH, 512])
        sbb = sb("sbb", [128, NCH, 512])
        sy = sb("sy", [128, 512])
        p = Prog(nc, es)
        lx = p.sem("lx")
        v_c = p.sem("v_c")
        st = p.sem("st")
        block = es.enter_context(nc.Block())
        for (s0, w) in strips(T_OWN, 512):
            v_snap = v_c.v
            for i in range(NCH):
                p.emit("sync", lambda e, i=i, s0=s0, w=w: e.dma_start(
                    out=sa[:, i, :w], in_=a_h[i * 128:(i + 1) * 128, s0:s0 + w]),
                       waits=((v_c, v_snap),) if i == 0 else (), inc=lx, inc_n=16)
                p.emit("sync", lambda e, i=i, s0=s0, w=w: e.dma_start(
                    out=sbb[:, i, :w], in_=b_h[i * 128:(i + 1) * 128, s0:s0 + w]),
                       inc=lx, inc_n=16)
            x_ld = lx.v
            for i in range(NCH):
                p.emit("vector", lambda e, i=i, w=w: e.tensor_add(
                    out=sy[:, :w], in0=sa[:, i, :w], in1=sbb[:, i, :w]),
                       waits=((lx, x_ld),) if i == 0 else ((st, st.v),), inc=v_c)
                y_done = v_c.v
                p.emit("gpsimd", lambda e, i=i, s0=s0, w=w: e.dma_start(
                    out=y_h[i * 128:(i + 1) * 128, s0:s0 + w], in_=sy[:, :w]),
                       waits=((v_c, y_done),), inc=st, inc_n=16)
        p.wait("gpsimd", st, st.v)
        p.finalize(block)
    return nc


def shard_own_fm(xfull):
    return [np.ascontiguousarray(xfull[c // 4, (c % 4) * T_OWN:((c % 4) + 1) * T_OWN, :].T)
            for c in range(NCORES)]



_NC_CACHE = {}


def _get_nc(name, builder):
    if name not in _NC_CACHE:
        _NC_CACHE[name] = builder()
    return _NC_CACHE[name]


def kernel(**inputs):
    ins = {k: np.asarray(v, np.float32) for k, v in inputs.items()}
    x0 = ins["L0_attn_x"]
    cores = list(range(NCORES))

    nc = _get_nc("l0_attn", build_l0_attn)
    wm = l0_attn_weight_maps(ins)
    maps = [dict(m, **wm) for m in l0_attn_in_maps(x0)]
    res = run_bass_kernel_spmd(nc, maps, core_ids=cores)
    x1 = assemble_l1_attn_out(res.results)

    nc = _get_nc("ln", build_ln)
    maps = [{"x": s} for s in shard_own_fm(x1)]
    res = run_bass_kernel_spmd(nc, maps, core_ids=cores)
    h = assemble_l1_attn_out(res.results)

    for layer in range(6):
        nc = _get_nc("l0f_attn", build_l0f_attn)
        wm = l0f_attn_layer_maps(ins, layer)
        maps = [dict(m, **wm) for m in l0f_attn_in_maps(h)]
        res = run_bass_kernel_spmd(nc, maps, core_ids=cores)
        hmid = assemble_l1_attn_out(res.results)

        nc = _get_nc("l0f_ffn", build_l0f_ffn)
        wm = l0f_ffn_layer_maps(ins, layer)
        maps = [dict(m, **wm) for m in l0f_ffn_in_maps(hmid)]
        res = run_bass_kernel_spmd(nc, maps, core_ids=cores)
        h = assemble_l1_attn_out(res.results)

    nc = _get_nc("add", build_add)
    a_sh, b_sh = shard_own_fm(x1), shard_own_fm(h)
    maps = [{"a": a_sh[c], "b": b_sh[c]} for c in cores]
    res = run_bass_kernel_spmd(nc, maps, core_ids=cores)
    x2 = assemble_l1_attn_out(res.results)

    nc = _get_nc("l1_attn", build_l1_attn)
    wm = l1_attn_weight_maps(ins["L1_attn_Wq"], ins["L1_attn_Wk"], ins["L1_attn_Wv"])
    maps = [dict(m, **wm) for m in l1_attn_in_maps(x2)]
    res = run_bass_kernel_spmd(nc, maps, core_ids=cores)
    x3 = assemble_l1_attn_out(res.results)

    nc = _get_nc("l1_ffn", build_l1_ffn)
    maps = l1_ffn_in_maps(x3, ins["L1_ffn_W_r"], ins["L1_ffn_W_k"],
                          ins["L1_ffn_W_v"], ins["L1_ffn_mu_r"], ins["L1_ffn_mu_k"])
    res = run_bass_kernel_spmd(nc, maps, core_ids=cores)
    return assemble_fm_out(res.results)


# version 06 (final)
# speedup vs baseline: 1.1908x; 1.0739x over previous
import numpy as np
import ml_dtypes
from contextlib import ExitStack

import concourse.bass as bass
import concourse.mybir as mybir
from concourse.bass_utils import run_bass_kernel_spmd

F32 = mybir.dt.float32
BF16 = mybir.dt.bfloat16
AF = mybir.ActivationFunctionType
ALU = mybir.AluOpType
AX = mybir.AxisListType

BF = ml_dtypes.bfloat16

NCORES = 8
D = 1024
NCH = D // 128
SEQ = 8192
BATCH = 2
T_OWN = 2048
T = T_OWN + 1
T_PAD = 2560
EPS = 1e-5



class Ctr:
    __slots__ = ("h", "v")

    def __init__(self, h):
        self.h = h
        self.v = 0

    def inc(self, n=1):
        self.v += n
        return self.v


class Prog:
    ENGINES = ("sync", "tensor", "vector", "scalar", "gpsimd")

    def __init__(self, nc, es):
        self.nc = nc
        self.es = es
        self.streams = {e: [] for e in self.ENGINES}

    def sem(self, name):
        h = self.es.enter_context(self.nc.semaphore(name))
        return Ctr(h)

    def emit(self, eng, fn, waits=(), inc=None, inc_n=1):
        waits = tuple((c, v) for (c, v) in waits if v > 0)

        def run(e, fn=fn, waits=waits, inc=inc, inc_n=inc_n):
            for c, v in waits:
                e.wait_ge(c.h, v)
            inst = fn(e)
            if inc is not None:
                assert inst is not None
                inst.then_inc(inc.h, inc_n)

        self.streams[eng].append(run)
        if inc is not None:
            return inc.inc(inc_n)
        return None

    def wait(self, eng, ctr, val):
        if val > 0:
            self.emit(eng, lambda e: None, waits=((ctr, val),))

    def finalize(self, block):
        for eng in self.ENGINES:
            fns = self.streams[eng]
            if not fns:
                continue
            getattr(block, eng)(lambda e, fns=fns: [f(e) for f in fns])


def strips(total, width, start=0):
    return [(s, min(width, total + start - s)) for s in range(start, total + start, width)]



def emit_ln_rows(p, *, x_chunks, w, prow_m, prow_v, sq, row_a, row_b, ones,
                 t_c, v_c, s_c, eps_ap=None, first_t_waits=(), first_s_waits=()):
    for i, xc in enumerate(x_chunks):
        p.emit("tensor",
               lambda e, xc=xc, i=i: e.matmul(prow_m, ones, xc,
                                              start=(i == 0), stop=(i == NCH - 1)),
               waits=first_t_waits if i == 0 else (),
               inc=t_c if i == NCH - 1 else None)
    mdone = t_c.v
    for i, xc in enumerate(x_chunks):
        p.emit("scalar",
               lambda e, xc=xc: e.activation(out=sq, in_=xc, func=AF.Square),
               waits=first_s_waits if i == 0 else (), inc=s_c)
        sdone = s_c.v
        p.emit("tensor",
               lambda e, i=i: e.matmul(prow_v, ones, sq,
                                       start=(i == 0), stop=(i == NCH - 1)),
               waits=((s_c, sdone),), inc=t_c)
        if i < NCH - 1:
            p.wait("scalar", t_c, t_c.v)
    vsum_done = t_c.v
    p.emit("vector", lambda e: e.tensor_scalar_mul(out=row_b, in0=prow_m, scalar1=1.0 / D),
           waits=((t_c, mdone),))
    p.emit("vector", lambda e: e.tensor_scalar_mul(out=row_a, in0=prow_v, scalar1=1.0 / D),
           waits=((t_c, vsum_done),))
    p.emit("vector", lambda e: e.tensor_mul(out=sq[0:1, :w], in0=row_b, in1=row_b))
    p.emit("vector", lambda e: e.tensor_sub(out=row_a, in0=row_a, in1=sq[0:1, :w]),
           inc=v_c)
    vdone = v_c.v
    p.emit("scalar", lambda e: e.activation(out=row_a, in_=row_a, func=AF.Ln, bias=eps_ap),
           waits=((v_c, vdone),))
    p.emit("scalar", lambda e: e.activation(out=row_a, in_=row_a, func=AF.Exp, scale=-0.5),
           inc=s_c)
    rstd_done = s_c.v
    p.emit("vector", lambda e: e.tensor_mul(out=row_b, in0=row_b, in1=row_a),
           waits=((s_c, rstd_done),))
    p.emit("vector", lambda e: e.tensor_scalar_mul(out=row_b, in0=row_b, scalar1=-1.0),
           inc=v_c)
    return v_c.v



def build_l1_ffn():
    nc = bass.Bass("TRN2", target_bir_lowering=False, debug=False)
    x_h = nc.dram_tensor("x", [D, T_PAD], F32, kind="ExternalInput").ap()
    w_h = {k: nc.dram_tensor(k, [D, D], F32, kind="ExternalInput").ap()
           for k in ("wr", "wk", "wv")}
    mu_h = {k: nc.dram_tensor(k, [128, NCH], F32, kind="ExternalInput").ap()
            for k in ("mur", "muk")}
    y_h = nc.dram_tensor("y", [D, T_OWN], F32, kind="ExternalOutput").ap()

    es = ExitStack()
    sb = lambda n, s, d=F32: es.enter_context(nc.sbuf_tensor(n, s, d)).ap()
    psb = lambda n, s, d=F32: es.enter_context(nc.psum_tensor(n, s, d)).ap()
    with es:
        sxs = sb("sxs", [128, NCH, 512])
        sxo = sb("sxo", [128, NCH, 512])
        swf = sb("swf", [128, 2, D])
        sw = {k: sb("sw_" + k, [128, NCH, D], BF16) for k in ("wr", "wk", "wv")}
        smu = {k: sb("smu_" + k, [128, NCH]) for k in ("mur", "muk")}
        som = {k: sb("som_" + k, [128, NCH]) for k in ("mur", "muk")}
        sxln = sb("sxln", [128, NCH, T_PAD], BF16)
        sxr = sb("sxr", [128, NCH, 512], BF16)
        sxk = sb("sxk", [128, NCH, 512], BF16)
        sr = sb("sr", [128, NCH, 512])
        sk2 = sb("sk2", [128, NCH, 512], BF16)
        sy = sb("sy", [128, NCH, 512])
        ssq = sb("ssq", [128, 512])
        ssq2 = sb("ssq2", [128, 512])
        srow_a = sb("srow_a", [1, 512])
        srow_b = sb("srow_b", [1, 512])
        sones = sb("sones", [128, 1])
        sones_r = sb("sones_r", [1, 128])
        seps = sb("seps", [1, 1])
        prow_m = psb("prow_m", [1, 512])
        prow_v = psb("prow_v", [1, 512])
        pbcA = psb("pbcA", [128, 512])
        pbcB = psb("pbcB", [128, 512])
        pmm = psb("pmm", [128, 512])
        pmm2 = psb("pmm2", [128, 512])

        p = Prog(nc, es)
        li = p.sem("li")
        lx = p.sem("lx")
        lw = [p.sem("lw0"), p.sem("lw1")]
        t_c = p.sem("t_c")
        v_c = p.sem("v_c")
        s_c = p.sem("s_c")
        st = p.sem("st")

        block = es.enter_context(nc.Block())

        for k in ("mur", "muk"):
            p.emit("sync", lambda e, k=k: e.dma_start(out=smu[k], in_=mu_h[k]),
                   inc=li, inc_n=16)
        mu_ld = li.v
        wcast_done = {}
        prev_cast = [0, 0]
        for k in ("wr", "wk", "wv"):
            for i in range(NCH):
                ib = i % 2
                p.emit("sync", lambda e, k=k, i=i, ib=ib: e.dma_start(
                    out=swf[:, ib, :], in_=w_h[k][i * 128:(i + 1) * 128, :]),
                       waits=((v_c, prev_cast[ib]),),
                       inc=lw[ib], inc_n=16)
                wl = lw[ib].v
                p.emit("vector", lambda e, k=k, i=i, ib=ib: e.tensor_copy(
                    out=sw[k][:, i, :], in_=swf[:, ib, :]),
                       waits=((lw[ib], wl),), inc=v_c)
                prev_cast[ib] = v_c.v
            wcast_done[k] = v_c.v
        p.emit("vector", lambda e: e.tensor_scalar(
            out=som["mur"], in0=smu["mur"], scalar1=-1.0, scalar2=1.0,
            op0=ALU.mult, op1=ALU.add), waits=((li, mu_ld),))
        p.emit("vector", lambda e: e.tensor_scalar(
            out=som["muk"], in0=smu["muk"], scalar1=-1.0, scalar2=1.0,
            op0=ALU.mult, op1=ALU.add), inc=v_c)
        p.emit("vector", lambda e: e.memset(sones, 1.0))
        p.emit("vector", lambda e: e.memset(sones_r, 1.0))
        p.emit("vector", lambda e: e.memset(seps, EPS), inc=v_c)
        ones_done = v_c.v

        for (s0, w) in strips(T_PAD, 512):
            t_snap, v_snap, s_snap = t_c.v, v_c.v, s_c.v
            for i in range(NCH):
                p.emit("sync", lambda e, i=i, s0=s0, w=w: e.dma_start(
                    out=sxs[:, i, :w], in_=x_h[i * 128:(i + 1) * 128, s0:s0 + w]),
                       waits=((v_c, v_snap),) if i == 0 else (),
                       inc=lx, inc_n=16)
            x_ld = lx.v
            emit_ln_rows(
                p, x_chunks=[sxs[:, i, :w] for i in range(NCH)], w=w,
                prow_m=prow_m[:, :w], prow_v=prow_v[:, :w], sq=ssq2[:, :w],
                row_a=srow_a[:, :w], row_b=srow_b[:, :w],
                ones=sones, t_c=t_c, v_c=v_c, s_c=s_c, eps_ap=seps,
                first_t_waits=((lx, x_ld), (v_c, max(ones_done, v_snap))),
                first_s_waits=((lx, x_ld), (t_c, t_snap), (v_c, v_snap)))
            rows_done = v_c.v
            p.emit("tensor", lambda e, w=w: e.matmul(
                pbcA[:, :w], sones_r, srow_a[:, :w], start=True, stop=True),
                   waits=((v_c, rows_done),))
            p.emit("tensor", lambda e, w=w: e.matmul(
                pbcB[:, :w], sones_r, srow_b[:, :w], start=True, stop=True),
                   inc=t_c)
            bc_done = t_c.v
            for i in range(NCH):
                p.emit("vector", lambda e, i=i, s0=s0, w=w: e.tensor_mul(
                    out=ssq[:, :w], in0=sxs[:, i, :w], in1=pbcA[:, :w]),
                       waits=((t_c, bc_done),) if i == 0 else ())
                p.emit("vector", lambda e, i=i, s0=s0, w=w: e.tensor_add(
                    out=sxln[:, i, s0:s0 + w], in0=ssq[:, :w], in1=pbcB[:, :w]),
                       inc=v_c)
            p.wait("tensor", v_c, v_c.v)
        xln_done = v_c.v

        for si, (s0, w) in enumerate(strips(T_OWN, 512)):
            c0 = s0 + 1
            t_snap = t_c.v
            v_snap2 = v_c.v
            for i in range(NCH):
                p.emit("sync", lambda e, i=i, c0=c0, w=w: e.dma_start(
                    out=sxo[:, i, :w], in_=x_h[i * 128:(i + 1) * 128, c0:c0 + w]),
                       waits=((v_c, v_snap2),) if i == 0 else (),
                       inc=lx, inc_n=16)
            xo_ld = lx.v
            for i in range(NCH):
                p.emit("vector", lambda e, i=i, c0=c0, w=w: e.tensor_scalar_mul(
                    out=ssq[:, :w], in0=sxln[:, i, c0:c0 + w],
                    scalar1=smu["mur"][:, i:i + 1]),
                       waits=((v_c, xln_done), (t_c, t_snap)) if i == 0 else ())
                p.emit("vector", lambda e, i=i, c0=c0, w=w: e.scalar_tensor_tensor(
                    out=sxr[:, i, :w], in0=sxln[:, i, c0 - 1:c0 - 1 + w],
                    scalar=som["mur"][:, i:i + 1], in1=ssq[:, :w],
                    op0=ALU.mult, op1=ALU.add))
                p.emit("vector", lambda e, i=i, c0=c0, w=w: e.tensor_scalar_mul(
                    out=ssq[:, :w], in0=sxln[:, i, c0:c0 + w],
                    scalar1=smu["muk"][:, i:i + 1]))
                p.emit("vector", lambda e, i=i, c0=c0, w=w: e.scalar_tensor_tensor(
                    out=sxk[:, i, :w], in0=sxln[:, i, c0 - 1:c0 - 1 + w],
                    scalar=som["muk"][:, i:i + 1], in1=ssq[:, :w],
                    op0=ALU.mult, op1=ALU.add), inc=v_c)
            mix_done = v_c.v

            for o in range(NCH):
                for i in range(NCH):
                    p.emit("tensor", lambda e, o=o, i=i, w=w: e.matmul(
                        pmm[:, :w], sw["wr"][:, i, o * 128:(o + 1) * 128],
                        sxr[:, i, :w], start=(i == 0), stop=(i == NCH - 1)),
                           waits=((v_c, max(mix_done, wcast_done["wr"])),)
                           if i == 0 else (),
                           inc=t_c if i == NCH - 1 else None)
                mmr = t_c.v
                p.emit("scalar", lambda e, o=o, w=w: e.activation(
                    out=sr[:, o, :w], in_=pmm[:, :w], func=AF.Sigmoid),
                       waits=((t_c, mmr),), inc=s_c)
                for i in range(NCH):
                    p.emit("tensor", lambda e, o=o, i=i, w=w: e.matmul(
                        pmm2[:, :w], sw["wk"][:, i, o * 128:(o + 1) * 128],
                        sxk[:, i, :w], start=(i == 0), stop=(i == NCH - 1)),
                           waits=((v_c, wcast_done["wv"]),) if (o == 0 and i == 0) else (),
                           inc=t_c if i == NCH - 1 else None)
                mmk = t_c.v
                p.emit("scalar", lambda e, o=o, w=w: e.activation(
                    out=ssq2[:, :w], in_=pmm2[:, :w], func=AF.Relu),
                       waits=((t_c, mmk),))
                p.emit("scalar", lambda e, o=o, w=w: e.activation(
                    out=sk2[:, o, :w], in_=ssq2[:, :w], func=AF.Square), inc=s_c)
                k2_done = s_c.v
                p.wait("tensor", s_c, k2_done)

            for o in range(NCH):
                for i in range(NCH):
                    p.emit("tensor", lambda e, o=o, i=i, w=w: e.matmul(
                        pmm[:, :w], sw["wv"][:, i, o * 128:(o + 1) * 128],
                        sk2[:, i, :w], start=(i == 0), stop=(i == NCH - 1)),
                           inc=t_c if i == NCH - 1 else None)
                mmv = t_c.v
                p.emit("vector", lambda e, o=o, w=w: e.tensor_mul(
                    out=sy[:, o, :w], in0=sr[:, o, :w], in1=pmm[:, :w]),
                       waits=((t_c, mmv),))
                p.emit("vector", lambda e, o=o, w=w: e.tensor_add(
                    out=sy[:, o, :w], in0=sy[:, o, :w], in1=sxo[:, o, :w]),
                       waits=((lx, xo_ld),) if o == 0 else (), inc=v_c)
                ydone = v_c.v
                p.wait("tensor", v_c, ydone)
                p.emit("gpsimd", lambda e, o=o, s0=s0, w=w: e.dma_start(
                    out=y_h[o * 128:(o + 1) * 128, s0:s0 + w], in_=sy[:, o, :w]),
                       waits=((v_c, ydone),), inc=st, inc_n=16)
            p.wait("vector", st, st.v)

        p.wait("gpsimd", st, st.v)
        p.finalize(block)
    return nc



def shard_tokens_fm(x_bt_d):
    out = []
    for c in range(NCORES):
        b, s = divmod(c, 4)
        s0 = s * T_OWN
        sl = np.zeros((D, T_PAD), dtype=np.float32)
        sl[:, 1:T] = x_bt_d[b, s0:s0 + T_OWN, :].T
        if s0 > 0:
            sl[:, 0] = x_bt_d[b, s0 - 1, :]
        out.append(np.ascontiguousarray(sl))
    return out


def mu_fm(mu):
    return np.ascontiguousarray(np.asarray(mu, np.float32).reshape(NCH, 128).T)


def l1_ffn_in_maps(x, W_r, W_k, W_v, mu_r, mu_k):
    wr = np.ascontiguousarray(np.asarray(W_r, np.float32).T)
    wk = np.ascontiguousarray(np.asarray(W_k, np.float32).T)
    wv = np.ascontiguousarray(np.asarray(W_v, np.float32).T)
    mur = mu_fm(mu_r)
    muk = mu_fm(mu_k)
    xs = shard_tokens_fm(x)
    return [dict(x=xs[c], wr=wr, wk=wk, wv=wv, mur=mur, muk=muk)
            for c in range(NCORES)]


def assemble_fm_out(results, key="y"):
    out = np.empty((BATCH, SEQ, D), dtype=np.float32)
    for c in range(NCORES):
        b, s = divmod(c, 4)
        out[b, s * T_OWN:(s + 1) * T_OWN, :] = results[c][key].T
    return out






def strip_barrier(p, t_c, v_c, s_c):
    ct, cv, cs = t_c.v, v_c.v, s_c.v
    p.wait("tensor", v_c, cv)
    p.wait("tensor", s_c, cs)
    p.wait("scalar", v_c, cv)
    p.wait("scalar", t_c, ct)
    p.wait("vector", t_c, ct)
    p.wait("vector", s_c, cs)


def emit_attn_b5(p, *, h, skne, sqne, svtm, psc, sp, pkv, snrm, sto, south,
                 sones_r, osc_h, t_c, v_c, s_c, g_c,
                 first_waits=(), qs_range=None, qne_col=None):
    if qs_range is None:
        qs_range = range(QSTRIPS)
    if qne_col is None:
        qne_col = lambda qs: qs * 512
    first = True
    for qs in qs_range:
        q0 = qs * 512
        qc = qne_col(qs)
        g_snap = g_c.v
        v_snap_b5 = v_c.v
        ngroups = KCH // 2
        fill_done = [0] * ngroups
        exp_done = [0] * ngroups
        for g in range(ngroups):
            gb = g % 2
            for cc in range(2):
                kchunk = g * 2 + cc
                k0 = kchunk * 128
                p.emit("tensor", lambda e, gb=gb, cc=cc, k0=k0, qc=qc: e.matmul(
                    psc[:, gb, cc * 512:(cc + 1) * 512],
                    skne[:, k0:k0 + 128], sqne[:, qc:qc + 512],
                    start=True, stop=True),
                       waits=tuple(
                           (list(first_waits) if (first and g == 0 and cc == 0) else [])
                           + ([(s_c, exp_done[g - 2])] if (g >= 2 and cc == 0) else [])),
                       inc=t_c if cc == 1 else None)
            fill_done[g] = t_c.v
            p.emit("scalar", lambda e, gb=gb: e.activation(
                out=sp[:, gb, :], in_=psc[:, gb, :], func=AF.Exp),
                   waits=((t_c, fill_done[g]),), inc=s_c)
            exp_done[g] = s_c.v
            for cc in range(2):
                kchunk = g * 2 + cc
                p.emit("tensor", lambda e, gb=gb, cc=cc, kchunk=kchunk: e.matmul(
                    pkv[0:65, :], svtm[:, kchunk, :], sp[:, gb, cc * 512:(cc + 1) * 512],
                    start=(kchunk == 0), stop=(kchunk == KCH - 1)),
                       waits=((s_c, exp_done[g]),) if cc == 0 else (),
                       inc=t_c if (cc == 1 and kchunk == KCH - 1) else None)
        pv_done = t_c.v
        p.emit("vector", lambda e: e.reciprocal(out=snrm[:, :512], in_=pkv[64:65, :512]),
               waits=((t_c, pv_done),), inc=v_c)
        den_done = v_c.v
        p.emit("scalar", lambda e: e.activation(
            out=sto[:, :], in_=pkv[0:64, :512], func=AF.Copy),
               waits=((t_c, pv_done), (v_c, v_snap_b5)), inc=s_c)
        out_evac = s_c.v
        p.emit("tensor", lambda e: e.matmul(
            psc[0:64, 0, 0:512],
            sones_r[:, 0:64], snrm[:, :512], start=True, stop=True),
               waits=((v_c, den_done), (s_c, exp_done[ngroups - 1]),), inc=t_c)
        den_bc = t_c.v
        p.emit("vector", lambda e: e.tensor_mul(
            out=south, in0=sto, in1=psc[0:64, 0, 0:512]),
               waits=((t_c, den_bc), (s_c, out_evac), (g_c, g_snap)), inc=v_c)
        south_done = v_c.v
        p.emit("gpsimd", lambda e, h=h, q0=q0: e.dma_start(
            out=osc_h[h * 64:(h + 1) * 64, q0:q0 + 512], in_=south),
               waits=((v_c, south_done),), inc=g_c, inc_n=16)
        p.wait("tensor", v_c, south_done)
        p.wait("scalar", v_c, south_done)
        first = False






NHEAD = 16
HD = 64
KCH = SEQ // 128
QSTRIPS = T_OWN // 512


def build_l1_attn():
    nc = bass.Bass("TRN2", target_bir_lowering=False, debug=False)
    xf_h = nc.dram_tensor("xf", [D, SEQ], F32, kind="ExternalInput").ap()
    wq_h = nc.dram_tensor("wq", [NHEAD, D, HD], F32, kind="ExternalInput").ap()
    wkv_h = nc.dram_tensor("wkv", [NHEAD, D, 128], F32, kind="ExternalInput").ap()
    ident_h = nc.dram_tensor("ident", [128, 128], BF16, kind="ExternalInput").ap()
    y_h = nc.dram_tensor("y", [D, T_OWN], F32, kind="ExternalOutput").ap()
    osc_h = nc.dram_tensor("osc", [D, T_OWN], BF16, kind="Internal").ap()

    es = ExitStack()
    sb = lambda n, s, d=F32: es.enter_context(nc.sbuf_tensor(n, s, d)).ap()
    psb = lambda n, s, d=F32: es.enter_context(nc.psum_tensor(n, s, d)).ap()
    with es:
        sxln = sb("sxln", [128, NCH, SEQ], BF16)
        skne = sb("skne", [65, SEQ], BF16)
        svtm = sb("svtm", [128, KCH, 65], BF16)
        sqne = sb("sqne", [65, T_OWN], BF16)
        swkv = sb("swkv", [128, 2, NCH, 128], BF16)
        swq = sb("swq", [128, 2, NCH, HD], BF16)
        swf = sb("swf", [128, 2, 128])
        sxf = sb("sxf", [128, NCH, 512])
        svfm = sb("svfm", [64, 512], BF16)
        sto = sb("sto", [64, 512], BF16)
        sp = sb("sp", [128, 2, 1024], BF16)
        snrm = sb("snrm", [1, 512])
        skmax = sb("skmax", [1, 1])
        srow_a = sb("srow_a", [1, 512])
        srow_b = sb("srow_b", [1, 512])
        ssq = sb("ssq", [128, 512])
        ssq2 = sb("ssq2", [128, 512])
        sident = sb("sident", [128, 128], BF16)
        sones = sb("sones", [128, 1])
        sones_r = sb("sones_r", [1, 128])
        seps = sb("seps", [1, 1])
        south = sb("south", [64, 512], BF16)
        sres = sb("sres", [128, NCH, 512], BF16)
        prow_m = psb("prow_m", [1, 512])
        prow_v = psb("prow_v", [1, 512])
        pkv = psb("pkv", [128, 512])
        ptr = psb("ptr", [128, 64], BF16)
        psc = psb("psc", [128, 2, 1024])

        p = Prog(nc, es)
        li = p.sem("li")
        lx = p.sem("lx")
        lw = [p.sem("lw0"), p.sem("lw1")]
        t_c = p.sem("t_c")
        v_c = p.sem("v_c")
        s_c = p.sem("s_c")
        g_c = p.sem("g_c")
        st = p.sem("st")

        block = es.enter_context(nc.Block())

        p.emit("sync", lambda e: e.dma_start(out=sident, in_=ident_h), inc=li, inc_n=16)
        p.emit("vector", lambda e: e.memset(sones, 1.0))
        p.emit("vector", lambda e: e.memset(sones_r, 1.0))
        p.emit("vector", lambda e: e.memset(seps, EPS))
        p.emit("vector", lambda e: e.memset(skne[64:65, :], 1.0))
        p.emit("vector", lambda e: e.memset(svtm[:, :, 64], 1.0), inc=v_c)
        init_done = v_c.v

        for (s0, w) in strips(SEQ, 512):
            t_snap, v_snap, s_snap = t_c.v, v_c.v, s_c.v
            for i in range(NCH):
                p.emit("sync", lambda e, i=i, s0=s0, w=w: e.dma_start(
                    out=sxf[:, i, :w], in_=xf_h[i * 128:(i + 1) * 128, s0:s0 + w]),
                       waits=((v_c, v_snap),) if i == 0 else (),
                       inc=lx, inc_n=16)
            x_ld = lx.v
            emit_ln_rows(
                p, x_chunks=[sxf[:, i, :w] for i in range(NCH)], w=w,
                prow_m=prow_m[:, :w], prow_v=prow_v[:, :w], sq=ssq2[:, :w],
                row_a=srow_a[:, :w], row_b=srow_b[:, :w],
                ones=sones, t_c=t_c, v_c=v_c, s_c=s_c, eps_ap=seps,
                first_t_waits=((lx, x_ld), (v_c, max(init_done, v_snap))),
                first_s_waits=((lx, x_ld), (t_c, t_snap), (v_c, v_snap)))
            rows_done = v_c.v
            p.emit("tensor", lambda e, w=w: e.matmul(
                pkv[:, :w], sones_r, srow_a[:, :w], start=True, stop=True),
                   waits=((v_c, rows_done),))
            p.emit("tensor", lambda e, w=w: e.matmul(
                psc[:, 0, :w], sones_r, srow_b[:, :w], start=True, stop=True),
                   inc=t_c)
            bc_done = t_c.v
            for i in range(NCH):
                p.emit("vector", lambda e, i=i, w=w: e.tensor_mul(
                    out=ssq[:, :w], in0=sxf[:, i, :w], in1=pkv[:, :w]),
                       waits=((t_c, bc_done),) if i == 0 else ())
                p.emit("vector", lambda e, i=i, s0=s0, w=w: e.tensor_add(
                    out=sxln[:, i, s0:s0 + w], in0=ssq[:, :w], in1=psc[:, 0, :w]),
                       inc=v_c)
            p.wait("tensor", v_c, v_c.v)
        xln_done = v_c.v

        for h in range(NHEAD):
            hb = h % 2
            cast_prev = [v_c.v, v_c.v]
            for i in range(NCH):
                ib = i % 2
                p.emit("sync", lambda e, h=h, i=i, ib=ib: e.dma_start(
                    out=swf[:, ib, :], in_=wkv_h[h, i * 128:(i + 1) * 128, :]),
                       waits=((v_c, cast_prev[ib]),), inc=lw[ib], inc_n=16)
                wld = lw[ib].v
                p.emit("vector", lambda e, hb=hb, i=i, ib=ib: e.tensor_copy(
                    out=swkv[:, hb, i, :], in_=swf[:, ib, :]),
                       waits=((lw[ib], wld),), inc=v_c)
                cast_prev[ib] = v_c.v
            kvw_done = v_c.v
            for i in range(NCH):
                ib = i % 2
                p.emit("sync", lambda e, h=h, i=i, ib=ib: e.dma_start(
                    out=swf[:, ib, :HD], in_=wq_h[h, i * 128:(i + 1) * 128, :]),
                       waits=((v_c, cast_prev[ib]),), inc=lw[ib], inc_n=16)
                wld = lw[ib].v
                p.emit("vector", lambda e, hb=hb, i=i, ib=ib: e.tensor_copy(
                    out=swq[:, hb, i, :], in_=swf[:, ib, :HD]),
                       waits=((lw[ib], wld),), inc=v_c)
                cast_prev[ib] = v_c.v
            qw_done = v_c.v

            p.emit("vector", lambda e: e.memset(skmax, 0.0), inc=v_c)
            prev_head_attn_t = t_c.v
            for (s0, w) in strips(SEQ, 512):
                strip_barrier(p, t_c, v_c, s_c)
                for i in range(NCH):
                    p.emit("tensor", lambda e, hb=hb, i=i, s0=s0, w=w: e.matmul(
                        pkv[:, :w], swkv[:, hb, i, :], sxln[:, i, s0:s0 + w],
                        start=(i == 0), stop=(i == NCH - 1)),
                           waits=((v_c, max(kvw_done, xln_done)),) if i == 0 else (),
                           inc=t_c if i == NCH - 1 else None)
                kv_mm = t_c.v
                p.emit("scalar", lambda e, s0=s0, w=w: e.activation(
                    out=skne[0:64, s0:s0 + w], in_=pkv[0:64, :w], func=AF.Copy),
                       waits=((t_c, kv_mm),))
                p.emit("scalar", lambda e, w=w: e.activation(
                    out=svfm[:, :w], in_=pkv[64:128, :w], func=AF.Copy), inc=s_c)
                kv_evac = s_c.v
                p.emit("scalar", lambda e, s0=s0, w=w: e.activation(
                    out=ssq2[0:64, :w], in_=pkv[0:64, :w], func=AF.Square), inc=s_c)
                ksq_done = s_c.v
                p.wait("tensor", s_c, kv_evac)
                p.emit("tensor", lambda e, w=w: e.matmul(
                    prow_v[:, :w], sones[0:64], ssq2[0:64, :w], start=True, stop=True),
                       waits=((s_c, ksq_done),), inc=t_c)
                knorm_mm = t_c.v
                p.emit("vector", lambda e, w=w: e.reduce_max(
                    out=snrm[:, 0:1], in_=prow_v[:, :w], axis=AX.X),
                       waits=((t_c, knorm_mm),))
                p.emit("vector", lambda e: e.tensor_max(
                    out=skmax, in0=skmax, in1=snrm[:, 0:1]), inc=v_c)
                for j in range(w // 128):
                    kchunk = s0 // 128 + j
                    p.emit("tensor", lambda e, j=j: e.transpose(
                        ptr[:, :], svfm[:, j * 128:(j + 1) * 128],
                        sident[0:64, 0:64]),
                           waits=((s_c, kv_evac),) if j == 0 else (), inc=t_c)
                    tr_done = t_c.v
                    p.emit("vector", lambda e, kchunk=kchunk: e.tensor_copy(
                        out=svtm[:, kchunk, 0:64], in_=ptr[:, :]),
                           waits=((t_c, tr_done),), inc=v_c)
                    p.wait("tensor", v_c, v_c.v)
                p.wait("scalar", t_c, t_c.v)
            kn_done_v = v_c.v
            kn_done_s = s_c.v

            p.emit("scalar", lambda e: e.activation(out=skmax, in_=skmax, func=AF.Ln),
                   waits=((v_c, kn_done_v),))
            p.emit("scalar", lambda e: e.activation(out=skmax, in_=skmax, func=AF.Exp,
                                                    scale=0.5), inc=s_c)
            kmax_done = s_c.v

            for (s0, w) in strips(T_OWN, 512):
                strip_barrier(p, t_c, v_c, s_c)
                for i in range(NCH):
                    p.emit("tensor", lambda e, hb=hb, i=i, s0=s0, w=w: e.matmul(
                        pkv[0:64, :w], swq[:, hb, i, :], sxln[:, i, s0:s0 + w],
                        start=(i == 0), stop=(i == NCH - 1)),
                           waits=((v_c, qw_done),) if i == 0 else (),
                           inc=t_c if i == NCH - 1 else None)
                q_mm = t_c.v
                p.emit("scalar", lambda e, s0=s0, w=w: e.activation(
                    out=sqne[0:64, s0:s0 + w], in_=pkv[0:64, :w], func=AF.Copy,
                    scale=0.125), waits=((t_c, q_mm),))
                p.emit("scalar", lambda e, w=w: e.activation(
                    out=ssq2[0:64, :w], in_=pkv[0:64, :w], func=AF.Square), inc=s_c)
                qsq_done = s_c.v
                p.wait("tensor", s_c, qsq_done)
                p.emit("tensor", lambda e, w=w: e.matmul(
                    prow_v[:, :w], sones[0:64], ssq2[0:64, :w], start=True, stop=True),
                       inc=t_c)
                qn_mm = t_c.v
                p.emit("scalar", lambda e, w=w: e.activation(
                    out=snrm[:, :w], in_=prow_v[:, :w], func=AF.Ln),
                       waits=((t_c, qn_mm),))
                p.emit("scalar", lambda e, w=w: e.activation(
                    out=snrm[:, :w], in_=snrm[:, :w], func=AF.Exp,
                    scale=0.5), inc=s_c)
                qrow_done = s_c.v
                p.wait("tensor", s_c, qrow_done)
                p.emit("vector", lambda e, s0=s0, w=w: e.tensor_scalar(
                    out=sqne[64:65, s0:s0 + w], in0=snrm[:, :w], scalar1=skmax,
                    scalar2=-0.125, op0=ALU.mult, op1=ALU.mult),
                       waits=((s_c, max(qrow_done, kmax_done)),))
                p.emit("vector", lambda e, s0=s0, w=w: e.tensor_scalar_add(
                    out=sqne[64:65, s0:s0 + w], in0=sqne[64:65, s0:s0 + w],
                    scalar1=-0.05), inc=v_c)
                p.wait("scalar", v_c, v_c.v)
            bias_done = v_c.v

            emit_attn_b5(p, h=h, skne=skne, sqne=sqne, svtm=svtm, psc=psc,
                         sp=sp, pkv=pkv, snrm=snrm, sto=sto, south=south,
                         sones_r=sones_r, osc_h=osc_h,
                         t_c=t_c, v_c=v_c, s_c=s_c, g_c=g_c,
                         first_waits=((v_c, bias_done), (s_c, kn_done_s)))
        attn_out_dma = g_c.v

        for (s0, w) in strips(T_OWN, 512):
            v_snap = v_c.v
            for i in range(NCH):
                p.emit("sync", lambda e, i=i, s0=s0, w=w: e.dma_start(
                    out=sxf[:, i, :w], in_=xf_h[i * 128:(i + 1) * 128, s0:s0 + w]),
                       waits=((v_c, v_snap),) if i == 0 else (), inc=lx, inc_n=16)
            x_ld2 = lx.v
            p.emit("sync", lambda e, s0=s0, w=w: e.dma_start(
                out=sres[:, :, :w],
                in_=osc_h.rearrange("(c p) t -> p c t", p=128)[:, :, s0:s0 + w]),
                   waits=((g_c, attn_out_dma),), inc=lx, inc_n=16)
            o_ld = lx.v
            for i in range(NCH):
                p.emit("vector", lambda e, i=i, w=w: e.tensor_add(
                    out=ssq[:, :w], in0=sxf[:, i, :w], in1=sres[:, i, :w]),
                       waits=((lx, max(x_ld2, o_ld)),) if i == 0 else ())
                p.emit("vector", lambda e, i=i, w=w: e.tensor_copy(
                    out=sxf[:, i, :w], in_=ssq[:, :w]), inc=v_c)
                ydone = v_c.v
                p.emit("gpsimd", lambda e, i=i, s0=s0, w=w: e.dma_start(
                    out=y_h[i * 128:(i + 1) * 128, s0:s0 + w], in_=sxf[:, i, :w]),
                       waits=((v_c, ydone),), inc=st, inc_n=16)
            p.wait("sync", st, st.v)

        p.wait("gpsimd", st, st.v)
        p.finalize(block)
    return nc


def l1_attn_in_maps(x2):
    maps = []
    for c in range(NCORES):
        b, s = divmod(c, 4)
        xf = np.ascontiguousarray(np.roll(x2[b], -s * T_OWN, axis=0).T)
        maps.append({"xf": xf})
    return maps


def l1_attn_weight_maps(Wq, Wk, Wv):
    wqT = np.asarray(Wq, np.float32).T
    wkT = np.asarray(Wk, np.float32).T
    wvT = np.asarray(Wv, np.float32).T
    wq = np.stack([np.ascontiguousarray(wqT[:, h * HD:(h + 1) * HD])
                   for h in range(NHEAD)])
    wkv = np.stack([np.ascontiguousarray(np.concatenate(
        [wkT[:, h * HD:(h + 1) * HD], wvT[:, h * HD:(h + 1) * HD]], axis=1))
        for h in range(NHEAD)])
    ident = np.eye(128, dtype=BF)
    return dict(wq=wq, wkv=wkv, ident=ident)


def assemble_l1_attn_out(results):
    out = np.empty((BATCH, SEQ, D), dtype=np.float32)
    for c in range(NCORES):
        b, s = divmod(c, 4)
        out[b, s * T_OWN:(s + 1) * T_OWN, :] = results[c]["y"].T
    return out



def build_l0_attn():
    nc = bass.Bass("TRN2", target_bir_lowering=False, debug=False)
    xf_h = nc.dram_tensor("xf", [D, SEQ], F32, kind="ExternalInput").ap()
    cos_h = nc.dram_tensor("costab", [32, SEQ], F32, kind="ExternalInput").ap()
    sin_h = nc.dram_tensor("sintab", [32, SEQ], F32, kind="ExternalInput").ap()
    wq_h = nc.dram_tensor("wq", [NHEAD, D, HD], F32, kind="ExternalInput").ap()
    wkv_h = nc.dram_tensor("wkv", [NHEAD, D, 128], F32, kind="ExternalInput").ap()
    qb_h = nc.dram_tensor("qb", [HD, NHEAD], F32, kind="ExternalInput").ap()
    kvb_h = nc.dram_tensor("kvb", [128, NHEAD], F32, kind="ExternalInput").ap()
    lnwb_h = nc.dram_tensor("lnwb", [HD, 2], F32, kind="ExternalInput").ap()
    qnw_h = nc.dram_tensor("qnw", [128, NCH], F32, kind="ExternalInput").ap()
    qnb_h = nc.dram_tensor("qnb", [128, NCH], F32, kind="ExternalInput").ap()
    kvnw_h = nc.dram_tensor("kvnw", [128, NCH], F32, kind="ExternalInput").ap()
    kvnb_h = nc.dram_tensor("kvnb", [128, NCH], F32, kind="ExternalInput").ap()
    rh_h = nc.dram_tensor("rh", [128, NCH], F32, kind="ExternalInput").ap()
    rhb_h = nc.dram_tensor("rhb", [1, 1], F32, kind="ExternalInput").ap()
    ow_h = nc.dram_tensor("ow", [D, D], F32, kind="ExternalInput").ap()
    ob_h = nc.dram_tensor("ob", [128, NCH], F32, kind="ExternalInput").ap()
    ident_h = nc.dram_tensor("ident", [128, 128], BF16, kind="ExternalInput").ap()
    y_h = nc.dram_tensor("y", [D, T_OWN], F32, kind="ExternalOutput").ap()
    osc_h = nc.dram_tensor("osc", [D, T_OWN], BF16, kind="Internal").ap()

    AW = 128

    es = ExitStack()
    sb = lambda n, s, d=F32: es.enter_context(nc.sbuf_tensor(n, s, d)).ap()
    psb = lambda n, s, d=F32: es.enter_context(nc.psum_tensor(n, s, d)).ap()
    with es:
        st_ = sb("st_", [128, NCH, SEQ], BF16)
        srad = sb("srad", [1, SEQ], BF16)
        skne = sb("skne", [65, SEQ], BF16)
        svtm = sb("svtm", [128, KCH, 65], BF16)
        sqne = sb("sqne", [65, 512], BF16)
        stabA = sb("stabA", [64, 512])
        stabB = sb("stabB", [64, 512])
        sswap = sb("sswap", [64, 512])
        swkv = sb("swkv", [128, NCH, 128], BF16)
        swq = sb("swq", [128, NCH, HD], BF16)
        swf = sb("swf", [128, 2, 128])
        sbias = sb("sbias", [128, 1])
        sqbias = sb("sqbias", [64, 1])
        skr = sb("skr", [64, 512])
        skrot = sb("skrot", [64, 512])
        svfm = sb("svfm", [64, 512], BF16)
        sto = svfm
        south = svfm
        sp = sb("sp", [128, 2, 1024], BF16)
        sxf = sb("sxf", [128, NCH, AW])
        ssq = sb("ssq", [128, 512])
        ssq2 = sb("ssq2", [128, 512])
        srow1 = sb("srow1", [1, 512])
        srow2 = sb("srow2", [1, 512])
        srow3 = sb("srow3", [1, 512])
        snrm = sb("snrm", [1, 512])
        skmax = sb("skmax", [1, 1])
        ssmall = sb("ssmall", [128, 16])
        sqb_s = sb("sqb_s", [64, 16])
        slnwb = sb("slnwb", [64, 2])
        slnq = sb("slnq", [64, 2])
        sqnw = sb("sqnw", [128, NCH])
        sqnb = sb("sqnb", [128, NCH])
        skvnw = sb("skvnw", [128, NCH])
        skvnb = sb("skvnb", [128, NCH])
        srh = sb("srh", [128, NCH])
        srhb = sb("srhb", [1, 1])
        srhb01 = sb("srhb01", [1, 1])
        sob = sb("sob", [128, NCH])
        sow = sb("sow", [128, 1, 128], BF16)
        sident = sb("sident", [64, 64], BF16)
        sones = sb("sones", [128, 1])
        sones_r = sb("sones_r", [1, 128])
        seps = sb("seps", [1, 1])
        sy = sb("sy", [128, AW])
        prow_m = psb("prow_m", [1, 512])
        prow_v = psb("prow_v", [1, 512])
        pkv = psb("pkv", [128, 512])
        ptr = psb("ptr", [128, 64], BF16)
        psc = psb("psc", [128, 2, 1024])

        p = Prog(nc, es)
        li = p.sem("li")
        lx = p.sem("lx")
        lt = p.sem("lt")
        lw = [p.sem("lw0"), p.sem("lw1")]
        t_c = p.sem("t_c")
        v_c = p.sem("v_c")
        s_c = p.sem("s_c")
        g_c = p.sem("g_c")
        st = p.sem("st")

        sres = sp.rearrange("p a b -> p (a b)").rearrange(
            "p (c t) -> p c t", c=NCH)[:, :, :AW]

        block = es.enter_context(nc.Block())

        for name, dst, srch in (("ident", sident, ident_h), ("kvb", ssmall, kvb_h),
                                ("qb", sqb_s, qb_h), ("lnwb", slnwb, lnwb_h),
                                ("qnw", sqnw, qnw_h), ("qnb", sqnb, qnb_h),
                                ("kvnw", skvnw, kvnw_h), ("kvnb", skvnb, kvnb_h),
                                ("rh", srh, rh_h), ("rhb", srhb, rhb_h),
                                ("ob", sob, ob_h)):
            p.emit("sync", lambda e, dst=dst, srch=srch: e.dma_start(
                out=dst, in_=srch[0:64, 0:64] if dst is sident else srch),
                   inc=li, inc_n=16)
        init_ld = li.v
        p.emit("vector", lambda e: e.memset(sones, 1.0))
        p.emit("vector", lambda e: e.memset(sones_r, 1.0))
        p.emit("vector", lambda e: e.memset(seps, EPS))
        p.emit("vector", lambda e: e.memset(skne[64:65, :], 1.0))
        p.emit("vector", lambda e: e.memset(svtm[:, :, 64], 1.0))
        p.emit("vector", lambda e: e.tensor_scalar_mul(out=srhb01, in0=srhb, scalar1=0.01),
               waits=((li, init_ld),))
        p.emit("vector", lambda e: e.tensor_scalar_mul(
            out=slnq[:, :], in0=slnwb[:, :], scalar1=0.125), inc=v_c)
        init_done = v_c.v

        p.emit("tensor", lambda e: e.matmul(prow_m[:, 0:NCH], sones, srh,
                                            start=True, stop=True),
               waits=((li, init_ld), (v_c, init_done)), inc=t_c)
        rhs_mm = t_c.v
        p.emit("vector", lambda e: e.reduce_sum(out=snrm[:, 0:1], in_=prow_m[:, 0:NCH],
                                                axis=AX.X), waits=((t_c, rhs_mm),))
        srhsum = sb("srhsum", [1, 1])
        p.emit("vector", lambda e: e.tensor_copy(out=srhsum, in_=snrm[:, 0:1]), inc=v_c)
        pre_done = v_c.v

        prev_t_end, prev_v_end, prev_s_end = t_c.v, pre_done, s_c.v
        for (s0, w) in strips(SEQ, AW):
            for i in range(NCH):
                p.emit("sync", lambda e, i=i, s0=s0, w=w: e.dma_start(
                    out=sxf[:, i, :w], in_=xf_h[i * 128:(i + 1) * 128, s0:s0 + w]),
                       waits=((v_c, prev_v_end), (t_c, prev_t_end), (s_c, prev_s_end))
                       if i == 0 else (),
                       inc=lx, inc_n=16)
            x_ld = lx.v
            for i in range(NCH):
                p.emit("tensor", lambda e, i=i, w=w: e.matmul(
                    prow_m[:, :w], sones, sxf[:, i, :w],
                    start=(i == 0), stop=(i == NCH - 1)),
                       waits=((lx, x_ld), (v_c, prev_v_end)) if i == 0 else (),
                       inc=t_c if i == NCH - 1 else None)
            m_done = t_c.v
            for i in range(NCH):
                p.emit("tensor", lambda e, i=i, w=w: e.matmul(
                    psc[0:1, 1, :w], srh[:, i:i + 1], sxf[:, i, :w],
                    start=(i == 0), stop=(i == NCH - 1)),
                       inc=t_c if i == NCH - 1 else None)
            r_done = t_c.v
            for i in range(NCH):
                p.emit("scalar", lambda e, i=i, w=w: e.activation(
                    out=ssq2[:, :w], in_=sxf[:, i, :w], func=AF.Square),
                       waits=((lx, x_ld), (t_c, prev_t_end), (v_c, prev_v_end))
                       if i == 0 else ((t_c, t_c.v),), inc=s_c)
                sq_i = s_c.v
                p.emit("tensor", lambda e, i=i, w=w: e.matmul(
                    prow_v[:, :w], sones, ssq2[:, :w],
                    start=(i == 0), stop=(i == NCH - 1)),
                       waits=((s_c, sq_i),), inc=t_c)
            v_done = t_c.v
            p.emit("vector", lambda e, w=w: e.tensor_scalar_mul(
                out=srow1[:, :w], in0=prow_m[:, :w], scalar1=1.0 / D),
                   waits=((t_c, m_done),))
            p.emit("vector", lambda e, w=w: e.tensor_scalar_mul(
                out=srow2[:, :w], in0=prow_v[:, :w], scalar1=1.0 / D),
                   waits=((t_c, v_done),))
            p.emit("vector", lambda e, w=w: e.tensor_mul(
                out=srow3[:, :w], in0=srow1[:, :w], in1=srow1[:, :w]))
            p.emit("vector", lambda e, w=w: e.tensor_sub(
                out=srow2[:, :w], in0=srow2[:, :w], in1=srow3[:, :w]), inc=v_c)
            var_done = v_c.v
            p.emit("scalar", lambda e, w=w: e.activation(
                out=srow3[:, :w], in_=srow2[:, :w], func=AF.Ln, bias=seps),
                   waits=((v_c, var_done),))
            p.emit("scalar", lambda e, w=w: e.activation(
                out=srow3[:, :w], in_=srow3[:, :w], func=AF.Exp, scale=-0.5), inc=s_c)
            rstd1_done = s_c.v
            p.emit("vector", lambda e, w=w: e.tensor_scalar_mul(
                out=snrm[:, :w], in0=srow1[:, :w], scalar1=srhsum),
                   waits=((s_c, rstd1_done), (t_c, r_done)))
            p.emit("vector", lambda e, w=w: e.tensor_sub(
                out=snrm[:, :w], in0=psc[0:1, 1, :w], in1=snrm[:, :w]))
            p.emit("vector", lambda e, w=w: e.tensor_mul(
                out=snrm[:, :w], in0=snrm[:, :w], in1=srow3[:, :w]))
            p.emit("vector", lambda e, w=w: e.tensor_scalar(
                out=snrm[:, :w], in0=snrm[:, :w], scalar1=0.01, scalar2=srhb01,
                op0=ALU.mult, op1=ALU.add))
            p.emit("vector", lambda e, w=w: e.tensor_scalar_max(
                out=snrm[:, :w], in0=snrm[:, :w], scalar1=0.5))
            p.emit("vector", lambda e, w=w: e.tensor_scalar_min(
                out=snrm[:, :w], in0=snrm[:, :w], scalar1=2.0))
            p.emit("vector", lambda e, s0=s0, w=w: e.tensor_copy(
                out=srad[0:1, s0:s0 + w], in_=snrm[:, :w]), inc=v_c)
            p.emit("vector", lambda e, w=w: e.tensor_mul(
                out=snrm[:, :w], in0=srow2[:, :w], in1=srow3[:, :w]))
            p.emit("vector", lambda e, w=w: e.tensor_mul(
                out=snrm[:, :w], in0=snrm[:, :w], in1=srow3[:, :w]), inc=v_c)
            var2_done = v_c.v
            p.emit("scalar", lambda e, w=w: e.activation(
                out=snrm[:, :w], in_=snrm[:, :w], func=AF.Ln, bias=seps),
                   waits=((v_c, var2_done),))
            p.emit("scalar", lambda e, w=w: e.activation(
                out=snrm[:, :w], in_=snrm[:, :w], func=AF.Exp, scale=-0.5), inc=s_c)
            rstd2_done = s_c.v
            p.emit("vector", lambda e, w=w: e.tensor_mul(
                out=srow3[:, :w], in0=srow3[:, :w], in1=snrm[:, :w]),
                   waits=((s_c, rstd2_done),))
            p.emit("vector", lambda e, w=w: e.tensor_mul(
                out=srow1[:, :w], in0=srow1[:, :w], in1=srow3[:, :w]))
            p.emit("vector", lambda e, w=w: e.tensor_scalar_mul(
                out=srow1[:, :w], in0=srow1[:, :w], scalar1=-1.0), inc=v_c)
            rows_done = v_c.v
            p.emit("tensor", lambda e, w=w: e.matmul(
                psc[:, 0, 0:512][:, :w], sones_r, srow3[:, :w], start=True, stop=True),
                   waits=((v_c, rows_done),))
            p.emit("tensor", lambda e, w=w: e.matmul(
                psc[:, 0, 512:1024][:, :w], sones_r, srow1[:, :w], start=True, stop=True),
                   inc=t_c)
            bc_done = t_c.v
            for i in range(NCH):
                p.emit("vector", lambda e, i=i, w=w: e.tensor_mul(
                    out=ssq[:, :w], in0=sxf[:, i, :w], in1=psc[:, 0, 0:512][:, :w]),
                       waits=((t_c, bc_done),) if i == 0 else ())
                p.emit("vector", lambda e, i=i, s0=s0, w=w: e.tensor_add(
                    out=st_[:, i, s0:s0 + w], in0=ssq[:, :w],
                    in1=psc[:, 0, 512:1024][:, :w]), inc=v_c)
            prev_t_end, prev_v_end, prev_s_end = t_c.v, v_c.v, s_c.v
        a_done_v, a_done_t, a_done_s = v_c.v, t_c.v, s_c.v

        sones_rb = sb("sones_rb", [1, 128], BF16)
        p.emit("vector", lambda e: e.memset(sones_rb, 1.0), inc=v_c)

        for h in range(NHEAD):
            p.emit("vector", lambda e: e.memset(skmax, 0.0), inc=v_c)
            swf_free = [v_c.v, v_c.v]
            swf_free_t = [t_c.v, t_c.v]
            for i in range(NCH):
                ib = i % 2
                p.emit("sync", lambda e, h=h, i=i, ib=ib: e.dma_start(
                    out=swf[:, ib, :], in_=wkv_h[h, i * 128:(i + 1) * 128, :]),
                       waits=((v_c, swf_free[ib]), (t_c, swf_free_t[ib])),
                       inc=lw[ib], inc_n=16)
                wld = lw[ib].v
                p.emit("tensor", lambda e, i=i, ib=ib: e.matmul(
                    pkv[:, 0:1], swf[:, ib, :], skvnb[:, i:i + 1],
                    start=(i == 0), stop=(i == NCH - 1)),
                       waits=((lw[ib], wld), (li, init_ld)), inc=t_c)
                swf_free_t[ib] = t_c.v
                p.emit("vector", lambda e, i=i, ib=ib: e.tensor_scalar_mul(
                    out=swkv[:, i, :], in0=swf[:, ib, :], scalar1=skvnw[:, i:i + 1]),
                       waits=((lw[ib], wld),), inc=v_c)
                swf_free[ib] = v_c.v
            kvb_mm = t_c.v
            p.emit("vector", lambda e, h=h: e.tensor_add(
                out=sbias, in0=pkv[:, 0:1], in1=ssmall[:, h:h + 1]),
                   waits=((t_c, kvb_mm),), inc=v_c)
            kvprep = v_c.v
            for i in range(NCH):
                ib = i % 2
                p.emit("sync", lambda e, h=h, i=i, ib=ib: e.dma_start(
                    out=swf[:, ib, 0:HD], in_=wq_h[h, i * 128:(i + 1) * 128, :]),
                       waits=((v_c, max(swf_free[ib], kvprep)), (t_c, swf_free_t[ib])),
                       inc=lw[ib], inc_n=16)
                wld = lw[ib].v
                p.emit("tensor", lambda e, i=i, ib=ib: e.matmul(
                    pkv[0:64, 0:1], swf[:, ib, 0:HD], sqnb[:, i:i + 1],
                    start=(i == 0), stop=(i == NCH - 1)),
                       waits=((lw[ib], wld), (v_c, kvprep)), inc=t_c)
                swf_free_t[ib] = t_c.v
                p.emit("vector", lambda e, i=i, ib=ib: e.tensor_scalar_mul(
                    out=swq[:, i, :], in0=swf[:, ib, 0:HD], scalar1=sqnw[:, i:i + 1]),
                       waits=((lw[ib], wld),), inc=v_c)
                swf_free[ib] = v_c.v
            qb_mm = t_c.v
            p.emit("vector", lambda e, h=h: e.tensor_add(
                out=sqbias, in0=pkv[0:64, 0:1], in1=sqb_s[:, h:h + 1]),
                   waits=((t_c, qb_mm),), inc=v_c)
            qprep = v_c.v

            def head_vec(side, s0, w, out_slice, lnw_pair, bias_ap):
                kv_mm = t_c.v
                p.emit("scalar", lambda e, w=w, bias_ap=bias_ap: e.activation(
                    out=skr[:, :w], in_=pkv[0:64, :w], func=AF.Identity, bias=bias_ap),
                       waits=((t_c, kv_mm),), inc=s_c)
                kr_evac = s_c.v
                tab_v_guard = v_c.v
                p.emit("sync", lambda e, s0=s0, w=w: e.dma_start(
                    out=stabA[0:32, :w], in_=cos_h[:, s0:s0 + w]),
                       waits=((v_c, tab_v_guard),), inc=lt, inc_n=16)
                p.emit("sync", lambda e, s0=s0, w=w: e.dma_start(
                    out=stabA[32:64, :w], in_=sin_h[:, s0:s0 + w]), inc=lt, inc_n=16)
                p.emit("sync", lambda e, s0=s0, w=w: e.dma_start(
                    out=stabB[0:32, :w], in_=sin_h[:, s0:s0 + w]), inc=lt, inc_n=16)
                p.emit("sync", lambda e, s0=s0, w=w: e.dma_start(
                    out=stabB[32:64, :w], in_=cos_h[:, s0:s0 + w]), inc=lt, inc_n=16)
                p.emit("sync", lambda e, w=w: e.dma_start(
                    out=sswap[0:32, :w], in_=skr[32:64, :w]),
                       waits=((s_c, kr_evac),), inc=lt, inc_n=16)
                p.emit("sync", lambda e, w=w: e.dma_start(
                    out=sswap[32:64, :w], in_=skr[0:32, :w]), inc=lt, inc_n=16)
                tab_ld = lt.v
                p.emit("tensor", lambda e, s0=s0, w=w: e.matmul(
                    psc[0:64, 0, 0:512][:, :w], sones_rb[:, 0:64],
                    srad[0:1, s0:s0 + w], start=True, stop=True), inc=t_c)
                rad_bc = t_c.v
                p.emit("vector", lambda e, w=w: e.tensor_mul(
                    out=ssq[0:32, :w], in0=skr[0:32, :w], in1=stabA[0:32, :w]),
                       waits=((s_c, kr_evac), (lt, tab_ld)))
                p.emit("vector", lambda e, w=w: e.tensor_mul(
                    out=ssq2[0:32, :w], in0=sswap[0:32, :w], in1=stabB[0:32, :w]))
                p.emit("vector", lambda e, w=w: e.tensor_sub(
                    out=skrot[0:32, :w], in0=ssq[0:32, :w], in1=ssq2[0:32, :w]))
                p.emit("vector", lambda e, w=w: e.tensor_mul(
                    out=ssq[32:64, :w], in0=sswap[32:64, :w], in1=stabA[32:64, :w]))
                p.emit("vector", lambda e, w=w: e.tensor_mul(
                    out=ssq2[32:64, :w], in0=skr[32:64, :w], in1=stabB[32:64, :w]))
                p.emit("vector", lambda e, w=w: e.tensor_add(
                    out=skrot[32:64, :w], in0=ssq[32:64, :w], in1=ssq2[32:64, :w]))
                p.emit("vector", lambda e, w=w: e.tensor_mul(
                    out=skrot[:, :w], in0=skrot[:, :w], in1=psc[0:64, 0, 0:512][:, :w]),
                       waits=((t_c, rad_bc),), inc=v_c)
                rot_done = v_c.v
                p.emit("tensor", lambda e, w=w: e.matmul(
                    prow_m[:, :w], sones[0:64], skrot[:, :w], start=True, stop=True),
                       waits=((v_c, rot_done),), inc=t_c)
                hm_done = t_c.v
                p.emit("scalar", lambda e, w=w: e.activation(
                    out=ssq2[0:64, :w], in_=skrot[:, :w], func=AF.Square),
                       waits=((v_c, rot_done),), inc=s_c)
                hsq_done = s_c.v
                p.emit("tensor", lambda e, w=w: e.matmul(
                    prow_v[:, :w], sones[0:64], ssq2[0:64, :w], start=True, stop=True),
                       waits=((s_c, hsq_done),), inc=t_c)
                hv_done = t_c.v
                p.emit("vector", lambda e, w=w: e.tensor_scalar_mul(
                    out=srow1[:, :w], in0=prow_m[:, :w], scalar1=1.0 / HD),
                       waits=((t_c, hm_done),))
                p.emit("vector", lambda e, w=w: e.tensor_scalar_mul(
                    out=srow2[:, :w], in0=prow_v[:, :w], scalar1=1.0 / HD),
                       waits=((t_c, hv_done),))
                p.emit("vector", lambda e, w=w: e.tensor_mul(
                    out=srow3[:, :w], in0=srow1[:, :w], in1=srow1[:, :w]))
                p.emit("vector", lambda e, w=w: e.tensor_sub(
                    out=srow2[:, :w], in0=srow2[:, :w], in1=srow3[:, :w]), inc=v_c)
                hvar_done = v_c.v
                p.emit("scalar", lambda e, w=w: e.activation(
                    out=srow2[:, :w], in_=srow2[:, :w], func=AF.Ln, bias=seps),
                       waits=((v_c, hvar_done),))
                p.emit("scalar", lambda e, w=w: e.activation(
                    out=srow2[:, :w], in_=srow2[:, :w], func=AF.Exp, scale=-0.5),
                       inc=s_c)
                hr_done = s_c.v
                p.emit("vector", lambda e, w=w: e.tensor_mul(
                    out=srow1[:, :w], in0=srow1[:, :w], in1=srow2[:, :w]),
                       waits=((s_c, hr_done),))
                p.emit("vector", lambda e, w=w: e.tensor_scalar_mul(
                    out=srow1[:, :w], in0=srow1[:, :w], scalar1=-1.0), inc=v_c)
                hrows_done = v_c.v
                p.emit("tensor", lambda e, w=w: e.matmul(
                    psc[0:64, 0, 512:1024][:, :w], sones_r[:, 0:64], srow2[:, :w],
                    start=True, stop=True), waits=((v_c, hrows_done),))
                p.emit("tensor", lambda e, w=w: e.matmul(
                    psc[0:64, 1, 0:512][:, :w], sones_r[:, 0:64], srow1[:, :w],
                    start=True, stop=True), inc=t_c)
                hbc_done = t_c.v
                p.emit("vector", lambda e, w=w: e.tensor_mul(
                    out=ssq[0:64, :w], in0=skrot[:, :w],
                    in1=psc[0:64, 0, 512:1024][:, :w]),
                       waits=((t_c, hbc_done),))
                p.emit("vector", lambda e, w=w: e.tensor_add(
                    out=ssq[0:64, :w], in0=ssq[0:64, :w],
                    in1=psc[0:64, 1, 0:512][:, :w]))
                p.emit("vector", lambda e, out_slice=out_slice, w=w,
                       lnw_pair=lnw_pair: e.tensor_scalar(
                    out=out_slice, in0=ssq[0:64, :w],
                    scalar1=lnw_pair[:, 0:1], scalar2=lnw_pair[:, 1:2],
                    op0=ALU.mult, op1=ALU.add), inc=v_c)
                return v_c.v

            for (s0, w) in strips(SEQ, 512):
                strip_barrier(p, t_c, v_c, s_c)
                for i in range(NCH):
                    p.emit("tensor", lambda e, i=i, s0=s0, w=w: e.matmul(
                        pkv[:, :w], swkv[:, i, :], st_[:, i, s0:s0 + w],
                        start=(i == 0), stop=(i == NCH - 1)),
                           waits=((v_c, max(kvprep, qprep, a_done_v)),)
                           if i == 0 else (),
                           inc=t_c if i == NCH - 1 else None)
                kv_mm = t_c.v
                p.emit("scalar", lambda e, w=w: e.activation(
                    out=svfm[:, :w], in_=pkv[64:128, :w], func=AF.Identity,
                    bias=sbias[64:128]), waits=((t_c, kv_mm),), inc=s_c)
                vfm_evac = s_c.v
                kn_done = head_vec("k", s0, w, skne[0:64, s0:s0 + w], slnwb,
                                   sbias[0:64])
                p.emit("scalar", lambda e, s0=s0, w=w: e.activation(
                    out=ssq2[0:64, :w], in_=skne[0:64, s0:s0 + w], func=AF.Square),
                       waits=((v_c, kn_done),), inc=s_c)
                knsq_done = s_c.v
                p.emit("tensor", lambda e, w=w: e.matmul(
                    prow_v[:, :w], sones[0:64], ssq2[0:64, :w], start=True, stop=True),
                       waits=((s_c, knsq_done),), inc=t_c)
                knn_mm = t_c.v
                p.emit("vector", lambda e, w=w: e.reduce_max(
                    out=snrm[:, 0:1], in_=prow_v[:, :w], axis=AX.X),
                       waits=((t_c, knn_mm),))
                p.emit("vector", lambda e: e.tensor_max(
                    out=skmax, in0=skmax, in1=snrm[:, 0:1]), inc=v_c)
                for j in range(w // 128):
                    kchunk = s0 // 128 + j
                    p.emit("tensor", lambda e, j=j: e.transpose(
                        ptr[:, :], svfm[:, j * 128:(j + 1) * 128],
                        sident),
                           waits=((s_c, vfm_evac),) if j == 0 else (), inc=t_c)
                    tr_done = t_c.v
                    p.emit("vector", lambda e, kchunk=kchunk: e.tensor_copy(
                        out=svtm[:, kchunk, 0:64], in_=ptr[:, :]),
                           waits=((t_c, tr_done),), inc=v_c)
                    p.wait("tensor", v_c, v_c.v)
                p.wait("scalar", t_c, t_c.v)
            kn_done_s = s_c.v
            p.emit("scalar", lambda e: e.activation(out=skmax, in_=skmax, func=AF.Ln),
                   waits=((v_c, v_c.v),))
            p.emit("scalar", lambda e: e.activation(out=skmax, in_=skmax, func=AF.Exp,
                                                    scale=0.5), inc=s_c)
            kmax_done = s_c.v

            for qs in range(QSTRIPS):
                q0 = qs * 512
                w = 512
                strip_barrier(p, t_c, v_c, s_c)
                for i in range(NCH):
                    p.emit("tensor", lambda e, i=i, q0=q0, w=w: e.matmul(
                        pkv[0:64, :w], swq[:, i, :], st_[:, i, q0:q0 + w],
                        start=(i == 0), stop=(i == NCH - 1)),
                           inc=t_c if i == NCH - 1 else None)
                q_done = head_vec("q", q0, w, sqne[0:64, 0:w], slnq, sqbias)
                p.emit("scalar", lambda e, w=w: e.activation(
                    out=ssq2[0:64, :w], in_=sqne[0:64, 0:w], func=AF.Square),
                       waits=((v_c, q_done),), inc=s_c)
                qsq_done = s_c.v
                p.emit("tensor", lambda e, w=w: e.matmul(
                    prow_v[:, :w], sones[0:64], ssq2[0:64, :w], start=True, stop=True),
                       waits=((s_c, qsq_done),), inc=t_c)
                qn_mm = t_c.v
                p.emit("scalar", lambda e, w=w: e.activation(
                    out=snrm[:, :w], in_=prow_v[:, :w], func=AF.Ln),
                       waits=((t_c, qn_mm),))
                p.emit("scalar", lambda e, w=w: e.activation(
                    out=snrm[:, :w], in_=snrm[:, :w], func=AF.Exp, scale=0.5),
                       inc=s_c)
                qrow_done = s_c.v
                p.wait("tensor", s_c, qrow_done)
                p.emit("vector", lambda e, w=w: e.tensor_scalar(
                    out=sqne[64:65, 0:w], in0=snrm[:, :w], scalar1=skmax,
                    scalar2=-1.0, op0=ALU.mult, op1=ALU.mult),
                       waits=((s_c, max(qrow_done, kmax_done)),))
                p.emit("vector", lambda e, w=w: e.tensor_scalar_add(
                    out=sqne[64:65, 0:w], in0=sqne[64:65, 0:w],
                    scalar1=-0.05), inc=v_c)
                p.wait("scalar", v_c, v_c.v)
                bias_done = v_c.v
                emit_attn_b5(p, h=h, skne=skne, sqne=sqne, svtm=svtm, psc=psc,
                             sp=sp, pkv=pkv, snrm=snrm, sto=sto, south=south,
                             sones_r=sones_r, osc_h=osc_h,
                             t_c=t_c, v_c=v_c, s_c=s_c, g_c=g_c,
                             first_waits=((v_c, bias_done), (s_c, kn_done_s)),
                             qs_range=[qs], qne_col=lambda qs: 0)
        attn_out_dma = g_c.v

        for (s0, w) in strips(T_OWN, AW):
            v_snap = v_c.v
            t_snap = t_c.v
            for i in range(NCH):
                p.emit("sync", lambda e, i=i, s0=s0, w=w: e.dma_start(
                    out=sxf[:, i, :w], in_=xf_h[i * 128:(i + 1) * 128, s0:s0 + w]),
                       waits=((v_c, v_snap),) if i == 0 else (), inc=lx, inc_n=16)
            p.emit("sync", lambda e, s0=s0, w=w: e.dma_start(
                out=sres[:, :, :w],
                in_=osc_h.rearrange("(c p) t -> p c t", p=128)[:, :, s0:s0 + w]),
                   waits=((g_c, attn_out_dma),), inc=lx, inc_n=16)
            c_ld = lx.v
            ow_free = [v_c.v, v_c.v]
            ow_free_t = [t_c.v, t_c.v]
            for o in range(NCH):
                for i in range(NCH):
                    ib = i % 2
                    p.emit("sync", lambda e, i=i, o=o, ib=ib: e.dma_start(
                        out=swf[:, ib, :], in_=ow_h[i * 128:(i + 1) * 128,
                                                    o * 128:(o + 1) * 128]),
                           waits=((v_c, ow_free[ib]), (t_c, ow_free_t[ib])),
                           inc=lw[ib], inc_n=16)
                    owld = lw[ib].v
                    p.emit("vector", lambda e, ib=ib: e.tensor_copy(
                        out=sow[:, 0, :], in_=swf[:, ib, :]),
                           waits=((lw[ib], owld), (t_c, ow_free_t[1 - ib])), inc=v_c)
                    cast_done = v_c.v
                    ow_free[ib] = cast_done
                    p.emit("tensor", lambda e, i=i, w=w: e.matmul(
                        psc[:, 0, 0:512][:, :w], sow[:, 0, :], sres[:, i, :w],
                        start=(i == 0), stop=(i == NCH - 1)),
                           waits=((v_c, cast_done),) + (((lx, c_ld), (s_c, s_c.v))
                                                        if i == 0 else ()),
                           inc=t_c)
                    ow_free_t[ib] = t_c.v
                o_mm = t_c.v
                p.emit("scalar", lambda e, o=o, w=w: e.activation(
                    out=ssq[:, :w], in_=psc[:, 0, 0:512][:, :w], func=AF.Identity,
                    bias=sob[:, o:o + 1]), waits=((t_c, o_mm),), inc=s_c)
                ob_done = s_c.v
                p.emit("vector", lambda e, o=o, w=w: e.tensor_add(
                    out=sy[:, :w], in0=ssq[:, :w], in1=sxf[:, o, :w]),
                       waits=((s_c, ob_done), (lx, c_ld)), inc=v_c)
                y_done = v_c.v
                p.wait("tensor", v_c, y_done)
                p.wait("scalar", v_c, y_done)
                p.emit("gpsimd", lambda e, o=o, s0=s0, w=w: e.dma_start(
                    out=y_h[o * 128:(o + 1) * 128, s0:s0 + w], in_=sy[:, :w]),
                       waits=((v_c, y_done),), inc=st, inc_n=16)
                p.wait("vector", st, st.v)
        p.wait("gpsimd", st, st.v)
        p.finalize(block)
    return nc



def _mel_tables():
    d2 = 32
    mel_max = 2595.0 * np.log10(1.0 + 4000.0 / 200.0)
    scale = np.power(10.0, np.linspace(0.0, mel_max, d2, dtype=np.float64) / 2595.0) - 1.0
    fb = (200.0 * scale / 1000.0).astype(np.float64)
    freqs = np.arange(SEQ, dtype=np.float64)[:, None] * fb[None, :]
    cos = np.cos(np.float32(freqs).astype(np.float64))
    sin = np.sin(np.float32(freqs).astype(np.float64))
    f32 = np.float32(freqs)
    cos = np.cos(f32.astype(np.float64)).astype(np.float32)
    sin = np.sin(f32.astype(np.float64)).astype(np.float32)
    return np.ascontiguousarray(cos.T), np.ascontiguousarray(sin.T)


_PERM = np.concatenate([np.arange(0, HD, 2), np.arange(1, HD, 2)])


def l0_attn_weight_maps(ins):
    q_w = np.asarray(ins["L0_attn_q_w"], np.float32)
    kv_w = np.asarray(ins["L0_attn_kv_w"], np.float32)
    q_b = np.asarray(ins["L0_attn_q_b"], np.float32)
    kv_b = np.asarray(ins["L0_attn_kv_b"], np.float32)
    wq = np.stack([np.ascontiguousarray(q_w[:, h * HD:(h + 1) * HD][:, _PERM])
                   for h in range(NHEAD)])
    wkv = np.stack([np.ascontiguousarray(np.concatenate(
        [kv_w[:, h * HD:(h + 1) * HD][:, _PERM],
         kv_w[:, D + h * HD:D + (h + 1) * HD]], axis=1)) for h in range(NHEAD)])
    qb = np.stack([q_b[h * HD:(h + 1) * HD][_PERM] for h in range(NHEAD)], axis=1)
    kvb = np.stack([np.concatenate([kv_b[h * HD:(h + 1) * HD][_PERM],
                                    kv_b[D + h * HD:D + (h + 1) * HD]])
                    for h in range(NHEAD)], axis=1)
    lnwb = np.stack([np.asarray(ins["L0_attn_ln_w"], np.float32)[_PERM],
                     np.asarray(ins["L0_attn_ln_b"], np.float32)[_PERM]], axis=1)
    chunked = lambda v: np.ascontiguousarray(
        np.asarray(v, np.float32).reshape(NCH, 128).T)
    return dict(
        wq=wq, wkv=wkv, qb=np.ascontiguousarray(qb),
        kvb=np.ascontiguousarray(kvb), lnwb=np.ascontiguousarray(lnwb),
        qnw=chunked(ins["L0_attn_qn_w"]), qnb=chunked(ins["L0_attn_qn_b"]),
        kvnw=chunked(ins["L0_attn_kvn_w"]), kvnb=chunked(ins["L0_attn_kvn_b"]),
        rh=chunked(ins["L0_attn_rh_w"][:, 0]),
        rhb=np.asarray(ins["L0_attn_rh_b"], np.float32).reshape(1, 1),
        ow=np.ascontiguousarray(np.asarray(ins["L0_attn_out_w"], np.float32)),
        ob=chunked(ins["L0_attn_out_b"]),
        ident=np.eye(128, dtype=BF))


def l0_attn_in_maps(x):
    cos_t, sin_t = _mel_tables()
    maps = []
    for c in range(NCORES):
        b, s = divmod(c, 4)
        r = s * T_OWN
        xf = np.ascontiguousarray(np.roll(x[b], -r, axis=0).T)
        maps.append({"xf": xf,
                     "costab": np.ascontiguousarray(np.roll(cos_t, -r, axis=1)),
                     "sintab": np.ascontiguousarray(np.roll(sin_t, -r, axis=1))})
    return maps



T_IN = 2560
Q0 = 512
NTILE = 16
KV_LO = 128
KV_W = T_IN - KV_LO


def build_l0f_attn(debug=False):
    nc = bass.Bass("TRN2", target_bir_lowering=False, debug=False)
    x_h = nc.dram_tensor("x", [D, T_IN], F32, kind="ExternalInput").ap()
    hm_h = nc.dram_tensor("hm", [1, T_IN], BF16, kind="ExternalInput").ap()
    wq_h = nc.dram_tensor("wqkvh", [NHEAD, D, 192], F32, kind="ExternalInput").ap()
    qsks_h = nc.dram_tensor("qsks", [HD, 2], F32, kind="ExternalInput").ap()
    wo_h = nc.dram_tensor("wo", [D, D], F32, kind="ExternalInput").ap()
    mtri_h = nc.dram_tensor("mtri", [128, 384], BF16, kind="ExternalInput").ap()
    mtri2_h = nc.dram_tensor("mtri2", [128, 384], BF16, kind="ExternalInput").ap()
    tq_a = nc.dram_tensor("tqA", [HD, T_IN], F32, kind="ExternalInput").ap()
    tq_b = nc.dram_tensor("tqB", [HD, T_IN], F32, kind="ExternalInput").ap()
    tk_a = nc.dram_tensor("tkA", [HD, T_IN], F32, kind="ExternalInput").ap()
    tk_b = nc.dram_tensor("tkB", [HD, T_IN], F32, kind="ExternalInput").ap()
    ident_h = nc.dram_tensor("ident", [128, 128], BF16, kind="ExternalInput").ap()
    y_h = nc.dram_tensor("y", [D, T_OWN], F32, kind="ExternalOutput").ap()
    if debug:
        dkne = nc.dram_tensor("dkne", [66, KV_W], BF16, kind="ExternalOutput").ap()
        dq2 = nc.dram_tensor("dq2", [66, T_OWN], BF16, kind="ExternalOutput").ap()
        dvtm = nc.dram_tensor("dvtm", [128, 19 * 65], BF16, kind="ExternalOutput").ap()
        dassem = nc.dram_tensor("dassem", [D, T_OWN], BF16, kind="ExternalOutput").ap()

    es = ExitStack()
    sb = lambda n, s, d=F32: es.enter_context(nc.sbuf_tensor(n, s, d)).ap()
    psb = lambda n, s, d=F32: es.enter_context(nc.psum_tensor(n, s, d)).ap()
    with es:
        xb = sb("xb", [128, NCH, T_IN], BF16)
        assem = sb("assem", [128, NCH, T_OWN], BF16)
        skne2 = sb("skne2", [66, KV_W], BF16)
        svtm2 = sb("svtm2", [128, 19, 65], BF16)
        sq2 = sb("sq2", [66, T_OWN], BF16)
        swh = sb("swh", [128, 2, NCH, 192], BF16)
        swf = sb("swf", [128, 2, 192])
        swo = sb("swo", [128, NCH, D], BF16)
        sqsks = sb("sqsks", [HD, 2])
        sqs8 = sb("sqs8", [HD, 1])
        skr = sb("skr", [64, KV_W])
        sswap = sb("sswap", [64, KV_W])
        svfm = sb("svfm", [64, 2, 512], BF16)
        stabA = sb("stabA", [64, KV_W])
        stabB = sb("stabB", [64, KV_W])
        srowf = sb("srowf", [1, KV_W])
        sout_h = sb("sout_h", [128, T_OWN], BF16)
        sden = sb("sden", [1, T_OWN])
        ssq = sb("ssq", [128, 512])
        ssq2 = sb("ssq2", [128, 512])
        snrm = sb("snrm", [1, 512])
        smaxneg = sb("smaxneg", [128, NTILE], BF16)
        smaxT = sb("smaxT", [NTILE, 128], BF16)
        sp = sb("sp", [128, 2, 384], BF16)
        sto = sb("sto", [128, 128], BF16)
        smtri = sb("smtri", [128, 384], BF16)
        smtri2 = sb("smtri2", [128, 384], BF16)
        sxf = sb("sxf", [128, NCH, 512])
        sy = sb("sy", [128, 512])
        sident = sb("sident", [128, 128], BF16)
        sones_r = sb("sones_r", [1, 128])
        sones = sb("sones", [128, 1])
        seps = sb("seps", [1, 1])
        pkv = psb("pkv", [128, 1024])
        pbc = psb("pbc", [64, 512])
        psc1 = psb("psc1", [128, 2, 512])
        psc2 = psb("psc2", [128, 2, 384])
        pdbc = psb("pdbc", [128, 512])
        ppv = pdbc[0:65, 0:128]

        p = Prog(nc, es)
        li = p.sem("li")
        lx = p.sem("lx")
        lt = p.sem("lt")
        lw = [p.sem("lw0"), p.sem("lw1")]
        t_c = p.sem("t_c")
        v_c = p.sem("v_c")
        s_c = p.sem("s_c")
        g_c = p.sem("g_c")
        st = p.sem("st")

        block = es.enter_context(nc.Block())

        p.emit("sync", lambda e: e.dma_start(out=sident, in_=ident_h), inc=li, inc_n=16)
        p.emit("sync", lambda e: e.dma_start(out=smtri, in_=mtri_h), inc=li, inc_n=16)
        p.emit("sync", lambda e: e.dma_start(out=smtri2, in_=mtri2_h), inc=li, inc_n=16)
        p.emit("sync", lambda e: e.dma_start(out=sqsks, in_=qsks_h), inc=li, inc_n=16)
        p.emit("sync", lambda e: e.dma_start(
            out=skne2[65:66, :], in_=hm_h[:, KV_LO:]),
               waits=((v_c, 1),), inc=li, inc_n=16)
        init_ld = li.v
        p.emit("vector", lambda e: e.memset(sones_r, 1.0))
        p.emit("vector", lambda e: e.memset(sones, 1.0))
        p.emit("vector", lambda e: e.memset(seps, EPS))
        p.emit("vector", lambda e: e.memset(skne2[64:66, :], 1.0), inc=v_c)
        p.emit("vector", lambda e: e.memset(sq2[64:66, :], 1.0))
        p.emit("vector", lambda e: e.memset(svtm2[:, :, 64], 1.0))
        p.emit("vector", lambda e: e.tensor_scalar_mul(
            out=sqs8, in0=sqsks[:, 0:1], scalar1=8.0),
               waits=((li, init_ld),), inc=v_c)
        init_done = v_c.v
        wo_free = [v_c.v, v_c.v]
        for i in range(NCH):
            for oc in range(NCH):
                ib = (i * NCH + oc) % 2
                p.emit("sync", lambda e, i=i, oc=oc, ib=ib: e.dma_start(
                    out=swf[:, ib, 0:128],
                    in_=wo_h[i * 128:(i + 1) * 128, oc * 128:(oc + 1) * 128]),
                       waits=((v_c, wo_free[ib]),), inc=lw[ib], inc_n=16)
                wld = lw[ib].v
                p.emit("vector", lambda e, i=i, oc=oc, ib=ib: e.tensor_copy(
                    out=swo[:, i, oc * 128:(oc + 1) * 128], in_=swf[:, ib, 0:128]),
                       waits=((lw[ib], wld),), inc=v_c)
                wo_free[ib] = v_c.v
        wo_done = v_c.v

        for (s0, w) in strips(T_IN, 512):
            v_snap = v_c.v
            for i in range(NCH):
                p.emit("sync", lambda e, i=i, s0=s0, w=w: e.dma_start(
                    out=sxf[:, i, :w], in_=x_h[i * 128:(i + 1) * 128, s0:s0 + w]),
                       waits=((v_c, v_snap),) if i == 0 else (), inc=lx, inc_n=16)
            x_ld = lx.v
            for i in range(NCH):
                p.emit("vector", lambda e, i=i, s0=s0, w=w: e.tensor_copy(
                    out=xb[:, i, s0:s0 + w], in_=sxf[:, i, :w]),
                       waits=((lx, x_ld),) if i == 0 else (), inc=v_c)
        xb_done = v_c.v

        def shifted_rhs(i, c0, w):
            if i < 4:
                return xb[:, i, c0:c0 + w]
            return xb[:, i, c0 - 1:c0 - 1 + w]

        def l2rot_full(lo, width, tabA_h, tabB_h, scal_ap, out_slice,
                       proj_cols, prev_done):
            p.emit("sync", lambda e, width=width, tabA_h=tabA_h, lo=lo: e.dma_start(
                out=stabA[:, :width], in_=tabA_h[:, lo:lo + width]),
                   waits=((v_c, prev_done),), inc=lt, inc_n=16)
            tab_ld = lt.v
            prev_sq = [s_c.v, s_c.v]
            for idx, (s0, w) in enumerate(strips(width, 512, start=lo)):
                r0 = s0 - lo
                sl = idx % 2
                for i in range(NCH):
                    p.emit("tensor", lambda e, i=i, s0=s0, w=w, sl=sl, hb=h % 2,
                           proj_cols=proj_cols: e.matmul(
                        pkv[0:64, sl * 512:sl * 512 + w], swh[:, hb, i, proj_cols],
                        shifted_rhs(i, s0, w),
                        start=(i == 0), stop=(i == NCH - 1)),
                           waits=((s_c, prev_sq[sl]), (v_c, prev_done))
                           if i == 0 else (),
                           inc=t_c if i == NCH - 1 else None)
                mm = t_c.v
                p.emit("scalar", lambda e, r0=r0, w=w, sl=sl: e.activation(
                    out=skr[:, r0:r0 + w], in_=pkv[0:64, sl * 512:sl * 512 + w],
                    func=AF.Copy),
                       waits=((t_c, mm),), inc=s_c)
                prev_sq[sl] = s_c.v
            proj_done = s_c.v
            p.emit("scalar", lambda e, width=width: e.activation(
                out=sswap[:, :width], in_=skr[:, :width], func=AF.Square),
                   inc=s_c)
            sq_done = s_c.v
            for (r0, w) in strips(width, 512):
                p.emit("tensor", lambda e, r0=r0, w=w: e.matmul(
                    pbc[0:1, :w], sones[0:64], sswap[:, r0:r0 + w],
                    start=True, stop=True),
                       waits=((s_c, sq_done),) if r0 == 0 else (), inc=t_c)
                nm = t_c.v
                p.emit("vector", lambda e, r0=r0, w=w: e.tensor_copy(
                    out=srowf[:, r0:r0 + w], in_=pbc[0:1, :w]),
                       waits=((t_c, nm),), inc=v_c)
                p.wait("tensor", v_c, v_c.v)
            ssq_rows = v_c.v
            p.emit("vector", lambda e, width=width: e.tensor_scalar_max(
                out=srowf[:, :width], in0=srowf[:, :width], scalar1=1e-24),
                   waits=((v_c, ssq_rows),), inc=v_c)
            clamp_done = v_c.v
            p.emit("scalar", lambda e, width=width: e.activation(
                out=srowf[:, :width], in_=srowf[:, :width], func=AF.Ln),
                   waits=((v_c, clamp_done),))
            p.emit("scalar", lambda e, width=width: e.activation(
                out=srowf[:, :width], in_=srowf[:, :width], func=AF.Exp,
                scale=-0.5), inc=s_c)
            rinv_done = s_c.v
            for (r0, w) in strips(width, 512):
                p.emit("tensor", lambda e, r0=r0, w=w: e.matmul(
                    pbc[:, :w], sones_r[:, 0:64], srowf[:, r0:r0 + w],
                    start=True, stop=True),
                       waits=((s_c, rinv_done),) if r0 == 0 else (), inc=t_c)
                bc = t_c.v
                p.emit("vector", lambda e, r0=r0, w=w: e.tensor_mul(
                    out=skr[:, r0:r0 + w], in0=skr[:, r0:r0 + w], in1=pbc[:, :w]),
                       waits=((t_c, bc),), inc=v_c)
                p.wait("tensor", v_c, v_c.v)
            p.emit("vector", lambda e, width=width, scal_ap=scal_ap:
                   e.tensor_scalar_mul(
                out=skr[:, :width], in0=skr[:, :width], scalar1=scal_ap), inc=v_c)
            scaled = v_c.v
            p.emit("sync", lambda e, width=width: e.dma_start(
                out=sswap[0:32, :width], in_=skr[32:64, :width]),
                   waits=((v_c, scaled),), inc=lt, inc_n=16)
            p.emit("sync", lambda e, width=width: e.dma_start(
                out=sswap[32:64, :width], in_=skr[0:32, :width]), inc=lt, inc_n=16)
            p.emit("sync", lambda e, width=width, tabB_h=tabB_h, lo=lo: e.dma_start(
                out=stabB[:, :width], in_=tabB_h[:, lo:lo + width]), inc=lt, inc_n=16)
            swap_ld = lt.v
            for (r0, w) in strips(width, 512):
                p.emit("vector", lambda e, r0=r0, w=w: e.tensor_mul(
                    out=ssq[0:32, :w], in0=skr[0:32, r0:r0 + w],
                    in1=stabA[0:32, r0:r0 + w]),
                       waits=((lt, swap_ld),) if r0 == 0 else ())
                p.emit("vector", lambda e, r0=r0, w=w: e.tensor_mul(
                    out=ssq2[0:32, :w], in0=sswap[0:32, r0:r0 + w],
                    in1=stabB[0:32, r0:r0 + w]))
                p.emit("vector", lambda e, r0=r0, w=w: e.tensor_sub(
                    out=ssq[0:32, :w], in0=ssq[0:32, :w], in1=ssq2[0:32, :w]))
                p.emit("vector", lambda e, r0=r0, w=w: e.tensor_mul(
                    out=ssq[32:64, :w], in0=sswap[32:64, r0:r0 + w],
                    in1=stabA[32:64, r0:r0 + w]))
                p.emit("vector", lambda e, r0=r0, w=w: e.tensor_mul(
                    out=ssq2[32:64, :w], in0=skr[32:64, r0:r0 + w],
                    in1=stabB[32:64, r0:r0 + w]))
                p.emit("vector", lambda e, r0=r0, w=w: e.tensor_add(
                    out=ssq[32:64, :w], in0=ssq[32:64, :w], in1=ssq2[32:64, :w]))
                p.emit("vector", lambda e, r0=r0, w=w, out_slice=out_slice:
                       e.tensor_copy(out=out_slice[:, r0:r0 + w], in_=ssq[0:64, :w]),
                       inc=v_c)
            return v_c.v

        for h in range(NHEAD):
            hb = h % 2
            wprev = [v_c.v, v_c.v]
            for i in range(NCH):
                ib = i % 2
                p.emit("sync", lambda e, h=h, i=i, ib=ib: e.dma_start(
                    out=swf[:, ib, :], in_=wq_h[h, i * 128:(i + 1) * 128, :]),
                       waits=((v_c, wprev[ib]),), inc=lw[ib], inc_n=16)
                wld = lw[ib].v
                p.emit("vector", lambda e, hb=hb, i=i, ib=ib: e.tensor_copy(
                    out=swh[:, hb, i, :], in_=swf[:, ib, :]),
                       waits=((lw[ib], wld),), inc=v_c)
                wprev[ib] = v_c.v
            w_done = v_c.v

            kdone = l2rot_full(KV_LO, KV_W, tk_a, tk_b, sqsks[:, 1:2],
                               skne2[0:64, :], slice(64, 128),
                               max(w_done, xb_done))
            prev_sq_v = [s_c.v, s_c.v]
            tcopy = [v_c.v, v_c.v]
            vstrips = list(enumerate(strips(KV_W, 512, start=KV_LO)))
            evacs = {}

            def emit_vtr(idx, s0, w):
                sl = idx % 2
                for j in range(w // 128):
                    kchunk = (s0 - KV_LO) // 128 + j
                    wt = [(v_c, tcopy[0])]
                    if j == 0:
                        wt.append((s_c, evacs[idx]))
                    p.emit("tensor", lambda e, sl=sl, j=j: e.transpose(
                        pdbc.bitcast(BF16)[:, 0:64],
                        svfm[:, sl, j * 128:(j + 1) * 128], sident[0:64, 0:64]),
                           waits=tuple(wt), inc=t_c)
                    trd = t_c.v
                    p.emit("vector", lambda e, kchunk=kchunk: e.tensor_copy(
                        out=svtm2[:, kchunk, 0:64],
                        in_=pdbc.bitcast(BF16)[:, 0:64]),
                           waits=((t_c, trd),), inc=v_c)
                    tcopy[0] = v_c.v

            for idx, (s0, w) in vstrips:
                sl = idx % 2
                for i in range(NCH):
                    p.emit("tensor", lambda e, hb=hb, i=i, s0=s0, w=w, sl=sl:
                           e.matmul(
                        pkv[0:64, sl * 512:sl * 512 + w], swh[:, hb, i, 128:192],
                        shifted_rhs(i, s0, w),
                        start=(i == 0), stop=(i == NCH - 1)),
                           waits=((s_c, prev_sq_v[sl]),) if i == 0 else (),
                           inc=t_c if i == NCH - 1 else None)
                vmm = t_c.v
                p.emit("scalar", lambda e, w=w, sl=sl: e.activation(
                    out=svfm[:, sl, :w], in_=pkv[0:64, sl * 512:sl * 512 + w],
                    func=AF.Copy),
                       waits=((t_c, vmm),), inc=s_c)
                prev_sq_v[sl] = s_c.v
                evacs[idx] = s_c.v
                if idx > 0:
                    emit_vtr(idx - 1, *vstrips[idx - 1][1])
            emit_vtr(vstrips[-1][0], *vstrips[-1][1])
            kn_done_s = s_c.v

            qdone = l2rot_full(Q0, T_OWN, tq_a, tq_b, sqs8,
                               sq2[0:64, :], slice(0, 64), kdone)

            p1_red = [0] * NTILE
            for t in range(NTILE):
                gb = t % 2
                c0 = Q0 + t * 128
                klo = c0 - 256 - KV_LO
                p.emit("tensor", lambda e, t=t, gb=gb, klo=klo: e.matmul(
                    psc1[:, gb, 0:384], sq2[0:64, t * 128:(t + 1) * 128],
                    skne2[0:64, klo:klo + 384], start=True, stop=True),
                       waits=tuple(([(v_c, v_c.v)] if t == 0 else [])
                                   + ([(v_c, p1_red[t - 2])] if t >= 2 else [])),
                       inc=t_c)
                p1mm = t_c.v
                p.emit("vector", lambda e, gb=gb: e.tensor_add(
                    out=psc1[:, gb, 0:384], in0=psc1[:, gb, 0:384], in1=smtri),
                       waits=((t_c, p1mm), (li, init_ld)))
                p.emit("vector", lambda e, t=t, gb=gb: e.reduce_max(
                    out=smaxneg[:, t:t + 1], in_=psc1[:, gb, 0:384], axis=AX.X,
                    negate=True), inc=v_c)
                p1_red[t] = v_c.v
            p.emit("tensor", lambda e: e.transpose(
                pdbc.bitcast(BF16)[0:NTILE, 0:128], smaxneg, sident),
                   waits=((v_c, v_c.v),), inc=t_c)
            trmax = t_c.v
            p.emit("vector", lambda e: e.tensor_copy(
                out=smaxT, in_=pdbc.bitcast(BF16)[0:NTILE, 0:128]),
                   waits=((t_c, trmax),), inc=v_c)
            maxT_done = v_c.v
            for t in range(NTILE):
                p.emit("gpsimd", lambda e, t=t: e.dma_start(
                    out=sq2[64:65, t * 128:(t + 1) * 128], in_=smaxT[t:t + 1, :]),
                       waits=((v_c, maxT_done),) if t == 0 else (), inc=g_c, inc_n=16)
            brow_ld = g_c.v

            exp_done = [0] * NTILE
            ev_done = [0] * NTILE
            for t in range(NTILE):
                gb = t % 2
                c0 = Q0 + t * 128
                klo = c0 - 256 - KV_LO
                for j in range(3):
                    p.emit("tensor", lambda e, t=t, gb=gb, klo=klo, j=j: e.matmul(
                        psc2[:, gb, j * 128:(j + 1) * 128],
                        skne2[:, klo + j * 128:klo + (j + 1) * 128],
                        sq2[:, t * 128:(t + 1) * 128], start=True, stop=True),
                           waits=tuple(
                               ([(g_c, brow_ld), (s_c, kn_done_s)] if (t == 0 and j == 0) else [])
                               + ([(s_c, exp_done[t - 2])] if (t >= 2 and j == 0) else [])
                               + ([(v_c, v_c.v)] if (t == 0 and j == 0) else [])),
                           inc=t_c if j == 2 else None)
                p2mm = t_c.v
                p.emit("vector", lambda e, gb=gb: e.tensor_add(
                    out=psc2[:, gb, :], in0=psc2[:, gb, :], in1=smtri2),
                       waits=((t_c, p2mm),), inc=v_c)
                mask_done = v_c.v
                p.emit("scalar", lambda e, gb=gb: e.activation(
                    out=sp[:, gb, :], in_=psc2[:, gb, :], func=AF.Exp),
                       waits=((v_c, mask_done),), inc=s_c)
                exp_done[t] = s_c.v
                for j in range(3):
                    kchunk = (klo // 128) + j
                    p.emit("tensor", lambda e, gb=gb, j=j, kchunk=kchunk: e.matmul(
                        ppv[:, :], svtm2[:, kchunk, :], sp[:, gb, j * 128:(j + 1) * 128],
                        start=(j == 0), stop=(j == 2)),
                           waits=tuple(([(s_c, exp_done[t])] if j == 0 else [])
                                       + ([(s_c, ev_done[t - 1])] if (j == 0 and t >= 1) else [])),
                           inc=t_c if j == 2 else None)
                pv_done = t_c.v
                p.emit("scalar", lambda e, t=t, ob=(h % 2) * 64: e.activation(
                    out=sout_h[ob:ob + 64, t * 128:(t + 1) * 128], in_=ppv[0:64, :],
                    func=AF.Copy), waits=((t_c, pv_done),))
                p.emit("scalar", lambda e, t=t: e.activation(
                    out=sden[:, t * 128:(t + 1) * 128], in_=ppv[64:65, :],
                    func=AF.Copy), inc=s_c)
                ev_done[t] = s_c.v
            p.emit("scalar", lambda e: e.activation(
                out=sden, in_=sden, func=AF.Ln), waits=((s_c, ev_done[-1]),))
            p.emit("scalar", lambda e: e.activation(
                out=sden, in_=sden, func=AF.Exp, scale=-1.0), inc=s_c)
            rden = s_c.v
            ob = (h % 2) * 64
            for (r0, w) in strips(T_OWN, 512):
                p.emit("tensor", lambda e, r0=r0, w=w, ob=ob: e.matmul(
                    pdbc[ob:ob + 64, :w],
                    sones_r[:, 0:64], sden[:, r0:r0 + w],
                    start=True, stop=True,
                    tile_position=(0, ob) if ob else None),
                       waits=((s_c, rden),) if r0 == 0 else (), inc=t_c)
                bcd = t_c.v
                p.emit("vector", lambda e, r0=r0, w=w, ob=ob, h=h: e.tensor_mul(
                    out=assem[ob:ob + 64, h // 2, r0:r0 + w],
                    in0=sout_h[ob:ob + 64, r0:r0 + w],
                    in1=pdbc[ob:ob + 64, :w]),
                       waits=((t_c, bcd),), inc=v_c)
                p.wait("tensor", v_c, v_c.v)
        heads_done_v = v_c.v
        if debug:
            p.wait("gpsimd", v_c, heads_done_v)
            for ci in range(NCH):
                p.emit("gpsimd", lambda e, ci=ci: e.dma_start(
                    out=dassem[ci * 128:(ci + 1) * 128, :], in_=assem[:, ci, :]),
                       inc=st, inc_n=16)

        for (s0, w) in strips(T_OWN, 512):
            v_snap = v_c.v
            for i in range(NCH):
                p.emit("sync", lambda e, i=i, s0=s0, w=w: e.dma_start(
                    out=sxf[:, i, :w], in_=x_h[i * 128:(i + 1) * 128, Q0 + s0:Q0 + s0 + w]),
                       waits=((v_c, v_snap),) if i == 0 else (), inc=lx, inc_n=16)
            x_ld = lx.v
            for o in range(NCH):
                for i in range(NCH):
                    p.emit("tensor", lambda e, i=i, o=o, s0=s0, w=w: e.matmul(
                        pkv[:, :w], swo[:, i, o * 128:(o + 1) * 128],
                        assem[:, i, s0:s0 + w], start=(i == 0), stop=(i == NCH - 1)),
                           waits=((v_c, max(heads_done_v, wo_done)),)
                           if (s0 == 0 and o == 0 and i == 0) else (),
                           inc=t_c if i == NCH - 1 else None)
                omm = t_c.v
                p.emit("scalar", lambda e, w=w: e.activation(
                    out=ssq[:, :w], in_=pkv[:, :w], func=AF.Copy),
                       waits=((t_c, omm),), inc=s_c)
                oevac = s_c.v
                p.emit("vector", lambda e, o=o, w=w: e.tensor_add(
                    out=sy[:, :w], in0=ssq[:, :w], in1=sxf[:, o, :w]),
                       waits=((s_c, oevac), (lx, x_ld)), inc=v_c)
                ydone = v_c.v
                p.wait("tensor", v_c, ydone)
                p.wait("scalar", v_c, ydone)
                p.emit("gpsimd", lambda e, o=o, s0=s0, w=w: e.dma_start(
                    out=y_h[o * 128:(o + 1) * 128, s0:s0 + w], in_=sy[:, :w]),
                       waits=((v_c, ydone),), inc=st, inc_n=16)
                p.wait("vector", st, st.v)
        p.wait("gpsimd", st, st.v)
        p.finalize(block)
    return nc


def _xpos_tables():
    half = np.arange(0, HD, 2, dtype=np.float64)
    inv_freq = 1.0 / 10000.0 ** (half / HD)
    base = (half + 0.4 * HD) / (1.4 * HD)
    pos = np.arange(-512, SEQ, dtype=np.float64)
    f32 = np.float32
    freqs = f32(pos[None, :]) * f32(inv_freq[:, None])
    cos = np.cos(freqs.astype(np.float64)).astype(f32)
    sin = np.sin(freqs.astype(np.float64)).astype(f32)
    power = f32((pos - SEQ // 2) / 128.0)
    scale = f32(base[:, None]) ** power[None, :]
    inv = (1.0 / scale).astype(f32)
    cs, ss = cos * scale, sin * scale
    ci, si = cos * inv, sin * inv
    stk = lambda a, b: np.ascontiguousarray(np.concatenate([a, b], axis=0))
    return (stk(cs, ss), stk(ss, cs), stk(ci, si), stk(si, ci))


def _l0f_mtri():
    i = np.arange(128)[:, None]
    m = np.zeros((128, 384), np.float32)
    jj0 = np.arange(128)[None, :]
    m[:, 0:128] = np.where(jj0 < i, -1e30, 0.0)
    m[:, 256:384] = np.where(jj0 > i, -1e30, 0.0)
    return m.astype(BF)


def _l0f_mtri2():
    m = np.asarray(_l0f_mtri(), np.float32)
    m2 = np.zeros((128, 384), np.float32)
    for j in range(3):
        m2[:, j * 128:(j + 1) * 128] = m[:, j * 128:(j + 1) * 128].T
    return m2.astype(BF)


def l0f_attn_layer_maps(ins, layer):
    Wqkv = np.asarray(ins["L0_ffn_Wqkv"], np.float32)[layer]
    wq = np.stack([np.ascontiguousarray(np.concatenate(
        [Wqkv[:, h * HD:(h + 1) * HD],
         Wqkv[:, D + h * HD:D + (h + 1) * HD],
         Wqkv[:, 2 * D + h * HD:2 * D + (h + 1) * HD]], axis=1))
        for h in range(NHEAD)])
    qsks = np.stack([np.asarray(ins["L0_ffn_q_scale"], np.float32)[layer],
                     np.asarray(ins["L0_ffn_k_scale"], np.float32)[layer]], axis=1)
    return dict(wqkvh=wq, qsks=np.ascontiguousarray(qsks),
                wo=np.ascontiguousarray(np.asarray(ins["L0_ffn_Wo"], np.float32)[layer]),
                mtri=_l0f_mtri(), mtri2=_l0f_mtri2(), ident=np.eye(128, dtype=BF))


def l0f_attn_in_maps(xcur):
    tqA, tqB, tkA, tkB = _XPOS_TABS
    maps = []
    for c in range(NCORES):
        b, s = divmod(c, 4)
        s0 = s * T_OWN
        xw = np.zeros((D, T_IN), np.float32)
        lo = s0 - 512
        src_lo = max(lo, 0)
        xw[:, src_lo - lo:] = xcur[b, src_lo:s0 + T_OWN, :].T
        hm = np.zeros((1, T_IN), np.float32)
        if lo < 0:
            hm[0, :-lo] = -1e30
        tab = lambda t: np.ascontiguousarray(t[:, 512 + lo:512 + lo + T_IN])
        maps.append(dict(x=np.ascontiguousarray(xw), hm=hm.astype(BF),
                         tqA=tab(tqA), tqB=tab(tqB), tkA=tab(tkA), tkB=tab(tkB)))
    return maps


_XPOS_TABS = _xpos_tables()



NHC = 22
T_F = T_OWN + 1


def build_l0f_ffn(debug=False):
    nc = bass.Bass("TRN2", target_bir_lowering=False, debug=False)
    x_h = nc.dram_tensor("x", [D, T_F], F32, kind="ExternalInput").ap()
    w1_h = nc.dram_tensor("w1", [D, 5460], F32, kind="ExternalInput").ap()
    w2_h = nc.dram_tensor("w2", [NHC * 128, D], F32, kind="ExternalInput").ap()
    b1_h = nc.dram_tensor("b1", [128, 2 * NHC], F32, kind="ExternalInput").ap()
    b2_h = nc.dram_tensor("b2", [128, NCH], F32, kind="ExternalInput").ap()
    lng_h = nc.dram_tensor("lng", [128, NCH], F32, kind="ExternalInput").ap()
    lnb_h = nc.dram_tensor("lnb", [128, NCH], F32, kind="ExternalInput").ap()
    y_h = nc.dram_tensor("y", [D, T_OWN], F32, kind="ExternalOutput").ap()
    if debug:
        dxs = nc.dram_tensor("dxs", [D, 256], BF16, kind="ExternalOutput").ap()
        dag = nc.dram_tensor("dag", [128, 256], BF16, kind="ExternalOutput").ap()
        db1 = nc.dram_tensor("db1", [128, 2 * NHC], F32, kind="ExternalOutput").ap()
        dw1 = nc.dram_tensor("dw1", [128, 256], BF16, kind="ExternalOutput").ap()

    HCW = [128] * (NHC - 1) + [42]

    es = ExitStack()
    sb = lambda n, s, d=F32: es.enter_context(nc.sbuf_tensor(n, s, d)).ap()
    psb = lambda n, s, d=F32: es.enter_context(nc.psum_tensor(n, s, d)).ap()
    with es:
        sx = sb("sx", [128, NCH, T_F])
        sw1 = sb("sw1", [128, NCH, 5460], BF16)
        sw2 = sb("sw2", [128, NHC, D], BF16)
        swf = sb("swf", [128, 2, 256])
        sxs = sb("sxs", [128, NCH, 256], BF16)
        sag = sb("sag", [128, 2, 256], BF16)
        sgel = sb("sgel", [128, 256])
        ssq = sb("ssq", [128, 256])
        ssq2 = sb("ssq2", [128, 256])
        sy = sb("sy", [128, 256])
        sb1 = sb("sb1", [128, 2 * NHC])
        sb2 = sb("sb2", [128, NCH])
        slng = sb("slng", [128, NCH])
        slnb = sb("slnb", [128, NCH])
        srow1 = sb("srow1", [1, 256])
        srow2 = sb("srow2", [1, 256])
        sones = sb("sones", [128, 1])
        sones_r = sb("sones_r", [1, 128])
        seps = sb("seps", [1, 1])
        pout = psb("pout", [128, NCH, 256])
        pha_d = psb("pha_d", [128, 1024])
        phg = psb("phg", [128, 256])
        prow_mv = psb("prow_mv", [1, 512])

        p = Prog(nc, es)
        li = p.sem("li")
        lx = p.sem("lx")
        lw = [p.sem("lw0"), p.sem("lw1")]
        t_c = p.sem("t_c")
        v_c = p.sem("v_c")
        s_c = p.sem("s_c")
        st = p.sem("st")

        block = es.enter_context(nc.Block())

        for name, dst, srch in (("b1", sb1, b1_h), ("b2", sb2, b2_h),
                                ("lng", slng, lng_h), ("lnb", slnb, lnb_h)):
            p.emit("sync", lambda e, dst=dst, srch=srch: e.dma_start(out=dst, in_=srch),
                   inc=li, inc_n=16)
        init_ld = li.v
        p.emit("vector", lambda e: e.memset(sones, 1.0))
        p.emit("vector", lambda e: e.memset(sones_r, 1.0))
        p.emit("vector", lambda e: e.memset(seps, EPS))
        p.emit("vector", lambda e: e.memset(sag, 0.0), inc=v_c)
        init_done = v_c.v

        for i in range(NCH):
            p.emit("sync", lambda e, i=i: e.dma_start(
                out=sx[:, i, :], in_=x_h[i * 128:(i + 1) * 128, :]), inc=lx, inc_n=16)
        x_ld = lx.v

        swf_free = [init_done, init_done]
        for i in range(NCH):
            for cg in range(22):
                cw = min(256, 5460 - cg * 256)
                ib = (i * 22 + cg) % 2
                p.emit("sync", lambda e, i=i, cg=cg, cw=cw, ib=ib: e.dma_start(
                    out=swf[:, ib, :cw], in_=w1_h[i * 128:(i + 1) * 128,
                                                  cg * 256:cg * 256 + cw]),
                       waits=((v_c, swf_free[ib]),),
                       inc=lw[ib], inc_n=16)
                wld = lw[ib].v
                p.emit("vector", lambda e, i=i, cg=cg, cw=cw, ib=ib: e.tensor_copy(
                    out=sw1[:, i, cg * 256:cg * 256 + cw], in_=swf[:, ib, :cw]),
                       waits=((lw[ib], wld),), inc=v_c)
                swf_free[ib] = v_c.v
        w1_done = v_c.v

        for k in range(NHC):
            for cg in range(4):
                ib = (k * 4 + cg) % 2
                p.emit("sync", lambda e, k=k, cg=cg, ib=ib: e.dma_start(
                    out=swf[:, ib, :], in_=w2_h[k * 128:(k + 1) * 128,
                                                cg * 256:(cg + 1) * 256]),
                       waits=((v_c, swf_free[ib]),),
                       inc=lw[ib], inc_n=16)
                wld = lw[ib].v
                p.emit("vector", lambda e, k=k, cg=cg, ib=ib: e.tensor_copy(
                    out=sw2[:, k, cg * 256:(cg + 1) * 256], in_=swf[:, ib, :]),
                       waits=((lw[ib], wld),), inc=v_c)
                swf_free[ib] = v_c.v
        w2_done = v_c.v

        for si in range(8):
            c0 = 1 + si * 256
            w = 256
            strip_barrier(p, t_c, v_c, s_c)

            def xsh(i, a, b):
                if i < 4:
                    return sx[:, i, a:b]
                return sx[:, i, a - 1:b - 1]

            for i in range(NCH):
                p.emit("tensor", lambda e, i=i, c0=c0: e.matmul(
                    prow_mv[:, 0:256], sones, xsh(i, c0, c0 + 256),
                    start=(i == 0), stop=(i == NCH - 1)),
                       waits=((lx, x_ld), (v_c, init_done)) if i == 0 else (),
                       inc=t_c if i == NCH - 1 else None)
            m_done = t_c.v
            p.emit("vector", lambda e: e.tensor_scalar_mul(
                out=srow1, in0=prow_mv[:, 0:256], scalar1=1.0 / D),
                   waits=((t_c, m_done),), inc=v_c)
            mread = v_c.v
            for i in range(NCH):
                p.emit("scalar", lambda e, i=i, c0=c0: e.activation(
                    out=ssq2, in_=xsh(i, c0, c0 + 256), func=AF.Square),
                       waits=((lx, x_ld),) if i == 0 else ((t_c, t_c.v),), inc=s_c)
                sqd = s_c.v
                p.emit("tensor", lambda e, i=i: e.matmul(
                    prow_mv[:, 256:512], sones, ssq2,
                    start=(i == 0), stop=(i == NCH - 1)),
                       waits=((s_c, sqd), (v_c, mread)) if i == 0
                       else ((s_c, sqd),),
                       inc=t_c)
            v_done = t_c.v
            p.emit("vector", lambda e: e.tensor_scalar_mul(
                out=srow2, in0=prow_mv[:, 256:512], scalar1=1.0 / D),
                   waits=((t_c, v_done),))
            p.emit("vector", lambda e: e.tensor_mul(out=ssq[0:1, :], in0=srow1,
                                                    in1=srow1))
            p.emit("vector", lambda e: e.tensor_sub(out=srow2, in0=srow2,
                                                    in1=ssq[0:1, :]), inc=v_c)
            var_done = v_c.v
            p.emit("scalar", lambda e: e.activation(out=srow2, in_=srow2, func=AF.Ln,
                                                    bias=seps),
                   waits=((v_c, var_done),))
            p.emit("scalar", lambda e: e.activation(out=srow2, in_=srow2, func=AF.Exp,
                                                    scale=-0.5), inc=s_c)
            rstd_done = s_c.v
            p.emit("vector", lambda e: e.tensor_mul(out=srow1, in0=srow1, in1=srow2),
                   waits=((s_c, rstd_done),))
            p.emit("vector", lambda e: e.tensor_scalar_mul(out=srow1, in0=srow1,
                                                           scalar1=-1.0), inc=v_c)
            rows_done = v_c.v
            p.emit("tensor", lambda e: e.matmul(pha_d[:, 0:256], sones_r, srow2,
                                                start=True, stop=True),
                   waits=((v_c, rows_done),))
            p.emit("tensor", lambda e: e.matmul(phg[:, :], sones_r, srow1,
                                                start=True, stop=True), inc=t_c)
            bc_done = t_c.v
            for i in range(NCH):
                p.emit("vector", lambda e, i=i, c0=c0: e.tensor_mul(
                    out=ssq, in0=xsh(i, c0, c0 + 256), in1=pha_d[:, 0:256]),
                       waits=((t_c, bc_done),) if i == 0 else ())
                p.emit("vector", lambda e, i=i: e.tensor_add(
                    out=ssq, in0=ssq, in1=phg[:, :]))
                p.emit("vector", lambda e, i=i: e.tensor_scalar_mul(
                    out=ssq, in0=ssq, scalar1=slng[:, i:i + 1]))
                p.emit("vector", lambda e, i=i: e.tensor_scalar_add(
                    out=sxs[:, i, :], in0=ssq, scalar1=slnb[:, i:i + 1]), inc=v_c)
            xs_done = v_c.v
            if debug and si == 0:
                p.wait("gpsimd", v_c, xs_done)
                for i in range(NCH):
                    p.emit("gpsimd", lambda e, i=i: e.dma_start(
                        out=dxs[i * 128:(i + 1) * 128, :], in_=sxs[:, i, :]),
                           waits=((v_c, xs_done),), inc=st, inc_n=16)
                p.emit("gpsimd", lambda e: e.dma_start(out=db1, in_=sb1),
                       inc=st, inc_n=16)
                p.emit("gpsimd", lambda e: e.dma_start(out=dw1, in_=sw1[:, 0, 0:256]),
                       inc=st, inc_n=16)

            ag_done = {}
            gel_done = {}
            w2k_done = {}

            def emit_w2(k):
                for oc in range(NCH):
                    p.emit("tensor", lambda e, k=k, oc=oc: e.matmul(
                        pout[:, oc, :], sw2[:, k, oc * 128:(oc + 1) * 128],
                        sag[:, k % 2, :],
                        start=(k == 0 and oc % 2 == 0), stop=(k == NHC - 1)),
                           waits=((v_c, max(ag_done[k], w2_done)),)
                           if oc == 0 else (),
                           inc=t_c if oc == NCH - 1 else None)
                w2k_done[k] = t_c.v

            for k in range(NHC):
                wk = HCW[k]
                kb = k % 2
                a0 = 128 * k
                g0 = 2730 + 128 * k
                pa = pha_d[:, kb * 512:kb * 512 + 256]
                if k >= 1:
                    emit_w2(k - 1)
                for i in range(NCH):
                    w_ = []
                    if i == 0:
                        if k >= 2:
                            w_.append((v_c, ag_done[k - 2]))
                        if k == 0:
                            w_.append((v_c, max(w1_done, xs_done)))
                    p.emit("tensor", lambda e, i=i, a0=a0, wk=wk, pa=pa: e.matmul(
                        pa[0:wk, :], sw1[:, i, a0:a0 + wk], sxs[:, i, :],
                        start=(i == 0), stop=(i == NCH - 1)),
                           waits=tuple(w_), inc=t_c if i == NCH - 1 else None)
                for i in range(NCH):
                    w_ = []
                    if i == 0:
                        if k >= 1:
                            w_.append((s_c, gel_done[k - 1]))
                        if k == 0:
                            w_.append((v_c, xs_done))
                    p.emit("tensor", lambda e, i=i, g0=g0, wk=wk: e.matmul(
                        phg[0:wk, :], sw1[:, i, g0:g0 + wk], sxs[:, i, :],
                        start=(i == 0), stop=(i == NCH - 1)),
                           waits=tuple(w_), inc=t_c if i == NCH - 1 else None)
                hg_done = t_c.v
                gw = [(t_c, hg_done)]
                if k >= 1:
                    gw.append((v_c, ag_done[k - 1]))
                if si == 0 and k == 0:
                    gw.append((li, init_ld))
                p.emit("scalar", lambda e, k=k, wk=wk: e.activation(
                    out=sgel[0:wk, :], in_=phg[0:wk, :], func=AF.Gelu,
                    bias=sb1[0:wk, NHC + k:NHC + k + 1]),
                       waits=tuple(gw), inc=s_c)
                gel_done[k] = s_c.v
                sw_ = [(s_c, gel_done[k])]
                if k >= 2:
                    sw_.append((t_c, w2k_done[k - 2]))
                p.emit("vector", lambda e, k=k, wk=wk, kb=kb, pa=pa:
                       e.scalar_tensor_tensor(
                    out=sag[0:wk, kb, :], in0=pa[0:wk, :],
                    scalar=sb1[0:wk, k:k + 1], in1=sgel[0:wk, :],
                    op0=ALU.add, op1=ALU.mult),
                       waits=tuple(sw_), inc=v_c)
                ag_done[k] = v_c.v
                if debug and si == 0 and k == 0:
                    p.emit("gpsimd", lambda e: e.dma_start(out=dag, in_=sag[:, 0, :]),
                           waits=((v_c, ag_done[k]),), inc=st, inc_n=16)
                    p.wait("tensor", st, st.v)
            emit_w2(NHC - 1)
            ffn_done = w2k_done[NHC - 1]
            for oc in range(NCH):
                p.emit("scalar", lambda e, oc=oc: e.activation(
                    out=ssq2, in_=pout[:, oc, :], func=AF.Identity,
                    bias=sb2[:, oc:oc + 1]),
                       waits=((t_c, ffn_done),) if oc == 0 else (), inc=s_c)
                ev_done = s_c.v
                p.emit("vector", lambda e, oc=oc, c0=c0: e.tensor_add(
                    out=sy, in0=ssq2, in1=sx[:, oc, c0:c0 + 256]),
                       waits=((s_c, ev_done),), inc=v_c)
                y_done = v_c.v
                p.wait("scalar", v_c, y_done)
                p.emit("gpsimd", lambda e, oc=oc, si=si: e.dma_start(
                    out=y_h[oc * 128:(oc + 1) * 128, si * 256:(si + 1) * 256],
                    in_=sy), waits=((v_c, y_done),), inc=st, inc_n=16)
                p.wait("vector", st, st.v)
        p.wait("gpsimd", st, st.v)
        p.finalize(block)
    return nc


def l0f_ffn_layer_maps(ins, layer):
    w1 = np.ascontiguousarray(np.asarray(ins["L0_ffn_W1"], np.float32)[layer])
    w2r = np.asarray(ins["L0_ffn_W2"], np.float32)[layer]
    w2 = np.zeros((NHC * 128, D), np.float32)
    w2[:2730] = w2r
    b1r = np.asarray(ins["L0_ffn_b1"], np.float32)[layer]
    b1 = np.zeros((128, 2 * NHC), np.float32)
    for k in range(NHC):
        wk = min(128, 2730 - 128 * k)
        b1[:wk, k] = b1r[128 * k:128 * k + wk]
        b1[:wk, NHC + k] = b1r[2730 + 128 * k:2730 + 128 * k + wk]
    chunked = lambda v: np.ascontiguousarray(
        np.asarray(v, np.float32).reshape(NCH, 128).T)
    return dict(w1=w1, w2=np.ascontiguousarray(w2), b1=b1,
                b2=chunked(ins["L0_ffn_b2"][layer]),
                lng=chunked(ins["L0_ffn_ln_g"][layer]),
                lnb=chunked(ins["L0_ffn_ln_b"][layer]))


def l0f_ffn_in_maps(xmid):
    maps = []
    for c in range(NCORES):
        b, s = divmod(c, 4)
        s0 = s * T_OWN
        xw = np.zeros((D, T_F), np.float32)
        lo = s0 - 1
        src_lo = max(lo, 0)
        xw[:, src_lo - lo:] = xmid[b, src_lo:s0 + T_OWN, :].T
        maps.append({"x": np.ascontiguousarray(xw)})
    return maps



def build_ln():
    nc = bass.Bass("TRN2", target_bir_lowering=False, debug=False)
    x_h = nc.dram_tensor("x", [D, T_OWN], F32, kind="ExternalInput").ap()
    y_h = nc.dram_tensor("y", [D, T_OWN], F32, kind="ExternalOutput").ap()
    es = ExitStack()
    sb = lambda n, s, d=F32: es.enter_context(nc.sbuf_tensor(n, s, d)).ap()
    psb = lambda n, s, d=F32: es.enter_context(nc.psum_tensor(n, s, d)).ap()
    with es:
        sxf = sb("sxf", [128, NCH, 512])
        ssq = sb("ssq", [128, 512])
        ssq2 = sb("ssq2", [128, 512])
        sy = sb("sy", [128, 512])
        srow_a = sb("srow_a", [1, 512])
        srow_b = sb("srow_b", [1, 512])
        sones = sb("sones", [128, 1])
        sones_r = sb("sones_r", [1, 128])
        seps = sb("seps", [1, 1])
        prow_m = psb("prow_m", [1, 512])
        prow_v = psb("prow_v", [1, 512])
        pbcA = psb("pbcA", [128, 512])
        pbcB = psb("pbcB", [128, 512])
        p = Prog(nc, es)
        lx = p.sem("lx")
        t_c = p.sem("t_c")
        v_c = p.sem("v_c")
        s_c = p.sem("s_c")
        st = p.sem("st")
        block = es.enter_context(nc.Block())
        p.emit("vector", lambda e: e.memset(sones, 1.0))
        p.emit("vector", lambda e: e.memset(sones_r, 1.0))
        p.emit("vector", lambda e: e.memset(seps, EPS), inc=v_c)
        init_done = v_c.v
        for (s0, w) in strips(T_OWN, 512):
            t_snap, v_snap, s_snap = t_c.v, v_c.v, s_c.v
            for i in range(NCH):
                p.emit("sync", lambda e, i=i, s0=s0, w=w: e.dma_start(
                    out=sxf[:, i, :w], in_=x_h[i * 128:(i + 1) * 128, s0:s0 + w]),
                       waits=((v_c, max(v_snap, init_done)), (st, st.v))
                       if i == 0 else (),
                       inc=lx, inc_n=16)
            x_ld = lx.v
            emit_ln_rows(
                p, x_chunks=[sxf[:, i, :w] for i in range(NCH)], w=w,
                prow_m=prow_m[:, :w], prow_v=prow_v[:, :w], sq=ssq2[:, :w],
                row_a=srow_a[:, :w], row_b=srow_b[:, :w],
                ones=sones, t_c=t_c, v_c=v_c, s_c=s_c, eps_ap=seps,
                first_t_waits=((lx, x_ld), (v_c, max(init_done, v_snap))),
                first_s_waits=((lx, x_ld), (t_c, t_snap), (v_c, v_snap)))
            rows_done = v_c.v
            p.emit("tensor", lambda e, w=w: e.matmul(
                pbcA[:, :w], sones_r, srow_a[:, :w], start=True, stop=True),
                   waits=((v_c, rows_done),))
            p.emit("tensor", lambda e, w=w: e.matmul(
                pbcB[:, :w], sones_r, srow_b[:, :w], start=True, stop=True), inc=t_c)
            bc_done = t_c.v
            for i in range(NCH):
                p.emit("vector", lambda e, i=i, w=w: e.tensor_mul(
                    out=ssq[:, :w], in0=sxf[:, i, :w], in1=pbcA[:, :w]),
                       waits=((t_c, bc_done),) if i == 0 else ())
                p.emit("vector", lambda e, i=i, w=w: e.tensor_add(
                    out=sy[:, :w], in0=ssq[:, :w], in1=pbcB[:, :w]), inc=v_c)
                y_done = v_c.v
                p.emit("gpsimd", lambda e, i=i, s0=s0, w=w: e.dma_start(
                    out=y_h[i * 128:(i + 1) * 128, s0:s0 + w], in_=sy[:, :w]),
                       waits=((v_c, y_done),), inc=st, inc_n=16)
                p.wait("vector", st, st.v)
            p.wait("tensor", v_c, v_c.v)
        p.wait("gpsimd", st, st.v)
        p.finalize(block)
    return nc


def build_add():
    nc = bass.Bass("TRN2", target_bir_lowering=False, debug=False)
    a_h = nc.dram_tensor("a", [D, T_OWN], F32, kind="ExternalInput").ap()
    b_h = nc.dram_tensor("b", [D, T_OWN], F32, kind="ExternalInput").ap()
    y_h = nc.dram_tensor("y", [D, T_OWN], F32, kind="ExternalOutput").ap()
    es = ExitStack()
    sb = lambda n, s, d=F32: es.enter_context(nc.sbuf_tensor(n, s, d)).ap()
    with es:
        sa = sb("sa", [128, NCH, 512])
        sbb = sb("sbb", [128, NCH, 512])
        sy = sb("sy", [128, 512])
        p = Prog(nc, es)
        lx = p.sem("lx")
        v_c = p.sem("v_c")
        st = p.sem("st")
        block = es.enter_context(nc.Block())
        for (s0, w) in strips(T_OWN, 512):
            v_snap = v_c.v
            for i in range(NCH):
                p.emit("sync", lambda e, i=i, s0=s0, w=w: e.dma_start(
                    out=sa[:, i, :w], in_=a_h[i * 128:(i + 1) * 128, s0:s0 + w]),
                       waits=((v_c, v_snap),) if i == 0 else (), inc=lx, inc_n=16)
                p.emit("sync", lambda e, i=i, s0=s0, w=w: e.dma_start(
                    out=sbb[:, i, :w], in_=b_h[i * 128:(i + 1) * 128, s0:s0 + w]),
                       inc=lx, inc_n=16)
            x_ld = lx.v
            for i in range(NCH):
                p.emit("vector", lambda e, i=i, w=w: e.tensor_add(
                    out=sy[:, :w], in0=sa[:, i, :w], in1=sbb[:, i, :w]),
                       waits=((lx, x_ld),) if i == 0 else ((st, st.v),), inc=v_c)
                y_done = v_c.v
                p.emit("gpsimd", lambda e, i=i, s0=s0, w=w: e.dma_start(
                    out=y_h[i * 128:(i + 1) * 128, s0:s0 + w], in_=sy[:, :w]),
                       waits=((v_c, y_done),), inc=st, inc_n=16)
        p.wait("gpsimd", st, st.v)
        p.finalize(block)
    return nc


def shard_own_fm(xfull):
    return [np.ascontiguousarray(xfull[c // 4, (c % 4) * T_OWN:((c % 4) + 1) * T_OWN, :].T)
            for c in range(NCORES)]



_NC_CACHE = {}


def _get_nc(name, builder):
    if name not in _NC_CACHE:
        _NC_CACHE[name] = builder()
    return _NC_CACHE[name]


def kernel(**inputs):
    ins = {k: np.asarray(v, np.float32) for k, v in inputs.items()}
    x0 = ins["L0_attn_x"]
    cores = list(range(NCORES))

    nc = _get_nc("l0_attn", build_l0_attn)
    wm = l0_attn_weight_maps(ins)
    maps = [dict(m, **wm) for m in l0_attn_in_maps(x0)]
    res = run_bass_kernel_spmd(nc, maps, core_ids=cores)
    x1 = assemble_l1_attn_out(res.results)

    nc = _get_nc("ln", build_ln)
    maps = [{"x": s} for s in shard_own_fm(x1)]
    res = run_bass_kernel_spmd(nc, maps, core_ids=cores)
    h = assemble_l1_attn_out(res.results)

    for layer in range(6):
        nc = _get_nc("l0f_attn", build_l0f_attn)
        wm = l0f_attn_layer_maps(ins, layer)
        maps = [dict(m, **wm) for m in l0f_attn_in_maps(h)]
        res = run_bass_kernel_spmd(nc, maps, core_ids=cores)
        hmid = assemble_l1_attn_out(res.results)

        nc = _get_nc("l0f_ffn", build_l0f_ffn)
        wm = l0f_ffn_layer_maps(ins, layer)
        maps = [dict(m, **wm) for m in l0f_ffn_in_maps(hmid)]
        res = run_bass_kernel_spmd(nc, maps, core_ids=cores)
        h = assemble_l1_attn_out(res.results)

    nc = _get_nc("add", build_add)
    a_sh, b_sh = shard_own_fm(x1), shard_own_fm(h)
    maps = [{"a": a_sh[c], "b": b_sh[c]} for c in cores]
    res = run_bass_kernel_spmd(nc, maps, core_ids=cores)
    x2 = assemble_l1_attn_out(res.results)

    nc = _get_nc("l1_attn", build_l1_attn)
    wm = l1_attn_weight_maps(ins["L1_attn_Wq"], ins["L1_attn_Wk"], ins["L1_attn_Wv"])
    maps = [dict(m, **wm) for m in l1_attn_in_maps(x2)]
    res = run_bass_kernel_spmd(nc, maps, core_ids=cores)
    x3 = assemble_l1_attn_out(res.results)

    nc = _get_nc("l1_ffn", build_l1_ffn)
    maps = l1_ffn_in_maps(x3, ins["L1_ffn_W_r"], ins["L1_ffn_W_k"],
                          ins["L1_ffn_W_v"], ins["L1_ffn_mu_r"], ins["L1_ffn_mu_k"])
    res = run_bass_kernel_spmd(nc, maps, core_ids=cores)
    return assemble_fm_out(res.results)
